# version 02
# baseline (speedup 1.0000x reference)
import numpy as np
import ml_dtypes
import concourse.bass as bass
import concourse.mybir as mybir
from concourse.bass_utils import run_bass_kernel_spmd
from concourse.bass_core_rust import IndirectOffsetOnAxis
from contextlib import ExitStack

F32 = mybir.dt.float32
BF16 = mybir.dt.bfloat16
I32 = mybir.dt.int32
U32 = mybir.dt.uint32
AF = mybir.ActivationFunctionType
ALU = mybir.AluOpType
AX = mybir.AxisListType

T = 2048
D = 2048
NT = T // 128
ND = D // 128

ENGINES = ("sync", "gpsimd", "tensor", "vector", "scalar")
LN_EPS = 1e-5
RMS_EPS = float(np.finfo(np.float32).eps)


class SemC:
    __slots__ = ("h", "n")
    def __init__(self, handle):
        self.h = handle
        self.n = 0
    def bump(self, k=1):
        self.n += k
        return (self, self.n)
    def cur(self):
        return (self, self.n)


def _flat(waits):
    out = []
    for w in waits:
        if w is None:
            continue
        if isinstance(w, list):
            out.extend(x for x in w if x is not None)
        else:
            out.append(w)
    return out


class PB:
    def __init__(self, nc, ctx):
        self.nc = nc
        self.ctx = ctx
        self.q = {e: [] for e in ENGINES}
        self.pending = {e: [] for e in ENGINES}

    def sem(self, name):
        return SemC(self.ctx.enter_context(self.nc.semaphore(name)))

    def op(self, eng, fn, waits=(), inc=None, k=1):
        w = tuple(self.pending[eng]) + tuple(_flat(waits))
        self.pending[eng] = []
        tok = inc.bump(k) if inc is not None else None
        self.q[eng].append((w, fn, inc, k))
        return tok

    def fence(self, waits):
        for e in ENGINES:
            self.pending[e].extend(_flat(waits))

    def build(self, block):
        for ename in ENGINES:
            ops = self.q[ename]
            if not ops:
                continue
            def mk(ops=ops):
                def body(e):
                    seen = {}
                    for waits, fn, inc, k in ops:
                        need = {}
                        for (sc, v) in waits:
                            if v > need.get(id(sc), (None, -1))[1]:
                                need[id(sc)] = (sc, v)
                        for sc, v in need.values():
                            if seen.get(id(sc), -1) < v:
                                e.wait_ge(sc.h, v)
                                seen[id(sc)] = v
                        inst = fn(e)
                        if inc is not None:
                            inst.then_inc(inc.h, k)
                return body
            getattr(block, ename)(mk())


class Arena:
    def __init__(self, handle, cols):
        self.h = handle
        self.cols = cols
        self.off = 0
        self.marks = []
    def tile(self, n):
        assert self.off + n <= self.cols, f"arena overflow {self.off}+{n}>{self.cols}"
        o = self.off
        self.off += n
        return self.h[:, o:o + n]
    def tiles(self, k, n):
        return [self.tile(n) for _ in range(k)]
    def push(self):
        self.marks.append(self.off)
    def pop(self):
        self.off = self.marks.pop()



def rows_chunks(w):
    K, N = w.shape
    assert K % 128 == 0
    return np.ascontiguousarray(np.asarray(w, np.float32).reshape(K // 128, 128, N))


def make_host_consts():
    ident = np.eye(128, dtype=np.float32)
    triu = np.triu(np.ones((128, 128), np.float32))
    addmask = np.where(np.tril(np.ones((128, 128), bool)), 0.0, -1e30).astype(np.float32)
    return np.concatenate([ident, triu, addmask, ident], axis=1).astype(np.float32)



def prep_l0_attn(inp):
    d = {}
    for nm in ("Wq", "Wk", "Wv", "Wg", "Wp"):
        w = np.asarray(inp[f"L0_attn_{nm}"], np.float32).T
        d[f"l0a_{nm}"] = rows_chunks(w)
    return d


def stat_flush_dst(pb):
    return pb.nc._flush_tile


def act_flush(pb, cst, s_ac):
    return pb.op("scalar", lambda e: e.copy(cst["flush"], cst["flush"]), inc=s_ac)


def pe_flush(pb, cst, A, s_pe):
    return pb.op("tensor", lambda e: e.matmul(A["PSF"], cst["ident"][0:1, 0:1], cst["ident"][0:1, 0:128],
                                               start=True, stop=True), inc=s_pe)


def stat_ln_rms(pb, cst, xt_ap, st, waits, s_dv, s_ac, junk):
    t1 = pb.op("vector", lambda e: e.reduce_sum(st["m"], xt_ap, axis=AX.X), waits=waits, inc=s_dv)
    t2 = pb.op("vector", lambda e: e.tensor_scalar_mul(st["m"], st["m"], 1.0 / D), waits=(t1,), inc=s_dv)
    t_sq = pb.op("scalar", lambda e: e.activation(junk, xt_ap, AF.Square, accum_out=st["ssq"]), waits=waits, inc=s_ac)
    t_fl = pb.op("scalar", lambda e: e.copy(stat_flush_dst(pb), stat_flush_dst(pb)), inc=s_ac)
    t3 = pb.op("vector", lambda e: e.tensor_mul(st["tmp"], st["m"], st["m"]), waits=(t2,), inc=s_dv)
    t4 = pb.op("vector", lambda e: e.tensor_scalar(st["var"], st["ssq"], 1.0 / D, None, ALU.mult), waits=(t_fl,), inc=s_dv)
    t_v = pb.op("vector", lambda e: e.tensor_sub(st["var"], st["var"], st["tmp"]), waits=(t3, t4), inc=s_dv)
    t_s = pb.op("scalar", lambda e: e.activation(st["tmp"], st["var"], AF.Sqrt, bias=cst["ln_eps"]), waits=(t_v,), inc=s_ac)
    t_r = pb.op("vector", lambda e: e.reciprocal(st["r"], st["tmp"]), waits=(t_s,), inc=s_dv)
    return t_r, t_sq


def stat_rms_extra(pb, cst, st, t_r, s_dv, s_ac):
    t1 = pb.op("vector", lambda e: e.tensor_mul(st["tmp"], st["r"], st["r"]), waits=(t_r,), inc=s_dv)
    t2 = pb.op("vector", lambda e: e.tensor_mul(st["tmp"], st["tmp"], st["var"]), waits=(t1,), inc=s_dv)
    t_s = pb.op("scalar", lambda e: e.activation(st["tmp"], st["tmp"], AF.Sqrt, bias=cst["rms_eps"]), waits=(t2,), inc=s_ac)
    t3 = pb.op("vector", lambda e: e.reciprocal(st["tmp"], st["tmp"]), waits=(t_s,), inc=s_dv)
    t4 = pb.op("vector", lambda e: e.tensor_mul(st["rho"], st["r"], st["tmp"]), waits=(t3,), inc=s_dv)
    return t4


L0A_PHASES = set("p0,proj,attn,wp".split(","))

def emit_l0_attn(pb, A, x_in, x_out, wd, cst, S, sp):
    nc = pb.nc
    bfA, f32A = A["bf"], A["f32"]
    s_w, s_pe, s_dv, s_ac = S["s_w"], S["s_pe"], S["s_dv"], S["s_ac"]
    s_x = [S["s_x0"], S["s_x1"]]
    s_ev = [S[f"s_ev{i}"] for i in range(4)]
    s_q, s_k, s_v, s_g = S["s_q"], [S["s_k0"], S["s_k1"]], S["s_v"], [S["s_g0"], S["s_g1"]]
    PS, PT = A["PS"], A["PT"]

    bfA.push(); f32A.push()
    xn2 = bfA.tiles(ND, T)
    wres = bfA.tiles(ND, T)
    evq = bfA.tiles(4, 512)
    xn_bf = bfA.tile(T)
    xt = f32A.tiles(2, T)
    junk = f32A.tile(T)
    statc = f32A.tile(6 * NT)

    psfree = [None] * 6
    ptfree = [None] * 8
    evfree = [None] * 4
    xfree = [None, None]

    x_tiled = x_in.rearrange("(c p) d -> c p d", p=128)
    x_out_tiled = x_out.rearrange("(c p) d -> c p d", p=128)

    xnb_free = None
    for c in range(NT):
        b = c % 2
        t_ld = pb.op("sync", lambda e, c=c, b=b: e.dma_start(out=xt[b][:], in_=x_tiled[c]),
                     waits=(xfree[b],), inc=s_x[b], k=16)
        st = {k: statc[:, c * 6 + i:c * 6 + i + 1] for i, k in enumerate(("m", "ssq", "var", "r", "rho", "tmp"))}
        t_r, t_sq = stat_ln_rms(pb, cst, xt[b][:], st, (t_ld,), s_dv, s_ac, junk)
        t_rho = stat_rms_extra(pb, cst, st, t_r, s_dv, s_ac)
        t_mr = pb.op("vector", lambda e, st=st: e.tensor_mul(st["tmp"], st["m"], st["rho"]), waits=(t_rho,), inc=s_dv)
        t_xn = pb.op("vector", lambda e, b=b, st=st: e.tensor_scalar(xn_bf[:], xt[b][:], st["rho"], st["tmp"], ALU.mult, ALU.subtract),
                     waits=(xnb_free, t_mr), inc=s_dv)
        xfree[b] = [t_xn, t_sq]
        tp_last = None
        for dch in range(ND):
            sl = PT[dch % 2]
            t_tp = pb.op("tensor", lambda e, dch=dch, sl=sl: e.transpose(sl, xn_bf[:, dch * 128:dch * 128 + 128], cst["ident"]),
                         waits=(t_xn, ptfree[dch % 2]), inc=s_pe)
            t_ev = pb.op("vector", lambda e, dch=dch, c=c, sl=sl: e.tensor_copy(xn2[dch][:, c * 128:c * 128 + 128], sl),
                         waits=(t_tp,), inc=s_dv)
            ptfree[dch % 2] = t_ev
            tp_last = t_tp
        xnb_free = tp_last

    if sp.get("xn_dbg") is not None:
        t_d = None
        for dch in range(ND):
            t_d = pb.op("gpsimd", lambda e, dch=dch: e.dma_start(out=sp["xn_dbg"][dch], in_=xn2[dch][:]),
                        waits=[ptfree[0], ptfree[1]] if dch == 0 else (), inc=s_w, k=16)
        w_reader0 = [t_d]
    else:
        w_reader0 = []
    xn2_done = [ptfree[0], ptfree[1]]
    w_reader = list(w_reader0)

    def load_w(name):
        tok = None
        for dch in range(ND):
            tok = pb.op("gpsimd", lambda e, dch=dch, name=name: e.dma_start(out=wres[dch][:], in_=wd[name][dch]),
                        waits=tuple(w_reader) if dch == 0 else (), inc=s_w, k=16)
        return tok

    def proj(name, act, mode, spill, k_spc=None):
        nonlocal w_reader
        tw = load_w(name)
        outer = ND if mode == "fm" else NT
        last_mm = None
        ev_last = [None] * 4
        for o in range(outer):
            for g in range(4):
                pi = g % 4
                ps = PS[pi]
                mm = None
                for d in range(ND):
                    if mode == "fm":
                        fn = lambda e, d=d, o=o, g=g, ps=ps: e.matmul(
                            ps, wres[d][:, o * 128:o * 128 + 128], xn2[d][:, g * 512:g * 512 + 512],
                            start=(d == 0), stop=(d == ND - 1))
                    else:
                        fn = lambda e, d=d, o=o, g=g, ps=ps: e.matmul(
                            ps, xn2[d][:, o * 128:o * 128 + 128], wres[d][:, g * 512:g * 512 + 512],
                            start=(d == 0), stop=(d == ND - 1))
                    mm = pb.op("tensor", fn,
                               waits=(tw, psfree[pi], xn2_done[0], xn2_done[1]) if d == 0 else (),
                               inc=s_pe if d == ND - 1 else None)
                ev = evq[pi]
                t_ev = pb.op("scalar", lambda e, ev=ev, ps=ps: e.activation(ev[:], ps, act),
                             waits=(mm, evfree[pi]), inc=s_ac)
                psfree[pi] = t_ev
                t_sp = None
                if mode == "fm":
                    if spill is not None:
                        t_sp = pb.op("sync", lambda e, ev=ev, o=o, g=g: e.dma_start(out=spill[g * ND + o], in_=ev[:]),
                                     waits=(t_ev,), inc=s_ev[pi], k=16)
                    if k_spc is not None:
                        for j in range(4):
                            t_sp = pb.op("sync", lambda e, ev=ev, o=o, g=g, j=j:
                                         e.dma_start(out=k_spc[g * 4 + j][o], in_=ev[:, j * 128:j * 128 + 128]),
                                         waits=(t_ev,) if (j == 0 and spill is None) else (), inc=s_ev[pi], k=16)
                else:
                    t_sp = pb.op("sync", lambda e, ev=ev, o=o, g=g: e.dma_start(out=spill[o][:, g * 512:g * 512 + 512], in_=ev[:]),
                                 waits=(t_ev,), inc=s_ev[pi], k=16)
                evfree[pi] = t_sp
                ev_last[pi] = t_sp
                last_mm = mm
        w_reader = [last_mm]
        return ev_last

    if "proj" not in L0A_PHASES:
        bfA.pop(); f32A.pop()
        return [xfree[0], xfree[1]]
    sp_q = proj("l0a_Wq", AF.Silu, "fm", sp["q_sp"])
    sp_g = proj("l0a_Wg", AF.Sigmoid, "fm", sp["g_sp"])
    sp_k = proj("l0a_Wk", AF.Silu, "fm", None, k_spc=sp["k_spc"])
    sp_v = proj("l0a_Wv", AF.Silu, "tm", sp["v_sp"])
    if "attn" not in L0A_PHASES:
        bfA.pop(); f32A.pop()
        return [sp_q[-1], sp_g[-1], sp_k[-1], sp_v[-1]]

    pb.fence(sp_q + sp_g + sp_k + sp_v)
    bfA.pop(); bfA.push()
    vres = bfA.tiles(NT, T)
    qg_sb = bfA.tiles(ND, 512)
    st_sb = bfA.tiles(NT, 512)
    og = bfA.tiles(ND, T)
    kt_sb = bfA.tiles(2, ND * 128)
    gt_sb = bfA.tiles(2, 512)

    t_v = None
    for kc in range(NT):
        t_v = pb.op("sync", lambda e, kc=kc: e.dma_start(out=vres[kc][:], in_=sp["v_sp"][kc]), inc=s_v, k=16)

    st_ready = [[None, 0] for _ in range(NT)]
    st_free = [None] * NT
    q_free = None
    kt_free = [None, None]
    gt_free = [None, None]
    ps4free = [None, None]
    ps5free = [None, None]
    last_og = None
    for qg in range(4):
        t_q = None
        for do in range(ND):
            t_q = pb.op("sync", lambda e, do=do, qg=qg: e.dma_start(out=qg_sb[do][:], in_=sp["q_sp"][qg * ND + do]),
                        waits=(q_free,) if do == 0 else (), inc=s_q, k=16)
        kmax = 4 * qg + 4
        for kc in range(kmax):
            kb = kc % 2
            t_kld = pb.op("sync", lambda e, kc=kc, kb=kb: e.dma_start(
                out=kt_sb[kb].rearrange("p (c m) -> p c m", m=128), in_=sp["k_spc_blk"][kc].rearrange("c p m -> p c m")),
                waits=(kt_free[kb],), inc=s_k[kb], k=16)
            ps = PS[4] if kc % 2 == 0 else PS[0]
            mm = None
            for do in range(ND):
                mm = pb.op("tensor", lambda e, do=do, kb=kb, ps=ps:
                           e.matmul(ps, kt_sb[kb][:, do * 128:do * 128 + 128], qg_sb[do][:],
                                    start=(do == 0), stop=(do == ND - 1)),
                           waits=(t_kld, t_q, ps4free[kc % 2]) if do == 0 else (),
                           inc=s_pe if do == ND - 1 else None)
            kt_free[kb] = mm
            band = kc * 128 - qg * 512
            if band < 0:
                t_ev = pb.op("vector", lambda e, kc=kc, ps=ps: e.tensor_copy(st_sb[kc][:], ps),
                             waits=(mm, st_free[kc]), inc=s_dv)
                st_ready[kc] = [t_ev, 0]
            else:
                t_ev = pb.op("vector", lambda e, kc=kc, ps=ps, band=band:
                             e.tensor_tensor(st_sb[kc][:, band:band + 128], ps[:, band:band + 128], cst["triu"], ALU.mult),
                             waits=(mm, st_free[kc]), inc=s_dv)
                if band + 128 < 512:
                    t_ev = pb.op("vector", lambda e, kc=kc, ps=ps, band=band:
                                 e.tensor_copy(st_sb[kc][:, band + 128:512], ps[:, band + 128:512]),
                                 inc=s_dv)
                st_ready[kc] = [t_ev, band]
            ps4free[kc % 2] = st_ready[kc][0]
        for dv in range(ND):
            ps = PS[5] if dv % 2 == 0 else PS[1]
            gb = dv % 2
            g_ld = pb.op("sync", lambda e, qg=qg, dv=dv, gb=gb: e.dma_start(out=gt_sb[gb][:], in_=sp["g_sp"][qg * ND + dv]),
                         waits=(gt_free[gb],), inc=s_g[gb], k=16)
            mm = None
            for kc in range(kmax):
                band = max(kc * 128 - qg * 512, 0)
                mm = pb.op("tensor", lambda e, kc=kc, dv=dv, band=band, ps=ps, kmax=kmax:
                           e.matmul(ps[:, band:512], vres[kc][:, dv * 128:dv * 128 + 128], st_sb[kc][:, band:512],
                                    start=(kc == 0), stop=(kc == kmax - 1)),
                           waits=(t_v, st_ready[kc][0], ps5free[dv % 2]) if kc == 0 else (st_ready[kc][0],),
                           inc=s_pe if kc == kmax - 1 else None)
            t_og = pb.op("vector", lambda e, dv=dv, qg=qg, gb=gb, ps=ps:
                         e.tensor_tensor(og[dv][:, qg * 512:qg * 512 + 512], ps, gt_sb[gb][:], ALU.mult),
                         waits=(mm, g_ld), inc=s_dv)
            gt_free[gb] = t_og
            ps5free[dv % 2] = t_og
            last_og = t_og
        for kc in range(kmax):
            st_free[kc] = last_og
        q_free = last_og

    if "wp" not in L0A_PHASES:
        bfA.pop(); f32A.pop()
        return [last_og]
    psfree[0] = ps4free[1]
    psfree[1] = ps5free[1]
    wp = vres
    tw = None
    for dch in range(ND):
        tw = pb.op("gpsimd", lambda e, dch=dch: e.dma_start(out=wp[dch][:], in_=wd["l0a_Wp"][dch]),
                   waits=(last_og,) if dch == 0 else (), inc=s_w, k=16)
    for tc in range(NT):
        b = tc % 2
        t_ldx = pb.op("sync", lambda e, tc=tc, b=b: e.dma_start(out=xt[b][:], in_=x_tiled[tc]),
                      waits=(xfree[b],), inc=s_x[b], k=16)
        ev_toks = []
        for g in range(4):
            pi = g % 4
            ps = PS[pi]
            mm = None
            for do in range(ND):
                mm = pb.op("tensor", lambda e, do=do, tc=tc, g=g, ps=ps:
                           e.matmul(ps, og[do][:, tc * 128:tc * 128 + 128], wp[do][:, g * 512:g * 512 + 512],
                                    start=(do == 0), stop=(do == ND - 1)),
                           waits=(tw, psfree[pi]) if do == 0 else (),
                           inc=s_pe if do == ND - 1 else None)
            t_ev = pb.op("vector", lambda e, ps=ps, b=b, g=g: e.tensor_add(xt[b][:, g * 512:g * 512 + 512], ps, xt[b][:, g * 512:g * 512 + 512]),
                         waits=(mm, t_ldx), inc=s_dv)
            psfree[pi] = t_ev
            ev_toks.append(t_ev)
        t_st = pb.op("sync", lambda e, tc=tc, b=b: e.dma_start(out=x_out_tiled[tc], in_=xt[b][:]),
                     waits=ev_toks, inc=s_x[b], k=16)
        xfree[b] = [t_st]
    bfA.pop(); f32A.pop()
    return [xfree[0][0], xfree[1][0]]




def prep_l0_ffn(inp):
    d = {}
    _ = (inp["L0_ffn_position_ids"], inp["L0_ffn_cache_position"])
    W_r = np.asarray(inp["L0_ffn_W_r"], np.float32)
    d["l0f_wr_rep"] = np.ascontiguousarray(np.broadcast_to(W_r.reshape(1, 2048), (128, 2048)))
    d["l0f_br"] = np.full((128, 1), float(np.asarray(inp["L0_ffn_b_r"]).reshape(())), np.float32)
    W1 = np.asarray(inp["L0_ffn_W1"], np.float32)
    d["l0f_w1t"] = np.ascontiguousarray(W1.reshape(16, 128, 64, 128).transpose(2, 0, 1, 3))
    d["l0f_b1"] = np.ascontiguousarray(np.asarray(inp["L0_ffn_b1"], np.float32).reshape(64, 128).T)
    W2 = np.asarray(inp["L0_ffn_W2"], np.float32)
    d["l0f_w2t"] = np.ascontiguousarray(W2.reshape(64, 128, 16, 128).transpose(2, 0, 1, 3))
    d["l0f_b2"] = np.ascontiguousarray(np.asarray(inp["L0_ffn_b2"], np.float32).reshape(16, 128).T)
    p = np.arange(128, dtype=np.int32)
    d["l0f_tokid"] = np.ascontiguousarray((p[:, None] + 128 * np.arange(16, dtype=np.int32)[None, :]))
    d["l0f_slotid"] = np.ascontiguousarray((4 * p[:, None] + np.arange(4)[None, :]).astype(np.float32))
    d["l0f_pad"] = np.full((512, 1), 9999, np.int32)
    d["l0f_zero512"] = np.zeros((512, 1), np.float32)
    return d


L0F_META = [
    ("l0f_wr_rep", (128, 2048), F32), ("l0f_br", (128, 1), F32),
    ("l0f_w1t", (64, 16, 128, 128), F32), ("l0f_b1", (128, 64), F32),
    ("l0f_w2t", (16, 64, 128, 128), F32), ("l0f_b2", (128, 16), F32),
    ("l0f_tokid", (128, 16), I32), ("l0f_slotid", (128, 4), F32),
    ("l0f_pad", (512, 1), I32), ("l0f_zero512", (512, 1), F32),
]


def emit_l0_ffn(pb, A, x_in, x_out, wd, cst, S, sp):
    nc = pb.nc
    bfA, f32A = A["bf"], A["f32"]
    s_w, s_pe, s_dv, s_ac = S["s_w"], S["s_pe"], S["s_dv"], S["s_ac"]
    s_w2 = S["s_w2"]
    s_x = [S["s_x0"], S["s_x1"]]
    s_gi, s_rb = S["s_gi"], S["s_rb"]
    PS, PT = A["PS"], A["PT"]
    IT = A["IT"]
    K_SEL = 512

    bfA.push(); f32A.push()
    xng_fm = [bfA.tile(512) for _ in range(ND)]
    h_fm = [bfA.tile(512) for _ in range(64)]
    blk_bf = [bfA.tile(512) for _ in range(ND)]
    w1buf = bfA.tiles(2, 2048)
    w2buf = bfA.tiles(2, 8192)
    xng_tm = bfA.tile(T)
    xt = [f32A.tile(T)]
    xg = f32A.tile(T)
    junk = xg
    val = f32A.tile(T)
    comp = val
    statc = f32A.tile(6 * NT)
    statg = f32A.tile(6)
    wals = f32A.tile(16)
    mk = f32A.tile(16)
    scan = f32A.tile(16)
    z16 = f32A.tile(16)
    posf = f32A.tile(16)
    big = f32A.tile(16)
    rtl = f32A.tile(16)
    acol = f32A.tile(16)
    bcol = f32A.tile(16)
    small = f32A.tile(32)
    gwv = f32A.tile(4)
    validt = f32A.tile(4)
    slotid = f32A.tile(4)
    rowm = f32A.tile(2)
    wr_rep = f32A.tile(2048)
    b1t = f32A.tile(64)
    b2t = f32A.tile(16)
    pos_i, tokid, idx_sb, mku = IT["pos_i"], IT["tokid"], IT["idx_sb"], IT["mku"]

    x_tiled = x_in.rearrange("(c p) d -> c p d", p=128)
    x_out_tiled = x_out.rearrange("(c p) d -> c p d", p=128)

    lo = small[:, 0:1]; hi = small[:, 1:2]; mid = small[:, 2:3]
    cntp = small[:, 3:4]; onescol = small[:, 4:5]
    midb = small[:, 6:7]; thrb = small[:, 7:8]; k512 = small[:, 8:9]
    cnt_sb = small[:, 9:10]; tt = small[:, 10:11]
    cval = small[:, 11:12]; cb = small[:, 12:13]
    cnt = small[:, 13:14]; base_col = small[:, 14:15]
    gmaxc = small[:, 16:17]; gminc = small[:, 17:18]
    brcol = small[:, 19:20]
    csum_row = comp[0:1, 0:128]
    scanex = comp[0:1, 128:256]
    zrow = comp[0:1, 256:384]
    wT = comp[0:16, 384:512]
    pid = cst["ident_f32"]

    t_wr = pb.op("sync", lambda e: e.dma_start(out=wr_rep[:], in_=wd["l0f_wr_rep"]), inc=s_rb, k=16)
    t_br = pb.op("sync", lambda e: e.dma_start(out=brcol, in_=wd["l0f_br"]), inc=s_rb, k=16)
    t_sl = pb.op("sync", lambda e: e.dma_start(out=slotid[:], in_=wd["l0f_slotid"]), inc=s_rb, k=16)
    t_b1 = pb.op("sync", lambda e: e.dma_start(out=b1t[:], in_=wd["l0f_b1"]), inc=s_rb, k=16)
    t_b2 = pb.op("sync", lambda e: e.dma_start(out=b2t[:], in_=wd["l0f_b2"]), inc=s_rb, k=16)
    t_tk = pb.op("gpsimd", lambda e: e.dma_start(out=tokid[:], in_=wd["l0f_tokid"]), inc=s_gi, k=16)
    t_pf1 = pb.op("gpsimd", lambda e: e.dma_start(out=sp["idx_hbm"], in_=wd["l0f_pad"]), inc=s_gi, k=16)
    t_pf2 = pb.op("gpsimd", lambda e: e.dma_start(out=sp["gw_hbm"], in_=wd["l0f_zero512"]), inc=s_gi, k=16)
    t_z = pb.op("vector", lambda e: e.memset(z16[:], 0.0), inc=s_dv)
    t_big = pb.op("vector", lambda e: e.memset(big[:], 9999.0), inc=s_dv)
    t_on1 = pb.op("vector", lambda e: e.memset(onescol, 1.0), inc=s_dv)
    t_k5 = pb.op("vector", lambda e: e.memset(k512, float(K_SEL)), inc=s_dv)
    t_xgz = pb.op("vector", lambda e: e.memset(xg[:], 0.0), inc=s_dv)

    xfree = [None]
    w_done = []
    prev_xn = None
    for c in range(NT):
        b = 0
        t_ld = pb.op("sync", lambda e, c=c, b=b: e.dma_start(out=xt[b][:], in_=x_tiled[c]),
                     waits=(xfree[b],), inc=s_x[b], k=16)
        st = {k2: statc[:, c * 6 + i:c * 6 + i + 1] for i, k2 in enumerate(("m", "ssq", "var", "r", "rho", "tmp"))}
        t_r, t_sq = stat_ln_rms(pb, cst, xt[b][:], st, (t_ld,), s_dv, s_ac, junk)
        t_mr = pb.op("vector", lambda e, st=st: e.tensor_mul(st["tmp"], st["m"], st["r"]), waits=(t_r,), inc=s_dv)
        t_xn = pb.op("vector", lambda e, b=b, st=st: e.tensor_scalar(val[:], xt[b][:], st["r"], st["tmp"], ALU.mult, ALU.subtract),
                     waits=(t_mr, prev_xn), inc=s_dv)
        t_w = pb.op("vector", lambda e, c=c: e.scalar_tensor_tensor(xg[:], val[:], 1.0, wr_rep[:], ALU.mult, ALU.mult,
                                                                     accum_out=wals[:, c:c + 1]),
                    waits=(t_xn, t_wr, t_xgz), inc=s_dv)
        prev_xn = t_w
        w_done.append(t_w)
        xfree[b] = [t_xn, t_sq]
    t_wb = pb.op("vector", lambda e: e.tensor_scalar(wals[:], wals[:], brcol, None, ALU.add),
                 waits=[w_done[-1], t_br], inc=s_dv)
    t_xgz = pb.op("vector", lambda e: e.memset(xg[:], 0.0), waits=(w_done[-1],), inc=s_dv)
    t_zr = pb.op("vector", lambda e: e.memset(zrow, 0.0), waits=(w_done[-1],), inc=s_dv)

    t_t1 = pb.op("tensor", lambda e: e.transpose(PS[4][0:16, 0:128], wals[:], pid), waits=(t_wb,), inc=s_pe)
    t_wT = pb.op("vector", lambda e: e.tensor_copy(wT, PS[4][0:16, 0:128]), waits=(t_t1,), inc=s_dv)
    t_mx = pb.op("vector", lambda e: e.reduce_max(rowm[0:16, 0:1], wT, axis=AX.X), waits=(t_wT,), inc=s_dv)
    t_mn = pb.op("vector", lambda e: e.tensor_reduce(rowm[0:16, 1:2], wT, AX.X, ALU.min), waits=(t_wT,), inc=s_dv)
    t_t2a = pb.op("tensor", lambda e: e.transpose(PS[5][0:1, 0:16], rowm[0:16, 0:1], pid[0:16, 0:16]), waits=(t_mx,), inc=s_pe)
    t_t2b = pb.op("tensor", lambda e: e.transpose(PS[5][0:1, 16:32], rowm[0:16, 1:2], pid[0:16, 0:16]), waits=(t_mn,), inc=s_pe)
    t_rT = pb.op("vector", lambda e: e.tensor_copy(comp[0:1, 384:416], PS[5][0:1, 0:32]), waits=(t_t2a, t_t2b), inc=s_dv)
    t_gmx = pb.op("vector", lambda e: e.reduce_max(gmaxc[0:1, :], comp[0:1, 384:400], axis=AX.X), waits=(t_rT,), inc=s_dv)
    t_gmn = pb.op("vector", lambda e: e.tensor_reduce(gminc[0:1, :], comp[0:1, 400:416], AX.X, ALU.min), waits=(t_rT,), inc=s_dv)

    t_lo = pb.op("vector", lambda e: e.tensor_scalar(lo[0:1, :], gminc[0:1, :], 1.0, None, ALU.subtract), waits=(t_gmn,), inc=s_dv)
    t_hi = pb.op("vector", lambda e: e.tensor_copy(hi[0:1, :], gmaxc[0:1, :]), waits=(t_gmx,), inc=s_dv)
    t_md = pb.op("vector", lambda e: e.tensor_add(tt[0:1, :], lo[0:1, :], hi[0:1, :]), waits=(t_lo, t_hi), inc=s_dv)
    t_mid = pb.op("vector", lambda e: e.tensor_scalar_mul(mid[0:1, :], tt[0:1, :], 0.5), waits=(t_md,), inc=s_dv)
    pe_free = [t_wT, t_rT]
    for it in range(45):
        pi = it % 2
        psb = PS[4 + pi]
        t_bc = pb.op("tensor", lambda e, psb=psb: e.matmul(psb[:, 0:1], cst["ones_row"], mid[0:1, :], start=True, stop=True),
                     waits=(t_mid, pe_free[pi]), inc=s_pe)
        t_mb = pb.op("vector", lambda e, psb=psb: e.tensor_copy(midb, psb[:, 0:1]), waits=(t_bc,), inc=s_dv)
        t_cp = pb.op("vector", lambda e: e.tensor_scalar(mk[:], wals[:], midb, None, ALU.is_gt, op1=ALU.add, accum_out=cntp),
                     waits=(t_mb,), inc=s_dv)
        t_ct = pb.op("tensor", lambda e, psb=psb: e.matmul(psb[0:1, 4:5], cntp, onescol, start=True, stop=True),
                     waits=(t_cp, t_on1), inc=s_pe)
        t_cs = pb.op("vector", lambda e, psb=psb: e.tensor_copy(cnt_sb[0:1, :], psb[0:1, 4:5]), waits=(t_ct,), inc=s_dv)
        pe_free[pi] = t_cs
        pb.op("vector", lambda e, it=it: e.tensor_copy(wr_rep[0:1, 900 + it:901 + it], cnt_sb[0:1, :]), waits=(t_cs,), inc=s_dv)
        pb.op("vector", lambda e, it=it: e.tensor_copy(wr_rep[0:1, 960 + it:961 + it], mid[0:1, :]), waits=(t_cs,), inc=s_dv)
        condf = small[:, 20:21]
        d1 = small[:, 21:22]
        d2 = small[:, 22:23]
        t_cd = pb.op("vector", lambda e, condf=condf: e.tensor_scalar(condf[0:1, :], cnt_sb[0:1, :], k512[0:1, :], None, ALU.is_ge),
                     waits=(t_cs, t_k5), inc=s_dv)
        t_d1 = pb.op("vector", lambda e, d1=d1: e.tensor_sub(d1[0:1, :], mid[0:1, :], lo[0:1, :]), waits=(t_cd,), inc=s_dv)
        t_d2 = pb.op("vector", lambda e, d2=d2: e.tensor_sub(d2[0:1, :], hi[0:1, :], mid[0:1, :]), waits=(t_cd,), inc=s_dv)
        t_l2 = pb.op("vector", lambda e, d1=d1, condf=condf: e.scalar_tensor_tensor(lo[0:1, :], d1[0:1, :], condf[0:1, :], lo[0:1, :], ALU.mult, ALU.add),
                     waits=(t_d1,), inc=s_dv)
        t_h2 = pb.op("vector", lambda e, d2=d2, condf=condf: e.scalar_tensor_tensor(hi[0:1, :], d2[0:1, :], condf[0:1, :], mid[0:1, :], ALU.mult, ALU.add),
                     waits=(t_d2,), inc=s_dv)
        t_md = pb.op("vector", lambda e: e.tensor_add(tt[0:1, :], lo[0:1, :], hi[0:1, :]), waits=(t_l2, t_h2), inc=s_dv)
        t_mid = pb.op("vector", lambda e: e.tensor_scalar_mul(mid[0:1, :], tt[0:1, :], 0.5), waits=(t_md,), inc=s_dv)
    t_tb = pb.op("tensor", lambda e: e.matmul(PS[4][:, 2:3], cst["ones_row"], hi[0:1, :], start=True, stop=True),
                 waits=(t_mid, pe_free[0]), inc=s_pe)
    t_thr = pb.op("vector", lambda e: e.tensor_copy(thrb, PS[4][:, 2:3]), waits=(t_tb,), inc=s_dv)

    t_mk = pb.op("vector", lambda e: e.tensor_scalar(mk[:], wals[:], thrb, None, ALU.is_gt), waits=(t_thr,), inc=s_dv)
    t_mku = pb.op("vector", lambda e: e.tensor_scalar(mku[:], wals[:], thrb, None, ALU.is_gt), waits=(t_thr,), inc=s_dv)
    rt_toks = []
    for c in range(NT):
        st_r = statc[:, c * 6 + 3:c * 6 + 4]
        st_m = statc[:, c * 6 + 0:c * 6 + 1]
        t1 = pb.op("vector", lambda e, c=c: e.tensor_scalar(rtl[:, c:c + 1], mk[:, c:c + 1], -1.0, 1.0, ALU.mult, ALU.add),
                   waits=(t_mk,), inc=s_dv)
        t2 = pb.op("vector", lambda e, c=c, st_r=st_r: e.tensor_mul(rtl[:, c:c + 1], rtl[:, c:c + 1], st_r), waits=(t1,), inc=s_dv)
        t3 = pb.op("vector", lambda e, c=c: e.tensor_scalar(acol[:, c:c + 1], rtl[:, c:c + 1], 1.0, None, ALU.add), waits=(t2,), inc=s_dv)
        t4 = pb.op("vector", lambda e, c=c, st_m=st_m: e.tensor_mul(bcol[:, c:c + 1], rtl[:, c:c + 1], st_m), waits=(t2,), inc=s_dv)
        rt_toks.append((t3, t4))

    t_cnt = pb.op("vector", lambda e: e.reduce_sum(cnt, mk[:], axis=AX.X), waits=(t_mk,), inc=s_dv)
    t_sc = pb.op("vector", lambda e: e.tensor_tensor_scan(scan[:], mk[:], z16[:], 0.0, ALU.add, ALU.add),
                 waits=(t_mk, t_z), inc=s_dv)
    t_t3 = pb.op("tensor", lambda e: e.transpose(PS[5][0:1, 32:160], cnt, pid), waits=(t_cnt, pe_free[1], t_rT), inc=s_pe)
    t_cr = pb.op("vector", lambda e: e.tensor_copy(csum_row, PS[5][0:1, 32:160]), waits=(t_t3,), inc=s_dv)
    t_s2 = pb.op("vector", lambda e: e.tensor_tensor_scan(scanex, csum_row, zrow, 0.0, ALU.add, ALU.add),
                 waits=(t_cr, t_zr), inc=s_dv)
    t_ex = pb.op("vector", lambda e: e.tensor_sub(scanex, scanex, csum_row), waits=(t_s2,), inc=s_dv)
    t_c1 = pb.op("tensor", lambda e: e.matmul(PS[4][0:1, 3:4], cnt, onescol, start=True, stop=True),
                 waits=(t_cnt, t_thr), inc=s_pe)
    t_c2 = pb.op("vector", lambda e: e.tensor_copy(cval[0:1, :], PS[4][0:1, 3:4]), waits=(t_c1,), inc=s_dv)
    t_c3 = pb.op("tensor", lambda e: e.matmul(PS[5][:, 2:3], cst["ones_row"], cval[0:1, :], start=True, stop=True),
                 waits=(t_c2, t_cr), inc=s_pe)
    t_c4 = pb.op("vector", lambda e: e.tensor_copy(cb, PS[5][:, 2:3]), waits=(t_c3,), inc=s_dv)
    t_t4 = pb.op("tensor", lambda e: e.transpose(PS[4][:, 3:4], scanex, pid[0:1, 0:1]), waits=(t_ex, t_c2), inc=s_pe)
    t_bc2 = pb.op("vector", lambda e: e.tensor_copy(base_col, PS[4][:, 3:4]), waits=(t_t4,), inc=s_dv)
    t_p1 = pb.op("vector", lambda e: e.tensor_scalar(posf[:], scan[:], base_col, 1.0, ALU.add, ALU.subtract),
                 waits=(t_sc, t_bc2), inc=s_dv)
    t_p2a = pb.op("vector", lambda e: e.tensor_sub(posf[:], posf[:], big[:]), waits=(t_p1, t_big), inc=s_dv)
    t_p2b = pb.op("vector", lambda e: e.tensor_mul(posf[:], posf[:], mk[:]), waits=(t_p2a, t_mk), inc=s_dv)
    t_p2 = pb.op("vector", lambda e: e.tensor_add(posf[:], posf[:], big[:]), waits=(t_p2b,), inc=s_dv)
    t_p3 = pb.op("vector", lambda e: e.tensor_copy(pos_i[:], posf[:]), waits=(t_p2,), inc=s_dv)

    sc_last = None
    for c in range(NT):
        sc_last = pb.op("gpsimd", lambda e, c=c: e.indirect_dma_start(
            out=sp["idx_hbm"], out_offset=IndirectOffsetOnAxis(pos_i[:, c:c + 1], 0),
            in_=tokid[:, c:c + 1], in_offset=None, bounds_check=511, oob_is_err=False),
            waits=(t_p3, t_tk, t_pf1, t_pf2) if c == 0 else (), inc=s_gi, k=16)
        sc_last = pb.op("gpsimd", lambda e, c=c: e.indirect_dma_start(
            out=sp["gw_hbm"], out_offset=IndirectOffsetOnAxis(pos_i[:, c:c + 1], 0),
            in_=wals[:, c:c + 1], in_offset=None, bounds_check=511, oob_is_err=False),
            inc=s_gi, k=16)
    t_ri = pb.op("sync", lambda e: e.dma_start(out=idx_sb[:], in_=sp["idx_hbm"].rearrange("(p g) one -> p (g one)", g=4)),
                 waits=(sc_last,), inc=s_rb, k=16)
    t_rw = pb.op("sync", lambda e: e.dma_start(out=gwv[:], in_=sp["gw_hbm"].rearrange("(p g) one -> p (g one)", g=4)),
                 inc=s_rb, k=16)
    t_vd = pb.op("vector", lambda e: e.tensor_scalar(validt[:], slotid[:], cb, None, ALU.is_lt),
                 waits=(t_rw, t_c4, t_sl), inc=s_dv)
    t_gv = pb.op("vector", lambda e: e.tensor_mul(gwv[:], gwv[:], validt[:]), waits=(t_vd, t_ri), inc=s_dv)

    ptfree = [None, None]
    prev_g = None
    for g in range(4):
        t_ga = pb.op("gpsimd", lambda e, g=g: e.indirect_dma_start(
            out=xg[:], out_offset=None, in_=x_in, in_offset=IndirectOffsetOnAxis(idx_sb[:, g:g + 1], 0),
            bounds_check=T - 1, oob_is_err=False),
            waits=(t_ri, t_xgz, prev_g), inc=s_gi, k=16)
        stg = {k2: statg[:, i:i + 1] for i, k2 in enumerate(("m", "ssq", "var", "r", "rho", "tmp"))}
        t_r2, t_sq2 = stat_ln_rms(pb, cst, xg[:], stg, (t_ga,), s_dv, s_ac, val)
        t_mr2 = pb.op("vector", lambda e, stg=stg: e.tensor_mul(stg["tmp"], stg["m"], stg["r"]), waits=(t_r2,), inc=s_dv)
        t_xn2 = pb.op("vector", lambda e, stg=stg: e.tensor_scalar(xng_tm[:], xg[:], stg["r"], stg["tmp"], ALU.mult, ALU.subtract),
                      waits=(t_mr2,), inc=s_dv)
        tp_last = None
        for dch in range(ND):
            sl = PT[dch % 2]
            t_tp = pb.op("tensor", lambda e, dch=dch, sl=sl: e.transpose(sl, xng_tm[:, dch * 128:dch * 128 + 128], cst["ident"]),
                         waits=(t_xn2, ptfree[dch % 2]), inc=s_pe)
            t_ev = pb.op("vector", lambda e, g=g, dch=dch, sl=sl: e.tensor_copy(xng_fm[dch][:, g * 128:g * 128 + 128], sl),
                         waits=(t_tp,), inc=s_dv)
            ptfree[dch % 2] = t_ev
            tp_last = t_tp
        prev_g = [tp_last, t_sq2, t_xn2]
    xng_all = [ptfree[0], ptfree[1]]

    w1_free = [None, None]
    psfree01 = [None, None]
    h_toks = []
    for ffc in range(64):
        wb = ffc % 2
        t_lw = pb.op("gpsimd", lambda e, ffc=ffc, wb=wb: e.dma_start(
            out=w1buf[wb].rearrange("p (c m) -> p c m", m=128), in_=wd["l0f_w1t"][ffc].rearrange("c p m -> p c m")),
            waits=(w1_free[wb],), inc=(s_w if wb == 0 else s_w2), k=16)
        ps = PS[wb]
        mm = None
        for dc in range(ND):
            mm = pb.op("tensor", lambda e, dc=dc, wb=wb, ps=ps: e.matmul(
                ps, w1buf[wb][:, dc * 128:dc * 128 + 128], xng_fm[dc][:], start=(dc == 0), stop=(dc == ND - 1)),
                waits=(t_lw, psfree01[wb], xng_all[0], xng_all[1]) if dc == 0 else (),
                inc=s_pe if dc == ND - 1 else None)
        w1_free[wb] = mm
        t_h = pb.op("scalar", lambda e, ffc=ffc, ps=ps: e.activation(h_fm[ffc][:], ps, AF.Gelu_apprx_tanh, bias=b1t[:, ffc:ffc + 1]),
                    waits=(mm, t_b1), inc=s_ac)
        psfree01[wb] = t_h
        h_toks.append(t_h)

    w2_free = [None, None]
    psfree23 = [None, None]
    blk_toks = []
    for dc in range(ND):
        wb = dc % 2
        t_lw = pb.op("gpsimd", lambda e, dc=dc, wb=wb: e.dma_start(
            out=w2buf[wb].rearrange("p (c m) -> p c m", m=128), in_=wd["l0f_w2t"][dc].rearrange("c p m -> p c m")),
            waits=(w2_free[wb], h_toks[-1] if dc <= 1 else None), inc=(s_w if wb == 0 else s_w2), k=16)
        ps = PS[2 + wb]
        mm = None
        for ffc in range(64):
            mm = pb.op("tensor", lambda e, ffc=ffc, wb=wb, ps=ps: e.matmul(
                ps, w2buf[wb][:, ffc * 128:ffc * 128 + 128], h_fm[ffc][:], start=(ffc == 0), stop=(ffc == 63)),
                waits=(t_lw, psfree23[wb], h_toks[-1]) if ffc == 0 else (),
                inc=s_pe if ffc == 63 else None)
        w2_free[wb] = mm
        t_bk = pb.op("scalar", lambda e, dc=dc, ps=ps: e.activation(blk_bf[dc][:], ps, AF.Identity, bias=b2t[:, dc:dc + 1]),
                     waits=(mm, t_b2), inc=s_ac)
        psfree23[wb] = t_bk
        blk_toks.append(t_bk)

    st_base = []
    for c in range(NT):
        b = 0
        t_ldx = pb.op("sync", lambda e, c=c, b=b: e.dma_start(out=xt[b][:], in_=x_tiled[c]),
                      waits=(xfree[b],), inc=s_x[b], k=16)
        t_bs = pb.op("vector", lambda e, c=c, b=b: e.tensor_scalar(xt[b][:], xt[b][:], acol[:, c:c + 1], bcol[:, c:c + 1],
                                                                    ALU.mult, ALU.subtract),
                     waits=(t_ldx, rt_toks[c][0], rt_toks[c][1]), inc=s_dv)
        t_st = pb.op("sync", lambda e, c=c, b=b: e.dma_start(out=x_out_tiled[c], in_=xt[b][:]),
                     waits=(t_bs,), inc=s_x[b], k=16)
        st_base.append(t_st)
        xfree[b] = [t_st]

    t_sa = None
    NG = 4
    for g in range(NG):
        toks = []
        for dc in range(ND):
            sl = PT[dc % 2]
            t_tp = pb.op("tensor", lambda e, g=g, dc=dc, sl=sl: e.transpose(sl, blk_bf[dc][:, g * 128:g * 128 + 128], cst["ident"]),
                         waits=(blk_toks[dc], ptfree[dc % 2]), inc=s_pe)
            t_vv = pb.op("vector", lambda e, g=g, dc=dc, sl=sl: e.tensor_scalar(val[:, dc * 128:dc * 128 + 128], sl,
                                                                                 gwv[:, g:g + 1], None, ALU.mult),
                         waits=(t_tp, t_gv, t_sa), inc=s_dv)
            ptfree[dc % 2] = t_vv
            toks.append(t_vv)
        t_sa = pb.op("gpsimd", lambda e, g=g: e.indirect_dma_start(
            out=x_out, out_offset=IndirectOffsetOnAxis(idx_sb[:, g:g + 1], 0),
            in_=val[:], in_offset=None, compute_op=ALU.add, bounds_check=T - 1, oob_is_err=False),
            waits=toks + st_base, inc=s_gi, k=16)

    bfA.pop(); f32A.pop()
    return [t_sa]




H = 16
QK_NOPE = 128
QK_ROPE = 64
V_HEAD = 128
KV_LORA = 512
Q_LORA = 1536
MLA_SCALE = float(192.0 ** -0.5)


def _deint_perm():
    p = np.empty(64, np.int64)
    p[:32] = np.arange(0, 64, 2)
    p[32:] = np.arange(1, 64, 2)
    return p


def prep_l1_attn(inp):
    d = {}
    perm = _deint_perm()
    qa = np.asarray(inp["L1_attn_q_a_w"], np.float32)
    d["l1a_qa"] = rows_chunks(qa)
    kva = np.asarray(inp["L1_attn_kv_a_w"], np.float32).copy()
    kva[:, 512:576] = kva[:, 512:576][:, perm]
    d["l1a_kva"] = rows_chunks(kva)
    qb = np.asarray(inp["L1_attn_q_b_w"], np.float32).copy()
    qb3 = qb.reshape(1536, 16, 192)
    qb3[:, :, 128:192] = qb3[:, :, 128:192][:, :, perm[:64] - 0] if False else qb3[:, :, 128:192]
    qbp = qb.reshape(1536, 16, 192).copy()
    qbp[:, :, 128:] = qbp[:, :, 128:][:, :, perm]
    d["l1a_qb"] = np.ascontiguousarray(qbp.reshape(12, 128, 16, 192).transpose(2, 0, 1, 3))
    kvb = np.asarray(inp["L1_attn_kv_b_w"], np.float32)
    d["l1a_kvb"] = np.ascontiguousarray(kvb.reshape(4, 128, 16, 256).transpose(2, 0, 1, 3))
    d["l1a_ow"] = rows_chunks(np.asarray(inp["L1_attn_o_w"], np.float32))
    d["l1a_qln"] = np.ascontiguousarray(np.asarray(inp["L1_attn_q_a_ln_w"], np.float32).reshape(12, 128).T)
    d["l1a_kvln"] = np.ascontiguousarray(np.asarray(inp["L1_attn_kv_a_ln_w"], np.float32).reshape(4, 128).T)
    i = np.arange(0, 64, 2, dtype=np.float64) / 64.0
    inv_freq = 1.0 / (500000.0 ** i)
    t = np.arange(2048, dtype=np.float64)
    fr = np.outer(t, inv_freq)
    cos = np.cos(fr); sin = np.sin(fr)
    cos_tm = cos.reshape(16, 128, 32).transpose(1, 0, 2).reshape(128, 512)
    sin_tm = sin.reshape(16, 128, 32).transpose(1, 0, 2).reshape(128, 512)
    d["l1a_costm"] = np.ascontiguousarray(cos_tm.astype(np.float32))
    d["l1a_sintm"] = np.ascontiguousarray(sin_tm.astype(np.float32))
    cos_fm = np.concatenate([cos.T, cos.T], axis=0)
    sin_fm = np.concatenate([sin.T, sin.T], axis=0)
    d["l1a_cosfm"] = np.ascontiguousarray(cos_fm.astype(np.float32))
    d["l1a_sinfm"] = np.ascontiguousarray(sin_fm.astype(np.float32))
    return d


L1A_META = [
    ("l1a_qa", (16, 128, 1536), F32), ("l1a_kva", (16, 128, 576), F32),
    ("l1a_qb", (16, 12, 128, 192), F32), ("l1a_kvb", (16, 4, 128, 256), F32),
    ("l1a_ow", (16, 128, 2048), F32),
    ("l1a_qln", (128, 12), F32), ("l1a_kvln", (128, 4), F32),
    ("l1a_costm", (128, 512), F32), ("l1a_sintm", (128, 512), F32),
    ("l1a_cosfm", (64, 2048), F32), ("l1a_sinfm", (64, 2048), F32),
]


def xn3_blk(sp, tc):
    return sp["_xn3_blk"][tc]


def attn_blk(sp, tc):
    return sp["_attn_blk"][tc]


L1A_PH = set("p1,p3,p5".split(","))
L1A_NH = 16

def emit_l1_attn(pb, A, x_in, x_out, wd, cst, S, sp):
    nc = pb.nc
    bfA, f32A = A["bf"], A["f32"]
    s_w, s_w2, s_pe, s_dv, s_ac = S["s_w"], S["s_w2"], S["s_pe"], S["s_dv"], S["s_ac"]
    s_x = [S["s_x0"], S["s_x1"]]
    s_ev = [S[f"s_ev{i}"] for i in range(4)]
    s_xnt = [S["s_q"], S["s_g0"]]
    s_qb = [S["s_k0"], S["s_k1"]]
    s_kvb = S["s_v"]
    s_sp2 = S["s_g1"]
    s_rb = S["s_rb"]
    PS, PT = A["PS"], A["PT"]

    sp["obfree"] = [None, None]
    sp["evt3free"] = [None, None]
    sp["last_osp"] = None
    bfA.push(); f32A.push()
    t_fm = [bfA.tile(T) for _ in range(12)]
    cn_fm = [bfA.tile(T) for _ in range(4)]
    krope_fm = bfA.tile(T)
    costm = bfA.tile(512); sintm = bfA.tile(512)
    cosfm = bfA.tile(T); sinfm = bfA.tile(T)
    qln = f32A.tile(12); kvln = f32A.tile(4)
    xt = f32A.tiles(2, T)
    junk = f32A.tile(T)
    statc = f32A.tile(6 * NT)
    strow = f32A.tile(64)

    x_tiled = x_in.rearrange("(c p) d -> c p d", p=128)
    x_out_tiled = x_out.rearrange("(c p) d -> c p d", p=128)

    t_ct1 = pb.op("gpsimd", lambda e: e.dma_start(out=costm[:], in_=wd["l1a_costm"]), inc=s_w, k=16)
    t_ct2 = pb.op("gpsimd", lambda e: e.dma_start(out=sintm[:], in_=wd["l1a_sintm"]), inc=s_w, k=16)
    t_ct3 = pb.op("gpsimd", lambda e: e.dma_start(out=cosfm[0:64, :], in_=wd["l1a_cosfm"]), inc=s_w, k=16)
    t_ct4 = pb.op("gpsimd", lambda e: e.dma_start(out=sinfm[0:64, :], in_=wd["l1a_sinfm"]), inc=s_w, k=16)
    t_ql = pb.op("sync", lambda e: e.dma_start(out=qln[:], in_=wd["l1a_qln"]), inc=s_rb, k=16)
    t_kl = pb.op("sync", lambda e: e.dma_start(out=kvln[:], in_=wd["l1a_kvln"]), inc=s_rb, k=16)

    bfA.push()
    xn_bf = bfA.tile(T)
    evtp = bfA.tiles(2, 128)
    xfree = [None, None]
    ptfree = [None, None]
    evfree = [None, None]
    xnb_free = None
    for c in range(NT):
        b = c % 2
        t_ld = pb.op("sync", lambda e, c=c, b=b: e.dma_start(out=xt[b][:], in_=x_tiled[c]),
                     waits=(xfree[b],), inc=s_x[b], k=16)
        st = {k2: statc[:, c * 6 + i:c * 6 + i + 1] for i, k2 in enumerate(("m", "ssq", "var", "r", "rho", "tmp"))}
        t_r, t_sq = stat_ln_rms(pb, cst, xt[b][:], st, (t_ld,), s_dv, s_ac, junk)
        t_mr = pb.op("vector", lambda e, st=st: e.tensor_mul(st["tmp"], st["m"], st["r"]), waits=(t_r,), inc=s_dv)
        t_xn = pb.op("vector", lambda e, b=b, st=st: e.tensor_scalar(xn_bf[:], xt[b][:], st["r"], st["tmp"], ALU.mult, ALU.subtract),
                     waits=(xnb_free, t_mr), inc=s_dv)
        xfree[b] = [t_xn, t_sq]
        tp_last = None
        for dch in (range(ND) if "p0tp" not in set("".split(",")) else range(0)):
            pi = dch % 2
            sl = PT[pi]
            t_tp = pb.op("tensor", lambda e, dch=dch, sl=sl: e.transpose(sl, xn_bf[:, dch * 128:dch * 128 + 128], cst["ident"]),
                         waits=(t_xn, ptfree[pi]), inc=s_pe)
            t_ev = pb.op("vector", lambda e, pi=pi, sl=sl: e.tensor_copy(evtp[pi][:], sl),
                         waits=(t_tp, evfree[pi]), inc=s_dv)
            ptfree[pi] = t_ev
            if "p0sp" in set("".split(",")):
                evfree[pi] = t_ev
            else:
                t_sp = pb.op("sync", lambda e, c=c, dch=dch, pi=pi: e.dma_start(out=sp["xn3_sp"][c][dch], in_=evtp[pi][:]),
                             waits=(t_ev,), inc=s_ev[pi], k=16)
                evfree[pi] = t_sp
            tp_last = t_tp
        xnb_free = tp_last
    pb.fence([evfree[0], evfree[1]])
    bfA.pop()
    if "p1" not in L1A_PH:
        bfA.pop(); f32A.pop()
        return [evfree[0], evfree[1]]

    bfA.push()
    qa_sb = [bfA.tile(1536) for _ in range(ND)]
    kva_sb = [bfA.tile(576) for _ in range(ND)]
    xnt_sb = bfA.tiles(2, 2048)
    ttm = bfA.tile(1536)
    ctm = bfA.tile(576)
    tscr = f32A.tile(1536)
    psq = strow

    t_qa = None
    for dch in range(ND):
        t_qa = pb.op("gpsimd", lambda e, dch=dch: e.dma_start(out=qa_sb[dch][:], in_=wd["l1a_qa"][dch]), inc=s_w, k=16)
    t_kva = None
    for dch in range(ND):
        t_kva = pb.op("gpsimd", lambda e, dch=dch: e.dma_start(out=kva_sb[dch][:], in_=wd["l1a_kva"][dch]), inc=s_w2, k=16)

    L1A_NTC = 16
    L1A_SUB = set("t,tstat,ttm,tp,c,rope,ctp".split(","))
    xnt_free = [None, None]
    psA = [None, None, None]
    psB = [None, None]
    ptB = [None, None]
    ttm_free = None
    ctm_free = None
    last_tfm = [None, None]
    last_cn = []
    for tc in range(L1A_NTC):
        b = tc % 2
        t_xl = pb.op("sync", lambda e, tc=tc, b=b: e.dma_start(
            out=xnt_sb[b].rearrange("p (c m) -> p c m", m=128), in_=xn3_blk(sp, tc).rearrange("c p m -> p c m")),
            waits=(xnt_free[b],), inc=s_xnt[b], k=16)
        ev_toks = []
        mm_toks = []
        for g in range(3):
            ps = PS[g]
            mm = None
            for d in range(ND):
                mm = pb.op("tensor", lambda e, d=d, g=g, b=b, ps=ps: e.matmul(
                    ps, xnt_sb[b][:, d * 128:d * 128 + 128], qa_sb[d][:, g * 512:g * 512 + 512],
                    start=(d == 0), stop=(d == ND - 1)),
                    waits=(t_xl, t_qa, psA[g]) if d == 0 else (),
                    inc=s_pe if d == ND - 1 else None)
            t_pef = pe_flush(pb, cst, A, s_pe)
            t_cp = pb.op("vector", lambda e, g=g, ps=ps: e.tensor_copy(tscr[:, g * 512:g * 512 + 512], ps),
                         waits=(t_pef, ttm_free), inc=s_dv)
            t_s = pb.op("scalar", lambda e, g=g, ps=ps: e.activation(junk[:, 0:512], ps, AF.Square, accum_out=psq[:, g:g + 1]),
                        waits=(mm, t_cp), inc=s_ac)
            psA[g] = [t_cp, t_s]
            ev_toks.extend([t_cp, t_s])
            mm_toks.append(mm)
        if "tstat" not in L1A_SUB:
            xnt_free[b] = [ev_toks[0]]
            continue
        t_afl = act_flush(pb, cst, s_ac)
        t_a1 = pb.op("vector", lambda e: e.tensor_add(psq[:, 3:4], psq[:, 0:1], psq[:, 1:2]), waits=ev_toks + [t_afl], inc=s_dv)
        t_a2 = pb.op("vector", lambda e: e.tensor_add(psq[:, 3:4], psq[:, 3:4], psq[:, 2:3]), waits=(t_a1,), inc=s_dv)
        t_a3 = pb.op("scalar", lambda e: e.activation(psq[:, 4:5], psq[:, 3:4], AF.Sqrt, bias=cst["mla_eps"], scale=1.0 / Q_LORA),
                     waits=(t_a2,), inc=s_ac)
        t_a4 = pb.op("vector", lambda e: e.reciprocal(psq[:, 5:6], psq[:, 4:5]), waits=(t_a3,), inc=s_dv)
        t_a5 = pb.op("vector", lambda e: e.tensor_scalar_mul(psq[:, 5:6], psq[:, 5:6], MLA_SCALE), waits=(t_a4,), inc=s_dv)
        if "ttm" not in L1A_SUB:
            xnt_free[b] = [ev_toks[0], t_a5]
            continue
        t_tn = pb.op("vector", lambda e: e.tensor_scalar(ttm[:], tscr[:], psq[:, 5:6], None, ALU.mult), waits=(t_a5, ttm_free), inc=s_dv)
        ttm_free = None
        for da in (range(12) if "tp" in L1A_SUB else range(0)):
            pi = da % 2
            sl = PT[pi]
            t_tp = pb.op("tensor", lambda e, da=da, sl=sl: e.transpose(sl, ttm[:, da * 128:da * 128 + 128], cst["ident"]),
                         waits=(t_tn, ptB[pi]), inc=s_pe)
            t_ev = pb.op("vector", lambda e, da=da, tc=tc, sl=sl: e.tensor_scalar(t_fm[da][:, tc * 128:tc * 128 + 128], sl,
                                                                                   qln[:, da:da + 1], None, ALU.mult),
                         waits=(t_tp, t_ql), inc=s_dv)
            ptB[pi] = t_ev
            if da == 11:
                ttm_free = t_tp
        last_tfm = [ptB[0], ptB[1]]
        if "c" not in L1A_SUB:
            xnt_free[b] = [ttm_free if ttm_free is not None else t_tn]
            last_tfm = [ptB[0], ptB[1]]
            continue
        mmc = [None, None]
        for g, (o0, nn_) in enumerate(((0, 512), (512, 64))):
            ps = PS[3 + g]
            mm = None
            for d in range(ND):
                mm = pb.op("tensor", lambda e, d=d, o0=o0, nn_=nn_, b=b, ps=ps: e.matmul(
                    ps[:, 0:nn_], xnt_sb[b][:, d * 128:d * 128 + 128], kva_sb[d][:, o0:o0 + nn_],
                    start=(d == 0), stop=(d == ND - 1)),
                    waits=(t_xl, t_kva, psB[g]) if d == 0 else (),
                    inc=s_pe if d == ND - 1 else None)
            mmc[g] = mm
        t_pef2 = pe_flush(pb, cst, A, s_pe)
        t_cs = pb.op("scalar", lambda e: e.activation(junk[:, 512:1024], PS[3][:, 0:512], AF.Square, accum_out=psq[:, 8:9]),
                     waits=(t_pef2,), inc=s_ac)
        t_k3 = pb.op("scalar", lambda e: e.activation(psq[:, 9:10], psq[:, 8:9], AF.Sqrt, bias=cst["mla_eps"], scale=1.0 / KV_LORA),
                     waits=(t_cs,), inc=s_ac)
        t_k4 = pb.op("vector", lambda e: e.reciprocal(psq[:, 10:11], psq[:, 9:10]), waits=(t_k3,), inc=s_dv)
        t_ce1 = pb.op("vector", lambda e: e.tensor_scalar(ctm[:, 0:512], PS[3][:, 0:512], psq[:, 10:11], None, ALU.mult),
                      waits=(t_k4, t_cs, ctm_free), inc=s_dv)
        t_ce2 = pb.op("vector", lambda e: e.tensor_copy(ctm[:, 512:576], PS[4][:, 0:64]), waits=(t_pef2, mmc[1], ctm_free), inc=s_dv)
        psB[0] = t_ce1
        psB[1] = t_ce2
        if "rope" not in L1A_SUB:
            te = tf = t_ce2
            psB[0] = t_ce1; psB[1] = t_ce2
            xnt_free[b] = [mmc[0], mmc[1], ttm_free]
            last_tfm = [ptB[0], ptB[1]]
            continue
        p1 = junk[:, 1024:1056]; p2 = junk[:, 1056:1088]; p3 = junk[:, 1088:1120]; p4 = junk[:, 1120:1152]
        ta = pb.op("vector", lambda e, tc=tc: e.tensor_mul(p1, ctm[:, 512:544], costm[:, tc * 32:tc * 32 + 32]), waits=(t_ce2, t_ct1), inc=s_dv)
        tb = pb.op("vector", lambda e, tc=tc: e.tensor_mul(p2, ctm[:, 544:576], sintm[:, tc * 32:tc * 32 + 32]), waits=(t_ce2, t_ct2), inc=s_dv)
        tc_ = pb.op("vector", lambda e, tc=tc: e.tensor_mul(p3, ctm[:, 544:576], costm[:, tc * 32:tc * 32 + 32]), waits=(t_ce2,), inc=s_dv)
        td = pb.op("vector", lambda e, tc=tc: e.tensor_mul(p4, ctm[:, 512:544], sintm[:, tc * 32:tc * 32 + 32]), waits=(t_ce2,), inc=s_dv)
        te = pb.op("vector", lambda e: e.tensor_sub(ctm[:, 512:544], p1, p2), waits=(ta, tb), inc=s_dv)
        tf = pb.op("vector", lambda e: e.tensor_add(ctm[:, 544:576], p3, p4), waits=(tc_, td), inc=s_dv)
        if "ctp" not in L1A_SUB:
            xnt_free[b] = [mmc[0], mmc[1], ttm_free]
            last_tfm = [ptB[0], ptB[1]]
            ctm_free = [te, tf]
            continue
        ctm_parts = []
        for dc in range(4):
            pi = dc % 2
            sl = PT[pi]
            t_tp = pb.op("tensor", lambda e, dc=dc, sl=sl: e.transpose(sl, ctm[:, dc * 128:dc * 128 + 128], cst["ident"]),
                         waits=(t_ce1, ptB[pi]), inc=s_pe)
            t_ev = pb.op("vector", lambda e, dc=dc, tc=tc, sl=sl: e.tensor_copy(cn_fm[dc][:, tc * 128:tc * 128 + 128], sl),
                         waits=(t_tp,), inc=s_dv)
            ptB[pi] = t_ev
            ctm_parts.append(t_tp)
        sl = PT[0]
        t_tpk = pb.op("tensor", lambda e, sl=sl: e.transpose(sl[0:64, :], ctm[:, 512:576], cst["ident"]),
                      waits=(te, tf, ptB[0]), inc=s_pe)
        t_evk = pb.op("vector", lambda e, tc=tc, sl=sl: e.tensor_copy(krope_fm[0:64, tc * 128:tc * 128 + 128], sl[0:64, :]),
                      waits=(t_tpk,), inc=s_dv)
        ptB[0] = t_evk
        last_cn = [ptB[0], ptB[1]]
        ctm_free = [t_tpk] + ctm_parts
        xnt_free[b] = [mmc[0], mmc[1], ttm_free]
    if sp.get("l1a_dbg") is not None:
        dbg = sp["l1a_dbg"]
        pb.op("gpsimd", lambda e: e.dma_start(out=dbg[0], in_=t_fm[0][:]), waits=last_tfm, inc=S["s_gi"], k=16)
        pb.op("gpsimd", lambda e: e.dma_start(out=dbg[1], in_=cn_fm[0][:]), waits=last_cn, inc=S["s_gi"], k=16)
        pb.op("gpsimd", lambda e: e.dma_start(out=dbg[2][:, 0:1536], in_=ttm[:]), waits=last_tfm, inc=S["s_gi"], k=16)
        pb.op("gpsimd", lambda e: e.dma_start(out=dbg[3][:, 0:576], in_=ctm[:]), waits=last_cn, inc=S["s_gi"], k=16)
        pb.op("gpsimd", lambda e: e.dma_start(out=dbg[4][:, 0:1536], in_=tscr[:]), waits=last_tfm, inc=S["s_gi"], k=16)
        pb.op("gpsimd", lambda e: e.dma_start(out=dbg[5][:, 0:64], in_=strow[:]), waits=last_tfm, inc=S["s_gi"], k=16)
    pb.fence(last_tfm + last_cn)
    bfA.pop()
    if "p3" not in L1A_PH:
        bfA.pop(); f32A.pop()
        return last_tfm + last_cn

    bfA.push()
    kvbh = bfA.tile(1024)
    qbh = bfA.tile(2304)
    kvh = [bfA.tile(256) for _ in range(NT)]
    knope = bfA.tile(T)
    qnope = bfA.tile(T)
    qrraw = bfA.tile(T)
    qswap = bfA.tile(T)
    qrope = bfA.tile(T)
    P_sb = bfA.tile(T)
    ptsb = bfA.tiles(2, 128)
    ob_sb = bfA.tiles(2, 128)
    evt3 = bfA.tiles(2, 128)

    kvb_free = None
    qb_free = [None, None]
    prev_attn = None
    psS_free = [None, None, None, None]
    ps4_free = None
    ps5_free = None
    qsw_free = None
    for h in range(min(H, L1A_NH)):
        t_kvb = pb.op("gpsimd", lambda e, h=h: e.dma_start(
            out=kvbh.rearrange("p (c m) -> p c m", m=256), in_=wd["l1a_kvb"][h].rearrange("c p m -> p c m")),
            waits=(kvb_free,), inc=s_kvb, k=16)
        kvsc = []
        for dc in range(4):
            tsc = pb.op("vector", lambda e, dc=dc: e.tensor_scalar(kvbh[:, dc * 256:dc * 256 + 256], kvbh[:, dc * 256:dc * 256 + 256],
                                                                    kvln[:, dc:dc + 1], None, ALU.mult),
                        waits=(t_kvb, t_kl, prev_attn), inc=s_dv)
            kvsc.append(tsc)
        kvh_ev = [None] * NT
        last_kv_mm = None
        for tc in range(NT):
            ps = PS[4 + tc % 2]
            psfree = ps4_free if tc % 2 == 0 else ps5_free
            mm = None
            for dc in range(4):
                mm = pb.op("tensor", lambda e, dc=dc, tc=tc, ps=ps: e.matmul(
                    ps[:, 0:256], cn_fm[dc][:, tc * 128:tc * 128 + 128], kvbh[:, dc * 256:dc * 256 + 256],
                    start=(dc == 0), stop=(dc == 3)),
                    waits=tuple(kvsc) + (psfree, prev_attn) if dc == 0 else (),
                    inc=s_pe if dc == 3 else None)
            t_ev = pb.op("vector", lambda e, tc=tc, ps=ps: e.tensor_copy(kvh[tc][:], ps[:, 0:256]), waits=(mm, prev_attn), inc=s_dv)
            kvh_ev[tc] = t_ev
            if tc % 2 == 0:
                ps4_free = t_ev
            else:
                ps5_free = t_ev
            last_kv_mm = mm
        kvb_free = last_kv_mm
        ptb2 = [None, None]
        for tc in range(NT):
            pi = tc % 2
            sl = PT[pi]
            t_tp = pb.op("tensor", lambda e, tc=tc, sl=sl: e.transpose(sl, kvh[tc][:, 0:128], cst["ident"]),
                         waits=(kvh_ev[tc], ptb2[pi], prev_attn), inc=s_pe)
            t_ev = pb.op("vector", lambda e, tc=tc, sl=sl: e.tensor_copy(knope[:, tc * 128:tc * 128 + 128], sl),
                         waits=(t_tp,), inc=s_dv)
            ptb2[pi] = t_ev
        kn_done = [ptb2[0], ptb2[1]]
        t_qb = pb.op("gpsimd", lambda e, h=h: e.dma_start(
            out=qbh.rearrange("p (c m) -> p c m", m=192), in_=wd["l1a_qb"][h].rearrange("c p m -> p c m")),
            waits=(qb_free[0],), inc=s_qb[0], k=16)
        mmn = [None] * 4
        for da in range(12):
            for g in range(4):
                w0 = (t_qb, psS_free[g], prev_attn) if da == 0 else ()
                mmn[g] = pb.op("tensor", lambda e, da=da, g=g: e.matmul(
                    PS[g], qbh[:, da * 192:da * 192 + 128], t_fm[da][:, g * 512:g * 512 + 512],
                    start=(da == 0), stop=(da == 11)),
                    waits=w0, inc=s_pe if da == 11 else None)
        qn_ev = []
        for g in range(4):
            t_ev = pb.op("vector", lambda e, g=g: e.tensor_copy(qnope[:, g * 512:g * 512 + 512], PS[g]),
                         waits=(mmn[g], prev_attn), inc=s_dv)
            qn_ev.append(t_ev)
            psS_free[g] = t_ev
        qr_ev = []
        mmr_last = None
        for g in range(4):
            ps = PS[4 + g % 2]
            psfree = ps4_free if g % 2 == 0 else ps5_free
            mm = None
            for da in range(12):
                w0 = (t_qb, psfree, prev_attn) if da == 0 else ()
                mm = pb.op("tensor", lambda e, da=da, g=g, ps=ps: e.matmul(
                    ps[0:64, 0:512], qbh[:, da * 192 + 128:da * 192 + 192], t_fm[da][:, g * 512:g * 512 + 512],
                    start=(da == 0), stop=(da == 11)),
                    waits=w0, inc=s_pe if da == 11 else None)
            t_ev = pb.op("vector", lambda e, g=g, ps=ps: e.tensor_copy(qrraw[0:64, g * 512:g * 512 + 512], ps[0:64, 0:512]),
                         waits=(mm, prev_attn, qsw_free), inc=s_dv)
            qr_ev.append(t_ev)
            if g % 2 == 0:
                ps4_free = t_ev
            else:
                ps5_free = t_ev
            mmr_last = mm
        qb_free[0] = mmr_last
        t_sw1 = pb.op("sync", lambda e: e.dma_start(out=qswap[0:32, :], in_=qrraw[32:64, :]), waits=qr_ev, inc=s_rb, k=16)
        t_sw2 = pb.op("sync", lambda e: e.dma_start(out=qswap[32:64, :], in_=qrraw[0:32, :]), inc=s_rb, k=16)
        rp = []
        qrp1 = pb.op("vector", lambda e: e.tensor_mul(qrope[0:32, :], qrraw[0:32, :], cosfm[0:32, :]), waits=qr_ev + [t_ct3], inc=s_dv)
        qrp2 = pb.op("vector", lambda e: e.tensor_mul(P_sb[0:32, :], qswap[0:32, :], sinfm[0:32, :]), waits=(t_sw2, t_ct4), inc=s_dv)
        qrp3 = pb.op("vector", lambda e: e.tensor_sub(qrope[0:32, :], qrope[0:32, :], P_sb[0:32, :]), waits=(qrp1, qrp2), inc=s_dv)
        qrp4 = pb.op("vector", lambda e: e.tensor_mul(qrope[32:64, :], qrraw[32:64, :], cosfm[32:64, :]), waits=qr_ev + [t_ct3], inc=s_dv)
        qrp5 = pb.op("vector", lambda e: e.tensor_mul(P_sb[32:64, :], qswap[32:64, :], sinfm[32:64, :]), waits=(t_sw2, t_ct4), inc=s_dv)
        qrp6 = pb.op("vector", lambda e: e.tensor_add(qrope[32:64, :], qrope[32:64, :], P_sb[32:64, :]), waits=(qrp4, qrp5), inc=s_dv)
        qsw_free = [qrp3, qrp6]
        rope_done = [qrp3, qrp6]

        den = strow[:, 16:17]; negm = strow[:, 17:18]; rden = strow[:, 18:19]
        mxs = [strow[:, 20 + i:21 + i] for i in range(4)]
        dns = [strow[:, 24 + i:25 + i] for i in range(4)]
        psP_free = None
        prev_den = ()
        for qc in range(16):
            ngr = qc // 4 + 1
            exp_toks = []
            mx_toks = []
            s_toks = []
            for kg in range(ngr):
                ncols = 512 if kg < qc // 4 else (qc % 4 + 1) * 128
                ps = PS[kg]
                mm1 = pb.op("tensor", lambda e, qc=qc, kg=kg, ncols=ncols, ps=ps: e.matmul(
                    ps[:, 0:ncols], qnope[:, qc * 128:qc * 128 + 128], knope[:, kg * 512:kg * 512 + ncols],
                    start=True, stop=False),
                    waits=(qn_ev[0], qn_ev[1], qn_ev[2], qn_ev[3], kn_done[0], kn_done[1], psS_free[kg]),
                    inc=None)
                mm2 = pb.op("tensor", lambda e, qc=qc, kg=kg, ncols=ncols, ps=ps: e.matmul(
                    ps[:, 0:ncols], qrope[0:64, qc * 128:qc * 128 + 128], krope_fm[0:64, kg * 512:kg * 512 + ncols],
                    start=False, stop=True),
                    waits=tuple(rope_done), inc=s_pe)
                dcol = qc * 128 - kg * 512
                if 0 <= dcol < 512:
                    t_msk = pb.op("vector", lambda e, kg=kg, dcol=dcol, ps=ps: e.tensor_tensor(
                        ps[:, dcol:dcol + 128], ps[:, dcol:dcol + 128], cst["addmask"], ALU.add),
                        waits=(mm2,), inc=s_dv)
                else:
                    t_msk = mm2
                t_mx = pb.op("vector", lambda e, kg=kg, ncols=ncols, ps=ps: e.reduce_max(mxs[kg], ps[:, 0:ncols], axis=AX.X),
                             waits=(t_msk,), inc=s_dv)
                mx_toks.append(t_mx)
                s_toks.append(t_msk)
            t_gm = mx_toks[0]
            for kg in range(1, ngr):
                t_gm = pb.op("vector", lambda e, kg=kg: e.tensor_max(mxs[0], mxs[0], mxs[kg]), waits=(t_gm, mx_toks[kg]), inc=s_dv)
            t_nm = pb.op("vector", lambda e: e.tensor_scalar_mul(negm, mxs[0], -1.0), waits=(t_gm,) + tuple(prev_den), inc=s_dv)
            den_toks = []
            for kg in range(ngr):
                ncols = 512 if kg < qc // 4 else (qc % 4 + 1) * 128
                ps = PS[kg]
                t_ex = pb.op("scalar", lambda e, kg=kg, ncols=ncols, ps=ps: e.activation(
                    P_sb[:, kg * 512:kg * 512 + ncols], ps[:, 0:ncols], AF.Exp, bias=negm, accum_out=dns[kg]),
                    waits=(t_nm, s_toks[kg], psP_free), inc=s_ac)
                psS_free[kg] = t_ex
                den_toks.append(t_ex)
            t_d = den_toks[0]
            prev = dns[0]
            for kg in range(1, ngr):
                t_d = pb.op("vector", lambda e, kg=kg: e.tensor_add(dns[0], dns[0], dns[kg]), waits=(t_d, den_toks[kg]), inc=s_dv)
            t_rd = pb.op("vector", lambda e: e.reciprocal(rden, dns[0]), waits=(t_d, den_toks[-1]), inc=s_dv)
            nk = qc + 1
            mmPV = None
            ptb3_local = ptb2
            for kc in range(nk):
                pi = kc % 2
                sl = PT[pi]
                t_tp = pb.op("tensor", lambda e, kc=kc, sl=sl: e.transpose(sl, P_sb[:, kc * 128:kc * 128 + 128], cst["ident"]),
                             waits=(den_toks[kc // 4], ptb2[pi]), inc=s_pe)
                t_pe = pb.op("vector", lambda e, pi=pi, sl=sl: e.tensor_copy(ptsb[pi][:], sl), waits=(t_tp,), inc=s_dv)
                mmPV = pb.op("tensor", lambda e, kc=kc, pi=pi, nk=nk: e.matmul(
                    PS[4][:, 0:128], ptsb[pi][:], kvh[kc][:, 128:256],
                    start=(kc == 0), stop=(kc == nk - 1)),
                    waits=(t_pe, ps4_free) if kc == 0 else (t_pe,),
                    inc=s_pe if kc == nk - 1 else None)
                ptb2[pi] = mmPV
            pi2 = (h * NT + qc) % 2
            t_oe = pb.op("vector", lambda e, pi2=pi2: e.tensor_scalar(ob_sb[pi2][:], PS[4][:, 0:128], rden, None, ALU.mult),
                         waits=(mmPV, t_rd, sp["obfree"][pi2]), inc=s_dv)
            if False:
                dbg = sp["l1a_dbg"]
                pb.op("gpsimd", lambda e: e.dma_start(out=dbg[0], in_=qnope[:]), waits=qn_ev, inc=S["s_gi"], k=16)
                pb.op("gpsimd", lambda e: e.dma_start(out=dbg[1], in_=qrope[:]), waits=rope_done, inc=S["s_gi"], k=16)
                pb.op("gpsimd", lambda e: e.dma_start(out=dbg[2], in_=knope[:]), waits=kn_done, inc=S["s_gi"], k=16)
                pb.op("gpsimd", lambda e: e.dma_start(out=dbg[3], in_=krope_fm[:]), waits=(), inc=S["s_gi"], k=16)
                pb.op("gpsimd", lambda e: e.dma_start(out=dbg[4][:, 0:256], in_=kvh[0][:]), waits=(kvh_ev[0],), inc=S["s_gi"], k=16)
                pb.op("gpsimd", lambda e: e.dma_start(out=dbg[5], in_=P_sb[:]), waits=tuple(den_toks), inc=S["s_gi"], k=16)
                pb.op("gpsimd", lambda e: e.dma_start(out=dbg[6][:, 0:128], in_=ob_sb[pi2][:]), waits=(t_oe,), inc=S["s_gi"], k=16)
                pb.op("gpsimd", lambda e: e.dma_start(out=dbg[7][:, 0:64], in_=strow[:]), waits=(t_rd,), inc=S["s_gi"], k=16)
            t_ot = pb.op("tensor", lambda e, pi2=pi2: e.transpose(PT[pi2], ob_sb[pi2][:], cst["ident"]),
                         waits=(t_oe, ptb2[pi2]), inc=s_pe)
            ptb2[pi2] = t_ot
            t_oev = pb.op("vector", lambda e, pi2=pi2: e.tensor_copy(evt3[pi2][:], PT[pi2]),
                          waits=(t_ot, sp["evt3free"][pi2]), inc=s_dv)
            t_osp = pb.op("sync", lambda e, qc=qc, h=h, pi2=pi2: e.dma_start(out=sp["attn_sp"][qc][h], in_=evt3[pi2][:]),
                          waits=(t_oev,), inc=s_sp2, k=16)
            sp["evt3free"][pi2] = t_osp
            sp["obfree"][pi2] = t_ot
            sp["last_osp"] = t_osp
            ps4_free = t_oe
            prev_attn = t_oe
            psP_free = mmPV
            prev_den = tuple(den_toks)
    pb.fence([prev_attn, sp["last_osp"]])
    bfA.pop()
    if "p5" not in L1A_PH:
        bfA.pop(); f32A.pop()
        return [prev_attn, sp["last_osp"]]

    bfA.push()
    ow_sb = [bfA.tile(T) for _ in range(ND)]
    at_sb = bfA.tiles(2, 2048)
    t_ow = None
    for dch in range(ND):
        t_ow = pb.op("gpsimd", lambda e, dch=dch: e.dma_start(out=ow_sb[dch][:], in_=wd["l1a_ow"][dch]),
                     waits=(kvb_free,) if dch == 0 else (), inc=s_w, k=16)
    at_free = [None, None]
    psfree = [None] * 4
    xfree2 = [None, None]
    final = None
    for tc in range(NT):
        b = tc % 2
        t_al = pb.op("sync", lambda e, tc=tc, b=b: e.dma_start(
            out=at_sb[b].rearrange("p (c m) -> p c m", m=128), in_=attn_blk(sp, tc).rearrange("c p m -> p c m")),
            waits=(at_free[b], sp["last_osp"]), inc=s_xnt[b], k=16)
        t_ldx = pb.op("sync", lambda e, tc=tc, b=b: e.dma_start(out=xt[b][:], in_=x_tiled[tc]),
                      waits=(xfree2[b], xfree[b]), inc=s_x[b], k=16)
        ev_toks = []
        for g in range(4):
            ps = PS[g]
            mm = None
            for dv in range(ND):
                mm = pb.op("tensor", lambda e, dv=dv, g=g, b=b, ps=ps: e.matmul(
                    ps, at_sb[b][:, dv * 128:dv * 128 + 128], ow_sb[dv][:, g * 512:g * 512 + 512],
                    start=(dv == 0), stop=(dv == ND - 1)),
                    waits=(t_al, t_ow, psfree[g]) if dv == 0 else (),
                    inc=s_pe if dv == ND - 1 else None)
            t_ev = pb.op("vector", lambda e, g=g, b=b, ps=ps: e.tensor_add(xt[b][:, g * 512:g * 512 + 512], ps, xt[b][:, g * 512:g * 512 + 512]),
                         waits=(mm, t_ldx), inc=s_dv)
            psfree[g] = t_ev
            ev_toks.append(t_ev)
        t_st = pb.op("sync", lambda e, tc=tc, b=b: e.dma_start(out=x_out_tiled[tc], in_=xt[b][:]),
                     waits=ev_toks, inc=s_x[b], k=16)
        at_free[b] = [ev_toks[-1], mm]
        xfree2[b] = [t_st]
        final = t_st
    bfA.pop()
    bfA.pop(); f32A.pop()
    return [final]




HID = 5632
NH_FF = 44


def prep_l1_ffn(inp):
    d = {}
    wg = np.asarray(inp["L1_ffn_wq_gate"], np.int32).T
    d["l1f_wg"] = np.ascontiguousarray(wg.reshape(16, 128, 44, 128).transpose(2, 0, 1, 3))
    wu = np.asarray(inp["L1_ffn_wq_up"], np.int32).T
    d["l1f_wu"] = np.ascontiguousarray(wu.reshape(16, 128, 44, 128).transpose(2, 0, 1, 3))
    wd_ = np.asarray(inp["L1_ffn_wq_down"], np.int32).T
    d["l1f_wd"] = np.ascontiguousarray(wd_.reshape(44, 128, 16, 128).transpose(2, 0, 1, 3))
    d["l1f_sg"] = np.ascontiguousarray(np.asarray(inp["L1_ffn_s_gate"], np.float32).reshape(44, 128).T)
    d["l1f_su"] = np.ascontiguousarray(np.asarray(inp["L1_ffn_s_up"], np.float32).reshape(44, 128).T)
    d["l1f_sd"] = np.ascontiguousarray(np.asarray(inp["L1_ffn_s_down"], np.float32).reshape(16, 128).T)
    return d


L1F_META = [
    ("l1f_wg", (44, 16, 128, 128), I32), ("l1f_wu", (44, 16, 128, 128), I32),
    ("l1f_wd", (16, 44, 128, 128), I32),
    ("l1f_sg", (128, 44), F32), ("l1f_su", (128, 44), F32), ("l1f_sd", (128, 16), F32),
]


def emit_l1_ffn(pb, A, x_in, x_out, wd, cst, S, sp):
    nc = pb.nc
    bfA, f32A = A["bf"], A["f32"]
    s_w, s_w2, s_pe, s_dv, s_ac = S["s_w"], S["s_w2"], S["s_pe"], S["s_dv"], S["s_ac"]
    s_x = [S["s_x0"], S["s_x1"]]
    s_ev = [S[f"s_ev{i}"] for i in range(4)]
    s_xnt = S["s_q"]
    s_wd = S["s_v"]
    PS, PT = A["PS"], A["PT"]

    bfA.push(); f32A.push()
    sgt = f32A.tile(NH_FF)
    sut = f32A.tile(NH_FF)
    sdt = f32A.tile(16)
    xt = f32A.tiles(2, T)
    junk = f32A.tile(T)
    statc = f32A.tile(6 * NT)
    gu = [bfA.tile(1024) for _ in range(NH_FF)]
    wgbuf = bfA.tiles(2, 2048)
    g_sb = bfA.tile(1024)

    t_sg = pb.op("sync", lambda e: e.dma_start(out=sgt[:], in_=wd["l1f_sg"]), inc=S["s_rb"], k=16)
    t_su = pb.op("sync", lambda e: e.dma_start(out=sut[:], in_=wd["l1f_su"]), inc=S["s_rb"], k=16)
    t_sd = pb.op("sync", lambda e: e.dma_start(out=sdt[:], in_=wd["l1f_sd"]), inc=S["s_rb"], k=16)

    x_tiled = x_in.rearrange("(c p) d -> c p d", p=128)
    x_out_tiled = x_out.rearrange("(c p) d -> c p d", p=128)

    psfree = [None] * 4
    ptfree = [None, None]
    xfree = [None, None]

    bfA.push()
    xn4 = [bfA.tile(T) for _ in range(ND)]
    xn_bf = bfA.tile(T)
    xnb_free = None
    for c in range(NT):
        b = c % 2
        t_ld = pb.op("sync", lambda e, c=c, b=b: e.dma_start(out=xt[b][:], in_=x_tiled[c]),
                     waits=(xfree[b],), inc=s_x[b], k=16)
        st = {k2: statc[:, c * 6 + i:c * 6 + i + 1] for i, k2 in enumerate(("m", "ssq", "var", "r", "rho", "tmp"))}
        t_r, t_sq = stat_ln_rms(pb, cst, xt[b][:], st, (t_ld,), s_dv, s_ac, junk)
        t_mr = pb.op("vector", lambda e, st=st: e.tensor_mul(st["tmp"], st["m"], st["r"]), waits=(t_r,), inc=s_dv)
        t_xn = pb.op("vector", lambda e, b=b, st=st: e.tensor_scalar(xn_bf[:], xt[b][:], st["r"], st["tmp"], ALU.mult, ALU.subtract),
                     waits=(xnb_free, t_mr), inc=s_dv)
        xfree[b] = [t_xn, t_sq]
        tp_last = None
        for dch in range(ND):
            pi = dch % 2
            sl = PT[pi]
            t_tp = pb.op("tensor", lambda e, dch=dch, sl=sl: e.transpose(sl, xn_bf[:, dch * 128:dch * 128 + 128], cst["ident"]),
                         waits=(t_xn, ptfree[pi]), inc=s_pe)
            t_ev = pb.op("vector", lambda e, dch=dch, c=c, sl=sl: e.tensor_copy(xn4[dch][:, c * 128:c * 128 + 128], sl),
                         waits=(t_tp,), inc=s_dv)
            ptfree[pi] = t_ev
            tp_last = t_tp
        xnb_free = tp_last
    sp_last = [None, None]
    for dch in range(ND):
        sp_last[dch % 2] = pb.op("sync", lambda e, dch=dch: e.dma_start(out=sp["xn4_sp"][dch], in_=xn4[dch][:]),
                                 waits=(ptfree[0], ptfree[1]) if dch == 0 else (), inc=s_ev[dch % 2], k=16)
    xn4_ready = [ptfree[0], ptfree[1]]

    halves_final = []
    wg_free = None
    wu_free = None
    g_done = None
    wd_free = [None, None]
    for half in range(2):
        tau0 = half * 1024
        if half == 1:
            bfA.pop(); bfA.push()
            xn4 = [bfA.tile(T) for _ in range(ND)]
            t_l = None
            for dch in range(ND):
                t_l = pb.op("sync", lambda e, dch=dch: e.dma_start(out=xn4[dch][:], in_=sp["xn4_sp"][dch]),
                            waits=(halves_final[-1], wd_free[0], wd_free[1], sp_last[0], sp_last[1]) if dch == 0 else (),
                            inc=s_xnt, k=16)
            xn4_ready = [t_l]
        gu_toks = [None] * NH_FF
        for h in range(NH_FF):
            wb = h % 2
            t_lg = pb.op("gpsimd", lambda e, h=h, wb=wb: e.dma_start(
                out=wgbuf[wb].rearrange("p (c m) -> p c m", m=128), in_=wd["l1f_wg"][h].rearrange("c p m -> p c m")),
                waits=(wg_free,), inc=(s_w if wb == 0 else s_w2), k=16)
            ev_g = []
            for g in range(2):
                ps = PS[g]
                mm = None
                for d in range(ND):
                    mm = pb.op("tensor", lambda e, d=d, g=g, wb=wb, tau0=tau0, ps=ps: e.matmul(
                        ps, wgbuf[wb][:, d * 128:d * 128 + 128], xn4[d][:, tau0 + g * 512:tau0 + g * 512 + 512],
                        start=(d == 0), stop=(d == ND - 1)),
                        waits=(t_lg, psfree[g], xn4_ready) if d == 0 else (),
                        inc=s_pe if d == ND - 1 else None)
                t_pf = pe_flush(pb, cst, A, s_pe)
                t_ev = pb.op("scalar", lambda e, g=g, h=h, ps=ps: e.activation(g_sb[:, g * 512:g * 512 + 512], ps, AF.Silu,
                                                                               scale=sgt[:, h:h + 1]),
                             waits=(t_pf, t_sg, g_done), inc=s_ac)
                psfree[g] = t_ev
                ev_g.append(t_ev)
            wg_free = mm
            t_lu = pb.op("gpsimd", lambda e, h=h, wb=wb: e.dma_start(
                out=wgbuf[wb].rearrange("p (c m) -> p c m", m=128), in_=wd["l1f_wu"][h].rearrange("c p m -> p c m")),
                waits=(wg_free,), inc=(s_w if wb == 0 else s_w2), k=16)
            ev_u = []
            for g in range(2):
                ps = PS[2 + g]
                mm = None
                for d in range(ND):
                    mm = pb.op("tensor", lambda e, d=d, g=g, wb=wb, tau0=tau0, ps=ps: e.matmul(
                        ps, wgbuf[wb][:, d * 128:d * 128 + 128], xn4[d][:, tau0 + g * 512:tau0 + g * 512 + 512],
                        start=(d == 0), stop=(d == ND - 1)),
                        waits=(t_lu, psfree[2 + g]) if d == 0 else (),
                        inc=s_pe if d == ND - 1 else None)
                t_pf = pe_flush(pb, cst, A, s_pe)
                t_ev = pb.op("vector", lambda e, g=g, h=h, ps=ps: e.scalar_tensor_tensor(
                    gu[h][:, g * 512:g * 512 + 512], ps, sut[:, h:h + 1], g_sb[:, g * 512:g * 512 + 512],
                    ALU.mult, ALU.mult),
                    waits=(t_pf, ev_g[g], t_su, gu_toks[h]), inc=s_dv)
                psfree[2 + g] = t_ev
                ev_u.append(t_ev)
            wu_free = mm
            g_done = ev_u[-1]
            gu_toks[h] = [ev_u[0], ev_u[1]]
        bfA.pop(); bfA.push()
        wdbuf = bfA.tiles(2, NH_FF * 128)
        delta = [bfA.tile(1024) for _ in range(ND)]
        wd_free = [None, None]
        delta_toks = []
        first_ev = True
        for dc in range(ND):
            wb = dc % 2
            t_lw = pb.op("gpsimd", lambda e, dc=dc, wb=wb: e.dma_start(
                out=wdbuf[wb].rearrange("p (c m) -> p c m", m=128), in_=wd["l1f_wd"][dc].rearrange("c p m -> p c m")),
                waits=(wd_free[wb],) if wd_free[wb] is not None else (wu_free, sp_last[0], sp_last[1]),
                inc=s_wd, k=16)
            t_sub = pb.op("vector", lambda e, wb=wb: e.tensor_scalar(wdbuf[wb][:], wdbuf[wb][:], -128.0, None, ALU.add),
                          waits=(t_lw,), inc=s_dv)
            evs = []
            for g in range(2):
                pi = (2 * dc + g) % 4
                ps = PS[pi]
                mm = None
                for h in range(NH_FF):
                    mm = pb.op("tensor", lambda e, h=h, g=g, wb=wb, ps=ps: e.matmul(
                        ps, wdbuf[wb][:, h * 128:h * 128 + 128], gu[h][:, g * 512:g * 512 + 512],
                        start=(h == 0), stop=(h == NH_FF - 1)),
                        waits=(t_sub, psfree[pi], gu_toks[h]) if h == 0 else (gu_toks[h],),
                        inc=s_pe if h == NH_FF - 1 else None)
                t_pf = pe_flush(pb, cst, A, s_pe)
                wex = (sp_last[0], sp_last[1]) if first_ev and half == 0 else ()
                first_ev = False
                t_ev = pb.op("vector", lambda e, dc=dc, g=g, ps=ps: e.tensor_scalar(delta[dc][:, g * 512:g * 512 + 512], ps,
                                                                                    sdt[:, dc:dc + 1], None, ALU.mult),
                             waits=(t_pf, t_sd) + wex, inc=s_dv)
                psfree[pi] = t_ev
                evs.append(t_ev)
            wd_free[wb] = mm
            delta_toks.append(evs)
        last_st = None
        for tl in range(8):
            tc = half * 8 + tl
            b = tc % 2
            t_ldx = pb.op("sync", lambda e, tc=tc, b=b: e.dma_start(out=xt[b][:], in_=x_tiled[tc]),
                          waits=(xfree[b],), inc=s_x[b], k=16)
            adds = []
            for dc in range(ND):
                pi = dc % 2
                sl = PT[pi]
                t_tp = pb.op("tensor", lambda e, dc=dc, tl=tl, sl=sl: e.transpose(sl, delta[dc][:, tl * 128:tl * 128 + 128], cst["ident"]),
                             waits=(delta_toks[dc][tl // 4], ptfree[pi]), inc=s_pe)
                t_ad = pb.op("vector", lambda e, dc=dc, b=b, sl=sl: e.tensor_add(xt[b][:, dc * 128:dc * 128 + 128], sl,
                                                                                  xt[b][:, dc * 128:dc * 128 + 128]),
                             waits=(t_tp, t_ldx), inc=s_dv)
                ptfree[pi] = t_ad
                adds.append(t_ad)
            t_st = pb.op("sync", lambda e, tc=tc, b=b: e.dma_start(out=x_out_tiled[tc], in_=xt[b][:]),
                         waits=adds, inc=s_x[b], k=16)
            xfree[b] = [t_st]
            last_st = t_st
        halves_final.append(last_st)
    bfA.pop()
    bfA.pop(); f32A.pop()
    return [halves_final[0], halves_final[1]]



BF_COLS = 87040
F32_COLS = 8704

SEM_NAMES = ("s_w", "s_w2", "s_pe", "s_dv", "s_ac", "s_c", "s_cf", "s_gi", "s_rb",
             "s_x0", "s_x1", "s_q", "s_k0", "s_k1", "s_v", "s_g0", "s_g1",
             "s_ev0", "s_ev1", "s_ev2", "s_ev3")


def build_launch(subblocks, weights_meta):
    nc = bass.Bass("TRN2", target_bir_lowering=False, debug=False, detect_race_conditions=False)
    ctx = ExitStack()
    nc._keepalive_ctx = ctx

    x_in = nc.dram_tensor("x_in", [T, D], F32, kind="ExternalInput").ap()
    x_out = nc.dram_tensor("x_out", [T, D], F32, kind="ExternalOutput").ap()
    cst_in = nc.dram_tensor("consts_f32", [128, 4 * 128], F32, kind="ExternalInput").ap()

    wd_aps = {}
    for name, shape, dt in weights_meta:
        wd_aps[name] = nc.dram_tensor(name, list(shape), dt, kind="ExternalInput").ap()

    pb = PB(nc, ctx)
    bf_h = ctx.enter_context(nc.sbuf_tensor("arena_bf", [128, BF_COLS], BF16))
    f32_h = ctx.enter_context(nc.sbuf_tensor("arena_f32", [128, F32_COLS], F32))
    cstb_h = ctx.enter_context(nc.sbuf_tensor("cst_bf", [128, 256], BF16))
    cstf_h = ctx.enter_context(nc.sbuf_tensor("cst_f32", [128, 420], F32))
    PS = [ctx.enter_context(nc.psum_tensor(f"ps{i}", [128, 512], F32)) for i in range(6)]
    PT = [ctx.enter_context(nc.psum_tensor(f"pt{i}", [128, 128], BF16)) for i in range(2)]

    flush_t = ctx.enter_context(nc.sbuf_tensor("flush_t", [1, 4], F32))
    nc._flush_tile = flush_t[0:1, 0:1]
    A = {"bf": Arena(bf_h, BF_COLS), "f32": Arena(f32_h, F32_COLS),
         "PS": [p[:] for p in PS], "PT": [p[:] for p in PT]}

    S = {k: pb.sem(k) for k in SEM_NAMES}

    pb.op("gpsimd", lambda e: e.dma_start(out=cstb_h[:, 0:128], in_=cst_in[:, 0:128]), inc=S["s_c"], k=16)
    pb.op("gpsimd", lambda e: e.dma_start(out=cstb_h[:, 128:256], in_=cst_in[:, 128:256]), inc=S["s_c"], k=16)
    t_c = pb.op("sync", lambda e: e.dma_start(out=cstf_h[:, 0:128], in_=cst_in[:, 256:384]), inc=S["s_cf"], k=16)
    t_cI = pb.op("sync", lambda e: e.dma_start(out=cstf_h[:, 160:288], in_=cst_in[:, 384:512]), inc=S["s_cf"], k=16)
    t_m1 = pb.op("vector", lambda e: e.memset(cstf_h[:, 128:129], LN_EPS), inc=S["s_dv"])
    t_m2 = pb.op("vector", lambda e: e.memset(cstf_h[:, 129:130], RMS_EPS), inc=S["s_dv"])
    t_m3 = pb.op("vector", lambda e: e.memset(cstf_h[:, 130:131], 1e-6), inc=S["s_dv"])
    t_m4 = pb.op("vector", lambda e: e.memset(cstf_h[0:1, 288:416], 1.0), inc=S["s_dv"])
    t_m5 = pb.op("vector", lambda e: e.memset(flush_t[:], 0.0), inc=S["s_dv"])
    pb.fence([t_c, t_cI, t_m4, t_m5, S["s_c"].cur(), S["s_cf"].cur()])

    cst = {
        "ident": cstb_h[:, 0:128],
        "triu": cstb_h[:, 128:256],
        "addmask": cstf_h[:, 0:128],
        "ln_eps": cstf_h[:, 128:129],
        "rms_eps": cstf_h[:, 129:130],
        "mla_eps": cstf_h[:, 130:131],
        "ident_f32": cstf_h[:, 160:288],
        "ones_row": cstf_h[0:1, 288:416],
        "flush": flush_t[0:1, 0:1],
    }
    it_pos = ctx.enter_context(nc.sbuf_tensor("it_pos", [128, 16], I32))
    it_tok = ctx.enter_context(nc.sbuf_tensor("it_tok", [128, 16], I32))
    it_idx = ctx.enter_context(nc.sbuf_tensor("it_idx", [128, 4], I32))
    it_mku = ctx.enter_context(nc.sbuf_tensor("it_mku", [128, 16], U32))
    A["PSF"] = PS[5][0:1, 0:128]
    A["IT"] = {"pos_i": it_pos[:], "tokid": it_tok[:], "idx_sb": it_idx[:], "mku": it_mku[:]}

    sp = {}
    _kind = "Internal"
    q_sp = nc.dram_tensor("q_spill", [64, 128, 512], BF16, kind=_kind).ap()
    g_sp = nc.dram_tensor("g_spill", [64, 128, 512], BF16, kind=_kind).ap()
    k_spc = nc.dram_tensor("k_spill_t", [16, 16, 128, 128], BF16, kind=_kind).ap()
    v_sp = nc.dram_tensor("v_spill", [16, 128, 2048], BF16, kind=_kind).ap()
    xn_dbg = None
    sp["xn_dbg"] = xn_dbg
    sp["q_sp"] = [q_sp[i] for i in range(64)]
    sp["g_sp"] = [g_sp[i] for i in range(64)]
    sp["k_spc"] = [[k_spc[kc, do] for do in range(16)] for kc in range(16)]
    sp["k_spc_blk"] = [k_spc[kc] for kc in range(16)]
    sp["v_sp"] = [v_sp[i] for i in range(16)]
    sp["idx_hbm"] = nc.dram_tensor("idx_hbm", [512, 1], I32, kind="Internal").ap()
    sp["gw_hbm"] = nc.dram_tensor("gw_hbm", [512, 1], F32, kind="Internal").ap()
    xn3_sp = nc.dram_tensor("xn3_sp", [16, 16, 128, 128], BF16, kind=_kind).ap()
    sp["l1a_dbg"] = None
    attn_sp = nc.dram_tensor("attn_sp", [16, 16, 128, 128], BF16, kind=_kind).ap()
    sp["xn3_sp"] = [[xn3_sp[a, b] for b in range(16)] for a in range(16)]
    sp["attn_sp"] = [[attn_sp[a, b] for b in range(16)] for a in range(16)]
    sp["_xn3_blk"] = [xn3_sp[a] for a in range(16)]
    xn4_sp = nc.dram_tensor("xn4_sp", [16, 128, 2048], BF16, kind="Internal").ap()
    sp["xn4_sp"] = [xn4_sp[i] for i in range(16)]
    sp["_attn_blk"] = [attn_sp[a] for a in range(16)]

    xmids = [nc.dram_tensor(f"x_mid{i}", [T, D], F32, kind="Internal").ap()
             for i in range(len(subblocks) - 1)]
    bufs = [x_in] + xmids + [x_out]
    final_toks = []
    for i, (name, emit_fn) in enumerate(subblocks):
        final_toks = emit_fn(pb, A, bufs[i], bufs[i + 1], wd_aps, cst, S, sp)
        pb.fence(final_toks + [S[k].cur() for k in SEM_NAMES])

    pb.op("gpsimd", lambda e: e.nop(), waits=tuple(final_toks))

    block = ctx.enter_context(nc.Block())
    pb.build(block)
    ctx.close()
    return nc



_DT_MAP = {np.dtype(np.float32): F32, np.dtype(np.int32): I32}


def kernel(**inputs):
    x = np.ascontiguousarray(np.asarray(inputs["L0_attn_x"], np.float32))
    consts = make_host_consts()
    stages = [
        ("l0a", emit_l0_attn, prep_l0_attn(inputs)),
        ("l0f", emit_l0_ffn, prep_l0_ffn(inputs)),
        ("l1a", emit_l1_attn, prep_l1_attn(inputs)),
        ("l1f", emit_l1_ffn, prep_l1_ffn(inputs)),
    ]
    wd_all = {}
    for name, emit, wd in stages:
        wd_all.update(wd)
    meta = [(k, v.shape, _DT_MAP[v.dtype]) for k, v in wd_all.items()]
    nc = build_launch([(name, emit) for name, emit, wd in stages], meta)
    in_maps = []
    for b in range(8):
        m = {"x_in": np.ascontiguousarray(x[b]), "consts_f32": consts}
        m.update(wd_all)
        in_maps.append(m)
    res = run_bass_kernel_spmd(nc, in_maps, core_ids=list(range(8)))
    return np.stack([np.asarray(res.results[b]["x_out"], np.float32) for b in range(8)], axis=0)


# version 03
# speedup vs baseline: 1.2499x; 1.2499x over previous
import numpy as np
import ml_dtypes
import concourse.bass as bass
import concourse.mybir as mybir
from concourse.bass_utils import run_bass_kernel_spmd
from concourse.bass_core_rust import IndirectOffsetOnAxis
from contextlib import ExitStack

F32 = mybir.dt.float32
BF16 = mybir.dt.bfloat16
I32 = mybir.dt.int32
U32 = mybir.dt.uint32
AF = mybir.ActivationFunctionType
ALU = mybir.AluOpType
AX = mybir.AxisListType

T = 2048
D = 2048
NT = T // 128
ND = D // 128

ENGINES = ("sync", "gpsimd", "tensor", "vector", "scalar")
LN_EPS = 1e-5
RMS_EPS = float(np.finfo(np.float32).eps)


class SemC:
    __slots__ = ("h", "n")
    def __init__(self, handle):
        self.h = handle
        self.n = 0
    def bump(self, k=1):
        self.n += k
        return (self, self.n)
    def cur(self):
        return (self, self.n)


def _flat(waits):
    out = []
    for w in waits:
        if w is None:
            continue
        if isinstance(w, list):
            out.extend(x for x in w if x is not None)
        else:
            out.append(w)
    return out


class PB:
    def __init__(self, nc, ctx):
        self.nc = nc
        self.ctx = ctx
        self.q = {e: [] for e in ENGINES}
        self.pending = {e: [] for e in ENGINES}

    def sem(self, name):
        return SemC(self.ctx.enter_context(self.nc.semaphore(name)))

    def op(self, eng, fn, waits=(), inc=None, k=1):
        w = tuple(self.pending[eng]) + tuple(_flat(waits))
        self.pending[eng] = []
        tok = inc.bump(k) if inc is not None else None
        self.q[eng].append((w, fn, inc, k))
        return tok

    def fence(self, waits):
        for e in ENGINES:
            self.pending[e].extend(_flat(waits))

    def build(self, block):
        for ename in ENGINES:
            ops = self.q[ename]
            if not ops:
                continue
            def mk(ops=ops):
                def body(e):
                    seen = {}
                    for waits, fn, inc, k in ops:
                        need = {}
                        for (sc, v) in waits:
                            if v > need.get(id(sc), (None, -1))[1]:
                                need[id(sc)] = (sc, v)
                        for sc, v in need.values():
                            if seen.get(id(sc), -1) < v:
                                e.wait_ge(sc.h, v)
                                seen[id(sc)] = v
                        inst = fn(e)
                        if inc is not None:
                            inst.then_inc(inc.h, k)
                return body
            getattr(block, ename)(mk())


class Arena:
    def __init__(self, handle, cols):
        self.h = handle
        self.cols = cols
        self.off = 0
        self.marks = []
    def tile(self, n):
        assert self.off + n <= self.cols, f"arena overflow {self.off}+{n}>{self.cols}"
        o = self.off
        self.off += n
        return self.h[:, o:o + n]
    def tiles(self, k, n):
        return [self.tile(n) for _ in range(k)]
    def push(self):
        self.marks.append(self.off)
    def pop(self):
        self.off = self.marks.pop()



def rows_chunks(w):
    K, N = w.shape
    assert K % 128 == 0
    return np.ascontiguousarray(np.asarray(w, np.float32).reshape(K // 128, 128, N))


def make_host_consts():
    ident = np.eye(128, dtype=np.float32)
    triu = np.triu(np.ones((128, 128), np.float32))
    addmask = np.where(np.tril(np.ones((128, 128), bool)), 0.0, -1e30).astype(np.float32)
    return np.concatenate([ident, triu, addmask, ident], axis=1).astype(np.float32)



def prep_l0_attn(inp):
    d = {}
    for nm in ("Wq", "Wk", "Wv", "Wg", "Wp"):
        w = np.asarray(inp[f"L0_attn_{nm}"], np.float32).T
        d[f"l0a_{nm}"] = rows_chunks(w)
    return d


def stat_flush_dst(pb):
    return pb.nc._flush_tile


def act_flush(pb, cst, s_ac):
    return pb.op("scalar", lambda e: e.copy(cst["flush"], cst["flush"]), inc=s_ac)


def pe_flush(pb, cst, A, s_pe):
    return pb.op("tensor", lambda e: e.matmul(A["PSF"], cst["ident"][0:1, 0:1], cst["ident"][0:1, 0:128],
                                               start=True, stop=True), inc=s_pe)


def stat_ln_rms(pb, cst, xt_ap, st, waits, s_dv, s_ac, junk):
    t1 = pb.op("vector", lambda e: e.reduce_sum(st["m"], xt_ap, axis=AX.X), waits=waits, inc=s_dv)
    t2 = pb.op("vector", lambda e: e.tensor_scalar_mul(st["m"], st["m"], 1.0 / D), waits=(t1,), inc=s_dv)
    t_sq = pb.op("scalar", lambda e: e.activation(junk, xt_ap, AF.Square, accum_out=st["ssq"]), waits=waits, inc=s_ac)
    t_fl = pb.op("scalar", lambda e: e.copy(stat_flush_dst(pb), stat_flush_dst(pb)), inc=s_ac)
    t3 = pb.op("vector", lambda e: e.tensor_mul(st["tmp"], st["m"], st["m"]), waits=(t2,), inc=s_dv)
    t4 = pb.op("vector", lambda e: e.tensor_scalar(st["var"], st["ssq"], 1.0 / D, None, ALU.mult), waits=(t_fl,), inc=s_dv)
    t_v = pb.op("vector", lambda e: e.tensor_sub(st["var"], st["var"], st["tmp"]), waits=(t3, t4), inc=s_dv)
    t_s = pb.op("scalar", lambda e: e.activation(st["tmp"], st["var"], AF.Sqrt, bias=cst["ln_eps"]), waits=(t_v,), inc=s_ac)
    t_r = pb.op("vector", lambda e: e.reciprocal(st["r"], st["tmp"]), waits=(t_s,), inc=s_dv)
    return t_r, t_sq


def stat_rms_extra(pb, cst, st, t_r, s_dv, s_ac):
    t1 = pb.op("vector", lambda e: e.tensor_mul(st["tmp"], st["r"], st["r"]), waits=(t_r,), inc=s_dv)
    t2 = pb.op("vector", lambda e: e.tensor_mul(st["tmp"], st["tmp"], st["var"]), waits=(t1,), inc=s_dv)
    t_s = pb.op("scalar", lambda e: e.activation(st["tmp"], st["tmp"], AF.Sqrt, bias=cst["rms_eps"]), waits=(t2,), inc=s_ac)
    t3 = pb.op("vector", lambda e: e.reciprocal(st["tmp"], st["tmp"]), waits=(t_s,), inc=s_dv)
    t4 = pb.op("vector", lambda e: e.tensor_mul(st["rho"], st["r"], st["tmp"]), waits=(t3,), inc=s_dv)
    return t4


L0A_PHASES = set("p0,proj,attn,wp".split(","))

def emit_l0_attn(pb, A, x_in, x_out, wd, cst, S, sp):
    nc = pb.nc
    bfA, f32A = A["bf"], A["f32"]
    s_w, s_pe, s_dv, s_ac = S["s_w"], S["s_pe"], S["s_dv"], S["s_ac"]
    s_x = [S["s_x0"], S["s_x1"]]
    s_ev = [S[f"s_ev{i}"] for i in range(4)]
    s_q, s_k, s_v, s_g = S["s_q"], [S["s_k0"], S["s_k1"]], S["s_v"], [S["s_g0"], S["s_g1"]]
    PS, PT = A["PS"], A["PT"]

    bfA.push(); f32A.push()
    xn2 = bfA.tiles(ND, T)
    wres = bfA.tiles(ND, T)
    evq = bfA.tiles(4, 512)
    xn_bf = bfA.tile(T)
    xt = f32A.tiles(2, T)
    junk = f32A.tile(T)
    statc = f32A.tile(6 * NT)

    psfree = [None] * 6
    ptfree = [None] * 8
    evfree = [None] * 4
    xfree = [None, None]

    x_tiled = x_in.rearrange("(c p) d -> c p d", p=128)
    x_out_tiled = x_out.rearrange("(c p) d -> c p d", p=128)

    xnb_free = None
    for c in range(NT):
        b = c % 2
        t_ld = pb.op("sync", lambda e, c=c, b=b: e.dma_start(out=xt[b][:], in_=x_tiled[c]),
                     waits=(xfree[b],), inc=s_x[b], k=16)
        st = {k: statc[:, c * 6 + i:c * 6 + i + 1] for i, k in enumerate(("m", "ssq", "var", "r", "rho", "tmp"))}
        t_r, t_sq = stat_ln_rms(pb, cst, xt[b][:], st, (t_ld,), s_dv, s_ac, junk)
        t_rho = stat_rms_extra(pb, cst, st, t_r, s_dv, s_ac)
        t_mr = pb.op("vector", lambda e, st=st: e.tensor_mul(st["tmp"], st["m"], st["rho"]), waits=(t_rho,), inc=s_dv)
        t_xn = pb.op("vector", lambda e, b=b, st=st: e.tensor_scalar(xn_bf[:], xt[b][:], st["rho"], st["tmp"], ALU.mult, ALU.subtract),
                     waits=(xnb_free, t_mr), inc=s_dv)
        xfree[b] = [t_xn, t_sq]
        tp_last = None
        for dch in range(ND):
            sl = PT[dch % 2]
            t_tp = pb.op("tensor", lambda e, dch=dch, sl=sl: e.transpose(sl, xn_bf[:, dch * 128:dch * 128 + 128], cst["ident"]),
                         waits=(t_xn, ptfree[dch % 2]), inc=s_pe)
            t_ev = pb.op("vector", lambda e, dch=dch, c=c, sl=sl: e.tensor_copy(xn2[dch][:, c * 128:c * 128 + 128], sl),
                         waits=(t_tp,), inc=s_dv)
            ptfree[dch % 2] = t_ev
            tp_last = t_tp
        xnb_free = tp_last

    if sp.get("xn_dbg") is not None:
        t_d = None
        for dch in range(ND):
            t_d = pb.op("gpsimd", lambda e, dch=dch: e.dma_start(out=sp["xn_dbg"][dch], in_=xn2[dch][:]),
                        waits=[ptfree[0], ptfree[1]] if dch == 0 else (), inc=s_w, k=16)
        w_reader0 = [t_d]
    else:
        w_reader0 = []
    xn2_done = [ptfree[0], ptfree[1]]
    w_reader = list(w_reader0)

    def load_w(name):
        tok = None
        for dch in range(ND):
            tok = pb.op("gpsimd", lambda e, dch=dch, name=name: e.dma_start(out=wres[dch][:], in_=wd[name][dch]),
                        waits=tuple(w_reader) if dch == 0 else (), inc=s_w, k=16)
        return tok

    def proj(name, act, mode, spill, k_spc=None):
        nonlocal w_reader
        tw = load_w(name)
        outer = ND if mode == "fm" else NT
        last_mm = None
        ev_last = [None] * 4
        for o in range(outer):
            for g in range(4):
                pi = g % 4
                ps = PS[pi]
                mm = None
                for d in range(ND):
                    if mode == "fm":
                        fn = lambda e, d=d, o=o, g=g, ps=ps: e.matmul(
                            ps, wres[d][:, o * 128:o * 128 + 128], xn2[d][:, g * 512:g * 512 + 512],
                            start=(d == 0), stop=(d == ND - 1))
                    else:
                        fn = lambda e, d=d, o=o, g=g, ps=ps: e.matmul(
                            ps, xn2[d][:, o * 128:o * 128 + 128], wres[d][:, g * 512:g * 512 + 512],
                            start=(d == 0), stop=(d == ND - 1))
                    mm = pb.op("tensor", fn,
                               waits=(tw, psfree[pi], xn2_done[0], xn2_done[1]) if d == 0 else (),
                               inc=s_pe if d == ND - 1 else None)
                ev = evq[pi]
                t_ev = pb.op("scalar", lambda e, ev=ev, ps=ps: e.activation(ev[:], ps, act),
                             waits=(mm, evfree[pi]), inc=s_ac)
                psfree[pi] = t_ev
                t_sp = None
                if mode == "fm":
                    if spill is not None:
                        t_sp = pb.op("sync", lambda e, ev=ev, o=o, g=g: e.dma_start(out=spill[g * ND + o], in_=ev[:]),
                                     waits=(t_ev,), inc=s_ev[pi], k=16)
                    if k_spc is not None:
                        for j in range(4):
                            t_sp = pb.op("sync", lambda e, ev=ev, o=o, g=g, j=j:
                                         e.dma_start(out=k_spc[g * 4 + j][o], in_=ev[:, j * 128:j * 128 + 128]),
                                         waits=(t_ev,) if (j == 0 and spill is None) else (), inc=s_ev[pi], k=16)
                else:
                    t_sp = pb.op("sync", lambda e, ev=ev, o=o, g=g: e.dma_start(out=spill[o][:, g * 512:g * 512 + 512], in_=ev[:]),
                                 waits=(t_ev,), inc=s_ev[pi], k=16)
                evfree[pi] = t_sp
                ev_last[pi] = t_sp
                last_mm = mm
        w_reader = [last_mm]
        return ev_last

    if "proj" not in L0A_PHASES:
        bfA.pop(); f32A.pop()
        return [xfree[0], xfree[1]]
    sp_q = proj("l0a_Wq", AF.Silu, "fm", sp["q_sp"])
    sp_g = proj("l0a_Wg", AF.Sigmoid, "fm", sp["g_sp"])
    sp_k = proj("l0a_Wk", AF.Silu, "fm", None, k_spc=sp["k_spc"])
    sp_v = proj("l0a_Wv", AF.Silu, "tm", sp["v_sp"])
    if "attn" not in L0A_PHASES:
        bfA.pop(); f32A.pop()
        return [sp_q[-1], sp_g[-1], sp_k[-1], sp_v[-1]]

    pb.fence(sp_q + sp_g + sp_k + sp_v)
    bfA.pop(); bfA.push()
    vres = bfA.tiles(NT, T)
    qg_sb = bfA.tiles(ND, 512)
    st_sb = bfA.tiles(NT, 512)
    og = bfA.tiles(ND, T)
    kt_sb = bfA.tiles(2, ND * 128)
    gt_sb = bfA.tiles(2, 512)

    t_v = None
    for kc in range(NT):
        t_v = pb.op("sync", lambda e, kc=kc: e.dma_start(out=vres[kc][:], in_=sp["v_sp"][kc]), inc=s_v, k=16)

    st_ready = [[None, 0] for _ in range(NT)]
    st_free = [None] * NT
    q_free = None
    kt_free = [None, None]
    gt_free = [None, None]
    ps4free = [None, None]
    ps5free = [None, None]
    last_og = None
    for qg in range(4):
        t_q = None
        for do in range(ND):
            t_q = pb.op("sync", lambda e, do=do, qg=qg: e.dma_start(out=qg_sb[do][:], in_=sp["q_sp"][qg * ND + do]),
                        waits=(q_free,) if do == 0 else (), inc=s_q, k=16)
        kmax = 4 * qg + 4
        for kc in range(kmax):
            kb = kc % 2
            t_kld = pb.op("sync", lambda e, kc=kc, kb=kb: e.dma_start(
                out=kt_sb[kb].rearrange("p (c m) -> p c m", m=128), in_=sp["k_spc_blk"][kc].rearrange("c p m -> p c m")),
                waits=(kt_free[kb],), inc=s_k[kb], k=16)
            ps = PS[4] if kc % 2 == 0 else PS[0]
            mm = None
            for do in range(ND):
                mm = pb.op("tensor", lambda e, do=do, kb=kb, ps=ps:
                           e.matmul(ps, kt_sb[kb][:, do * 128:do * 128 + 128], qg_sb[do][:],
                                    start=(do == 0), stop=(do == ND - 1)),
                           waits=(t_kld, t_q, ps4free[kc % 2]) if do == 0 else (),
                           inc=s_pe if do == ND - 1 else None)
            kt_free[kb] = mm
            band = kc * 128 - qg * 512
            if band < 0:
                t_ev = pb.op("vector", lambda e, kc=kc, ps=ps: e.tensor_copy(st_sb[kc][:], ps),
                             waits=(mm, st_free[kc]), inc=s_dv)
                st_ready[kc] = [t_ev, 0]
            else:
                t_ev = pb.op("vector", lambda e, kc=kc, ps=ps, band=band:
                             e.tensor_tensor(st_sb[kc][:, band:band + 128], ps[:, band:band + 128], cst["triu"], ALU.mult),
                             waits=(mm, st_free[kc]), inc=s_dv)
                if band + 128 < 512:
                    t_ev = pb.op("vector", lambda e, kc=kc, ps=ps, band=band:
                                 e.tensor_copy(st_sb[kc][:, band + 128:512], ps[:, band + 128:512]),
                                 inc=s_dv)
                st_ready[kc] = [t_ev, band]
            ps4free[kc % 2] = st_ready[kc][0]
        for dv in range(ND):
            ps = PS[5] if dv % 2 == 0 else PS[1]
            gb = dv % 2
            g_ld = pb.op("sync", lambda e, qg=qg, dv=dv, gb=gb: e.dma_start(out=gt_sb[gb][:], in_=sp["g_sp"][qg * ND + dv]),
                         waits=(gt_free[gb],), inc=s_g[gb], k=16)
            mm = None
            for kc in range(kmax):
                band = max(kc * 128 - qg * 512, 0)
                mm = pb.op("tensor", lambda e, kc=kc, dv=dv, band=band, ps=ps, kmax=kmax:
                           e.matmul(ps[:, band:512], vres[kc][:, dv * 128:dv * 128 + 128], st_sb[kc][:, band:512],
                                    start=(kc == 0), stop=(kc == kmax - 1)),
                           waits=(t_v, st_ready[kc][0], ps5free[dv % 2]) if kc == 0 else (st_ready[kc][0],),
                           inc=s_pe if kc == kmax - 1 else None)
            t_og = pb.op("vector", lambda e, dv=dv, qg=qg, gb=gb, ps=ps:
                         e.tensor_tensor(og[dv][:, qg * 512:qg * 512 + 512], ps, gt_sb[gb][:], ALU.mult),
                         waits=(mm, g_ld), inc=s_dv)
            gt_free[gb] = t_og
            ps5free[dv % 2] = t_og
            last_og = t_og
        for kc in range(kmax):
            st_free[kc] = last_og
        q_free = last_og

    if "wp" not in L0A_PHASES:
        bfA.pop(); f32A.pop()
        return [last_og]
    psfree[0] = ps4free[1]
    psfree[1] = ps5free[1]
    wp = vres
    tw = None
    for dch in range(ND):
        tw = pb.op("gpsimd", lambda e, dch=dch: e.dma_start(out=wp[dch][:], in_=wd["l0a_Wp"][dch]),
                   waits=(last_og,) if dch == 0 else (), inc=s_w, k=16)
    for tc in range(NT):
        b = tc % 2
        t_ldx = pb.op("sync", lambda e, tc=tc, b=b: e.dma_start(out=xt[b][:], in_=x_tiled[tc]),
                      waits=(xfree[b],), inc=s_x[b], k=16)
        ev_toks = []
        for g in range(4):
            pi = g % 4
            ps = PS[pi]
            mm = None
            for do in range(ND):
                mm = pb.op("tensor", lambda e, do=do, tc=tc, g=g, ps=ps:
                           e.matmul(ps, og[do][:, tc * 128:tc * 128 + 128], wp[do][:, g * 512:g * 512 + 512],
                                    start=(do == 0), stop=(do == ND - 1)),
                           waits=(tw, psfree[pi]) if do == 0 else (),
                           inc=s_pe if do == ND - 1 else None)
            t_ev = pb.op("vector", lambda e, ps=ps, b=b, g=g: e.tensor_add(xt[b][:, g * 512:g * 512 + 512], ps, xt[b][:, g * 512:g * 512 + 512]),
                         waits=(mm, t_ldx), inc=s_dv)
            psfree[pi] = t_ev
            ev_toks.append(t_ev)
        t_st = pb.op("sync", lambda e, tc=tc, b=b: e.dma_start(out=x_out_tiled[tc], in_=xt[b][:]),
                     waits=ev_toks, inc=s_x[b], k=16)
        xfree[b] = [t_st]
    bfA.pop(); f32A.pop()
    return [xfree[0][0], xfree[1][0]]




def prep_l0_ffn(inp):
    d = {}
    _ = (inp["L0_ffn_position_ids"], inp["L0_ffn_cache_position"])
    W_r = np.asarray(inp["L0_ffn_W_r"], np.float32)
    d["l0f_wr_rep"] = np.ascontiguousarray(np.broadcast_to(W_r.reshape(1, 2048), (128, 2048)))
    d["l0f_br"] = np.full((128, 1), float(np.asarray(inp["L0_ffn_b_r"]).reshape(())), np.float32)
    W1 = np.asarray(inp["L0_ffn_W1"], np.float32)
    d["l0f_w1t"] = np.ascontiguousarray(W1.reshape(16, 128, 64, 128).transpose(2, 0, 1, 3))
    d["l0f_b1"] = np.ascontiguousarray(np.asarray(inp["L0_ffn_b1"], np.float32).reshape(64, 128).T)
    W2 = np.asarray(inp["L0_ffn_W2"], np.float32)
    d["l0f_w2t"] = np.ascontiguousarray(W2.reshape(64, 128, 16, 128).transpose(2, 0, 1, 3))
    d["l0f_b2"] = np.ascontiguousarray(np.asarray(inp["L0_ffn_b2"], np.float32).reshape(16, 128).T)
    p = np.arange(128, dtype=np.int32)
    d["l0f_tokid"] = np.ascontiguousarray((p[:, None] + 128 * np.arange(16, dtype=np.int32)[None, :]))
    d["l0f_slotid"] = np.ascontiguousarray((4 * p[:, None] + np.arange(4)[None, :]).astype(np.float32))
    d["l0f_pad"] = np.full((512, 1), 9999, np.int32)
    d["l0f_zero512"] = np.zeros((512, 1), np.float32)
    return d


L0F_META = [
    ("l0f_wr_rep", (128, 2048), F32), ("l0f_br", (128, 1), F32),
    ("l0f_w1t", (64, 16, 128, 128), F32), ("l0f_b1", (128, 64), F32),
    ("l0f_w2t", (16, 64, 128, 128), F32), ("l0f_b2", (128, 16), F32),
    ("l0f_tokid", (128, 16), I32), ("l0f_slotid", (128, 4), F32),
    ("l0f_pad", (512, 1), I32), ("l0f_zero512", (512, 1), F32),
]


def emit_l0_ffn(pb, A, x_in, x_out, wd, cst, S, sp):
    nc = pb.nc
    bfA, f32A = A["bf"], A["f32"]
    s_w, s_pe, s_dv, s_ac = S["s_w"], S["s_pe"], S["s_dv"], S["s_ac"]
    s_w2 = S["s_w2"]
    s_x = [S["s_x0"], S["s_x1"]]
    s_gi, s_rb = S["s_gi"], S["s_rb"]
    PS, PT = A["PS"], A["PT"]
    IT = A["IT"]
    K_SEL = 512

    bfA.push(); f32A.push()
    xng_fm = [bfA.tile(512) for _ in range(ND)]
    h_fm = [bfA.tile(512) for _ in range(64)]
    blk_bf = [bfA.tile(512) for _ in range(ND)]
    w1buf = bfA.tiles(2, 2048)
    w2buf = bfA.tiles(2, 8192)
    xng_tm = bfA.tile(T)
    xt = [f32A.tile(T)]
    xg = f32A.tile(T)
    junk = xg
    val = f32A.tile(T)
    comp = val
    statc = f32A.tile(6 * NT)
    statg = f32A.tile(6)
    wals = f32A.tile(16)
    mk = f32A.tile(16)
    scan = f32A.tile(16)
    z16 = f32A.tile(16)
    posf = f32A.tile(16)
    big = f32A.tile(16)
    rtl = f32A.tile(16)
    acol = f32A.tile(16)
    bcol = f32A.tile(16)
    small = f32A.tile(32)
    gwv = f32A.tile(4)
    validt = f32A.tile(4)
    slotid = f32A.tile(4)
    rowm = f32A.tile(2)
    wr_rep = f32A.tile(2048)
    b1t = f32A.tile(64)
    b2t = f32A.tile(16)
    pos_i, tokid, idx_sb, mku = IT["pos_i"], IT["tokid"], IT["idx_sb"], IT["mku"]

    x_tiled = x_in.rearrange("(c p) d -> c p d", p=128)
    x_out_tiled = x_out.rearrange("(c p) d -> c p d", p=128)

    lo = small[:, 0:1]; hi = small[:, 1:2]; mid = small[:, 2:3]
    cntp = small[:, 3:4]; onescol = small[:, 4:5]
    midb = small[:, 6:7]; thrb = small[:, 7:8]; k512 = small[:, 8:9]
    cnt_sb = small[:, 9:10]; tt = small[:, 10:11]
    cval = small[:, 11:12]; cb = small[:, 12:13]
    cnt = small[:, 13:14]; base_col = small[:, 14:15]
    gmaxc = small[:, 16:17]; gminc = small[:, 17:18]
    brcol = small[:, 19:20]
    csum_row = comp[0:1, 0:128]
    scanex = comp[0:1, 128:256]
    zrow = comp[0:1, 256:384]
    wT = comp[0:16, 384:512]
    pid = cst["ident_f32"]

    t_wr = pb.op("sync", lambda e: e.dma_start(out=wr_rep[:], in_=wd["l0f_wr_rep"]), inc=s_rb, k=16)
    t_br = pb.op("sync", lambda e: e.dma_start(out=brcol, in_=wd["l0f_br"]), inc=s_rb, k=16)
    t_sl = pb.op("sync", lambda e: e.dma_start(out=slotid[:], in_=wd["l0f_slotid"]), inc=s_rb, k=16)
    t_b1 = pb.op("sync", lambda e: e.dma_start(out=b1t[:], in_=wd["l0f_b1"]), inc=s_rb, k=16)
    t_b2 = pb.op("sync", lambda e: e.dma_start(out=b2t[:], in_=wd["l0f_b2"]), inc=s_rb, k=16)
    t_tk = pb.op("gpsimd", lambda e: e.dma_start(out=tokid[:], in_=wd["l0f_tokid"]), inc=s_gi, k=16)
    t_pf1 = pb.op("gpsimd", lambda e: e.dma_start(out=sp["idx_hbm"], in_=wd["l0f_pad"]), inc=s_gi, k=16)
    t_pf2 = pb.op("gpsimd", lambda e: e.dma_start(out=sp["gw_hbm"], in_=wd["l0f_zero512"]), inc=s_gi, k=16)
    t_z = pb.op("vector", lambda e: e.memset(z16[:], 0.0), inc=s_dv)
    t_big = pb.op("vector", lambda e: e.memset(big[:], 9999.0), inc=s_dv)
    t_on1 = pb.op("vector", lambda e: e.memset(onescol, 1.0), inc=s_dv)
    t_k5 = pb.op("vector", lambda e: e.memset(k512, float(K_SEL)), inc=s_dv)
    t_xgz = pb.op("vector", lambda e: e.memset(xg[:], 0.0), inc=s_dv)

    xfree = [None]
    w_done = []
    prev_xn = None
    for c in range(NT):
        b = 0
        t_ld = pb.op("sync", lambda e, c=c, b=b: e.dma_start(out=xt[b][:], in_=x_tiled[c]),
                     waits=(xfree[b],), inc=s_x[b], k=16)
        st = {k2: statc[:, c * 6 + i:c * 6 + i + 1] for i, k2 in enumerate(("m", "ssq", "var", "r", "rho", "tmp"))}
        t_r, t_sq = stat_ln_rms(pb, cst, xt[b][:], st, (t_ld,), s_dv, s_ac, junk)
        t_mr = pb.op("vector", lambda e, st=st: e.tensor_mul(st["tmp"], st["m"], st["r"]), waits=(t_r,), inc=s_dv)
        t_xn = pb.op("vector", lambda e, b=b, st=st: e.tensor_scalar(val[:], xt[b][:], st["r"], st["tmp"], ALU.mult, ALU.subtract),
                     waits=(t_mr, prev_xn), inc=s_dv)
        t_w = pb.op("vector", lambda e, c=c: e.scalar_tensor_tensor(xg[:], val[:], 1.0, wr_rep[:], ALU.mult, ALU.mult,
                                                                     accum_out=wals[:, c:c + 1]),
                    waits=(t_xn, t_wr, t_xgz), inc=s_dv)
        prev_xn = t_w
        w_done.append(t_w)
        xfree[b] = [t_xn, t_sq]
    t_wb = pb.op("vector", lambda e: e.tensor_scalar(wals[:], wals[:], brcol, None, ALU.add),
                 waits=[w_done[-1], t_br], inc=s_dv)
    t_xgz = pb.op("vector", lambda e: e.memset(xg[:], 0.0), waits=(w_done[-1],), inc=s_dv)
    t_zr = pb.op("vector", lambda e: e.memset(zrow, 0.0), waits=(w_done[-1],), inc=s_dv)

    t_t1 = pb.op("tensor", lambda e: e.transpose(PS[4][0:16, 0:128], wals[:], pid), waits=(t_wb,), inc=s_pe)
    t_wT = pb.op("vector", lambda e: e.tensor_copy(wT, PS[4][0:16, 0:128]), waits=(t_t1,), inc=s_dv)
    t_mx = pb.op("vector", lambda e: e.reduce_max(rowm[0:16, 0:1], wT, axis=AX.X), waits=(t_wT,), inc=s_dv)
    t_mn = pb.op("vector", lambda e: e.tensor_reduce(rowm[0:16, 1:2], wT, AX.X, ALU.min), waits=(t_wT,), inc=s_dv)
    t_t2a = pb.op("tensor", lambda e: e.transpose(PS[5][0:1, 0:16], rowm[0:16, 0:1], pid[0:16, 0:16]), waits=(t_mx,), inc=s_pe)
    t_t2b = pb.op("tensor", lambda e: e.transpose(PS[5][0:1, 16:32], rowm[0:16, 1:2], pid[0:16, 0:16]), waits=(t_mn,), inc=s_pe)
    t_rT = pb.op("vector", lambda e: e.tensor_copy(comp[0:1, 384:416], PS[5][0:1, 0:32]), waits=(t_t2a, t_t2b), inc=s_dv)
    t_gmx = pb.op("vector", lambda e: e.reduce_max(gmaxc[0:1, :], comp[0:1, 384:400], axis=AX.X), waits=(t_rT,), inc=s_dv)
    t_gmn = pb.op("vector", lambda e: e.tensor_reduce(gminc[0:1, :], comp[0:1, 400:416], AX.X, ALU.min), waits=(t_rT,), inc=s_dv)

    t_lo = pb.op("vector", lambda e: e.tensor_scalar(lo[0:1, :], gminc[0:1, :], 1.0, None, ALU.subtract), waits=(t_gmn,), inc=s_dv)
    t_hi = pb.op("vector", lambda e: e.tensor_copy(hi[0:1, :], gmaxc[0:1, :]), waits=(t_gmx,), inc=s_dv)
    t_md = pb.op("vector", lambda e: e.tensor_add(tt[0:1, :], lo[0:1, :], hi[0:1, :]), waits=(t_lo, t_hi), inc=s_dv)
    t_mid = pb.op("vector", lambda e: e.tensor_scalar_mul(mid[0:1, :], tt[0:1, :], 0.5), waits=(t_md,), inc=s_dv)
    pe_free = [t_wT, t_rT]
    for it in range(45):
        pi = it % 2
        psb = PS[4 + pi]
        t_bc = pb.op("tensor", lambda e, psb=psb: e.matmul(psb[:, 0:1], cst["ones_row"], mid[0:1, :], start=True, stop=True),
                     waits=(t_mid, pe_free[pi]), inc=s_pe)
        t_mb = pb.op("vector", lambda e, psb=psb: e.tensor_copy(midb, psb[:, 0:1]), waits=(t_bc,), inc=s_dv)
        t_cp = pb.op("vector", lambda e: e.tensor_scalar(mk[:], wals[:], midb, None, ALU.is_gt, op1=ALU.add, accum_out=cntp),
                     waits=(t_mb,), inc=s_dv)
        t_ct = pb.op("tensor", lambda e, psb=psb: e.matmul(psb[0:1, 4:5], cntp, onescol, start=True, stop=True),
                     waits=(t_cp, t_on1), inc=s_pe)
        t_cs = pb.op("vector", lambda e, psb=psb: e.tensor_copy(cnt_sb[0:1, :], psb[0:1, 4:5]), waits=(t_ct,), inc=s_dv)
        pe_free[pi] = t_cs
        pb.op("vector", lambda e, it=it: e.tensor_copy(wr_rep[0:1, 900 + it:901 + it], cnt_sb[0:1, :]), waits=(t_cs,), inc=s_dv)
        pb.op("vector", lambda e, it=it: e.tensor_copy(wr_rep[0:1, 960 + it:961 + it], mid[0:1, :]), waits=(t_cs,), inc=s_dv)
        condf = small[:, 20:21]
        d1 = small[:, 21:22]
        d2 = small[:, 22:23]
        t_cd = pb.op("vector", lambda e, condf=condf: e.tensor_scalar(condf[0:1, :], cnt_sb[0:1, :], k512[0:1, :], None, ALU.is_ge),
                     waits=(t_cs, t_k5), inc=s_dv)
        t_d1 = pb.op("vector", lambda e, d1=d1: e.tensor_sub(d1[0:1, :], mid[0:1, :], lo[0:1, :]), waits=(t_cd,), inc=s_dv)
        t_d2 = pb.op("vector", lambda e, d2=d2: e.tensor_sub(d2[0:1, :], hi[0:1, :], mid[0:1, :]), waits=(t_cd,), inc=s_dv)
        t_l2 = pb.op("vector", lambda e, d1=d1, condf=condf: e.scalar_tensor_tensor(lo[0:1, :], d1[0:1, :], condf[0:1, :], lo[0:1, :], ALU.mult, ALU.add),
                     waits=(t_d1,), inc=s_dv)
        t_h2 = pb.op("vector", lambda e, d2=d2, condf=condf: e.scalar_tensor_tensor(hi[0:1, :], d2[0:1, :], condf[0:1, :], mid[0:1, :], ALU.mult, ALU.add),
                     waits=(t_d2,), inc=s_dv)
        t_md = pb.op("vector", lambda e: e.tensor_add(tt[0:1, :], lo[0:1, :], hi[0:1, :]), waits=(t_l2, t_h2), inc=s_dv)
        t_mid = pb.op("vector", lambda e: e.tensor_scalar_mul(mid[0:1, :], tt[0:1, :], 0.5), waits=(t_md,), inc=s_dv)
    t_tb = pb.op("tensor", lambda e: e.matmul(PS[4][:, 2:3], cst["ones_row"], hi[0:1, :], start=True, stop=True),
                 waits=(t_mid, pe_free[0]), inc=s_pe)
    t_thr = pb.op("vector", lambda e: e.tensor_copy(thrb, PS[4][:, 2:3]), waits=(t_tb,), inc=s_dv)

    t_mk = pb.op("vector", lambda e: e.tensor_scalar(mk[:], wals[:], thrb, None, ALU.is_gt), waits=(t_thr,), inc=s_dv)
    t_mku = pb.op("vector", lambda e: e.tensor_scalar(mku[:], wals[:], thrb, None, ALU.is_gt), waits=(t_thr,), inc=s_dv)
    rt_toks = []
    for c in range(NT):
        st_r = statc[:, c * 6 + 3:c * 6 + 4]
        st_m = statc[:, c * 6 + 0:c * 6 + 1]
        t1 = pb.op("vector", lambda e, c=c: e.tensor_scalar(rtl[:, c:c + 1], mk[:, c:c + 1], -1.0, 1.0, ALU.mult, ALU.add),
                   waits=(t_mk,), inc=s_dv)
        t2 = pb.op("vector", lambda e, c=c, st_r=st_r: e.tensor_mul(rtl[:, c:c + 1], rtl[:, c:c + 1], st_r), waits=(t1,), inc=s_dv)
        t3 = pb.op("vector", lambda e, c=c: e.tensor_scalar(acol[:, c:c + 1], rtl[:, c:c + 1], 1.0, None, ALU.add), waits=(t2,), inc=s_dv)
        t4 = pb.op("vector", lambda e, c=c, st_m=st_m: e.tensor_mul(bcol[:, c:c + 1], rtl[:, c:c + 1], st_m), waits=(t2,), inc=s_dv)
        rt_toks.append((t3, t4))

    t_cnt = pb.op("vector", lambda e: e.reduce_sum(cnt, mk[:], axis=AX.X), waits=(t_mk,), inc=s_dv)
    t_sc = pb.op("vector", lambda e: e.tensor_tensor_scan(scan[:], mk[:], z16[:], 0.0, ALU.add, ALU.add),
                 waits=(t_mk, t_z), inc=s_dv)
    t_t3 = pb.op("tensor", lambda e: e.transpose(PS[5][0:1, 32:160], cnt, pid), waits=(t_cnt, pe_free[1], t_rT), inc=s_pe)
    t_cr = pb.op("vector", lambda e: e.tensor_copy(csum_row, PS[5][0:1, 32:160]), waits=(t_t3,), inc=s_dv)
    t_s2 = pb.op("vector", lambda e: e.tensor_tensor_scan(scanex, csum_row, zrow, 0.0, ALU.add, ALU.add),
                 waits=(t_cr, t_zr), inc=s_dv)
    t_ex = pb.op("vector", lambda e: e.tensor_sub(scanex, scanex, csum_row), waits=(t_s2,), inc=s_dv)
    t_c1 = pb.op("tensor", lambda e: e.matmul(PS[4][0:1, 3:4], cnt, onescol, start=True, stop=True),
                 waits=(t_cnt, t_thr), inc=s_pe)
    t_c2 = pb.op("vector", lambda e: e.tensor_copy(cval[0:1, :], PS[4][0:1, 3:4]), waits=(t_c1,), inc=s_dv)
    t_c3 = pb.op("tensor", lambda e: e.matmul(PS[5][:, 2:3], cst["ones_row"], cval[0:1, :], start=True, stop=True),
                 waits=(t_c2, t_cr), inc=s_pe)
    t_c4 = pb.op("vector", lambda e: e.tensor_copy(cb, PS[5][:, 2:3]), waits=(t_c3,), inc=s_dv)
    t_t4 = pb.op("tensor", lambda e: e.transpose(PS[4][:, 3:4], scanex, pid[0:1, 0:1]), waits=(t_ex, t_c2), inc=s_pe)
    t_bc2 = pb.op("vector", lambda e: e.tensor_copy(base_col, PS[4][:, 3:4]), waits=(t_t4,), inc=s_dv)
    t_p1 = pb.op("vector", lambda e: e.tensor_scalar(posf[:], scan[:], base_col, 1.0, ALU.add, ALU.subtract),
                 waits=(t_sc, t_bc2), inc=s_dv)
    t_p2a = pb.op("vector", lambda e: e.tensor_sub(posf[:], posf[:], big[:]), waits=(t_p1, t_big), inc=s_dv)
    t_p2b = pb.op("vector", lambda e: e.tensor_mul(posf[:], posf[:], mk[:]), waits=(t_p2a, t_mk), inc=s_dv)
    t_p2 = pb.op("vector", lambda e: e.tensor_add(posf[:], posf[:], big[:]), waits=(t_p2b,), inc=s_dv)
    t_p3 = pb.op("vector", lambda e: e.tensor_copy(pos_i[:], posf[:]), waits=(t_p2,), inc=s_dv)

    sc_last = None
    for c in range(NT):
        sc_last = pb.op("gpsimd", lambda e, c=c: e.indirect_dma_start(
            out=sp["idx_hbm"], out_offset=IndirectOffsetOnAxis(pos_i[:, c:c + 1], 0),
            in_=tokid[:, c:c + 1], in_offset=None, bounds_check=511, oob_is_err=False),
            waits=(t_p3, t_tk, t_pf1, t_pf2) if c == 0 else (), inc=s_gi, k=16)
        sc_last = pb.op("gpsimd", lambda e, c=c: e.indirect_dma_start(
            out=sp["gw_hbm"], out_offset=IndirectOffsetOnAxis(pos_i[:, c:c + 1], 0),
            in_=wals[:, c:c + 1], in_offset=None, bounds_check=511, oob_is_err=False),
            inc=s_gi, k=16)
    t_ri = pb.op("sync", lambda e: e.dma_start(out=idx_sb[:], in_=sp["idx_hbm"].rearrange("(p g) one -> p (g one)", g=4)),
                 waits=(sc_last,), inc=s_rb, k=16)
    t_rw = pb.op("sync", lambda e: e.dma_start(out=gwv[:], in_=sp["gw_hbm"].rearrange("(p g) one -> p (g one)", g=4)),
                 inc=s_rb, k=16)
    t_vd = pb.op("vector", lambda e: e.tensor_scalar(validt[:], slotid[:], cb, None, ALU.is_lt),
                 waits=(t_rw, t_c4, t_sl), inc=s_dv)
    t_gv = pb.op("vector", lambda e: e.tensor_mul(gwv[:], gwv[:], validt[:]), waits=(t_vd, t_ri), inc=s_dv)

    ptfree = [None, None]
    prev_g = None
    for g in range(4):
        t_ga = pb.op("gpsimd", lambda e, g=g: e.indirect_dma_start(
            out=xg[:], out_offset=None, in_=x_in, in_offset=IndirectOffsetOnAxis(idx_sb[:, g:g + 1], 0),
            bounds_check=T - 1, oob_is_err=False),
            waits=(t_ri, t_xgz, prev_g), inc=s_gi, k=16)
        stg = {k2: statg[:, i:i + 1] for i, k2 in enumerate(("m", "ssq", "var", "r", "rho", "tmp"))}
        t_r2, t_sq2 = stat_ln_rms(pb, cst, xg[:], stg, (t_ga,), s_dv, s_ac, val)
        t_mr2 = pb.op("vector", lambda e, stg=stg: e.tensor_mul(stg["tmp"], stg["m"], stg["r"]), waits=(t_r2,), inc=s_dv)
        t_xn2 = pb.op("vector", lambda e, stg=stg: e.tensor_scalar(xng_tm[:], xg[:], stg["r"], stg["tmp"], ALU.mult, ALU.subtract),
                      waits=(t_mr2,), inc=s_dv)
        tp_last = None
        for dch in range(ND):
            sl = PT[dch % 2]
            t_tp = pb.op("tensor", lambda e, dch=dch, sl=sl: e.transpose(sl, xng_tm[:, dch * 128:dch * 128 + 128], cst["ident"]),
                         waits=(t_xn2, ptfree[dch % 2]), inc=s_pe)
            t_ev = pb.op("vector", lambda e, g=g, dch=dch, sl=sl: e.tensor_copy(xng_fm[dch][:, g * 128:g * 128 + 128], sl),
                         waits=(t_tp,), inc=s_dv)
            ptfree[dch % 2] = t_ev
            tp_last = t_tp
        prev_g = [tp_last, t_sq2, t_xn2]
    xng_all = [ptfree[0], ptfree[1]]

    w1_free = [None, None]
    psfree01 = [None, None]
    h_toks = []
    for ffc in range(64):
        wb = ffc % 2
        t_lw = pb.op("gpsimd", lambda e, ffc=ffc, wb=wb: e.dma_start(
            out=w1buf[wb].rearrange("p (c m) -> p c m", m=128), in_=wd["l0f_w1t"][ffc].rearrange("c p m -> p c m")),
            waits=(w1_free[wb],), inc=(s_w if wb == 0 else s_w2), k=16)
        ps = PS[wb]
        mm = None
        for dc in range(ND):
            mm = pb.op("tensor", lambda e, dc=dc, wb=wb, ps=ps: e.matmul(
                ps, w1buf[wb][:, dc * 128:dc * 128 + 128], xng_fm[dc][:], start=(dc == 0), stop=(dc == ND - 1)),
                waits=(t_lw, psfree01[wb], xng_all[0], xng_all[1]) if dc == 0 else (),
                inc=s_pe if dc == ND - 1 else None)
        w1_free[wb] = mm
        t_h = pb.op("scalar", lambda e, ffc=ffc, ps=ps: e.activation(h_fm[ffc][:], ps, AF.Gelu_apprx_tanh, bias=b1t[:, ffc:ffc + 1]),
                    waits=(mm, t_b1), inc=s_ac)
        psfree01[wb] = t_h
        h_toks.append(t_h)

    w2_free = [None, None]
    psfree23 = [None, None]
    blk_toks = []
    for dc in range(ND):
        wb = dc % 2
        t_lw = pb.op("gpsimd", lambda e, dc=dc, wb=wb: e.dma_start(
            out=w2buf[wb].rearrange("p (c m) -> p c m", m=128), in_=wd["l0f_w2t"][dc].rearrange("c p m -> p c m")),
            waits=(w2_free[wb], h_toks[-1] if dc <= 1 else None), inc=(s_w if wb == 0 else s_w2), k=16)
        ps = PS[2 + wb]
        mm = None
        for ffc in range(64):
            mm = pb.op("tensor", lambda e, ffc=ffc, wb=wb, ps=ps: e.matmul(
                ps, w2buf[wb][:, ffc * 128:ffc * 128 + 128], h_fm[ffc][:], start=(ffc == 0), stop=(ffc == 63)),
                waits=(t_lw, psfree23[wb], h_toks[-1]) if ffc == 0 else (),
                inc=s_pe if ffc == 63 else None)
        w2_free[wb] = mm
        t_bk = pb.op("scalar", lambda e, dc=dc, ps=ps: e.activation(blk_bf[dc][:], ps, AF.Identity, bias=b2t[:, dc:dc + 1]),
                     waits=(mm, t_b2), inc=s_ac)
        psfree23[wb] = t_bk
        blk_toks.append(t_bk)

    st_base = []
    for c in range(NT):
        b = 0
        t_ldx = pb.op("sync", lambda e, c=c, b=b: e.dma_start(out=xt[b][:], in_=x_tiled[c]),
                      waits=(xfree[b],), inc=s_x[b], k=16)
        t_bs = pb.op("vector", lambda e, c=c, b=b: e.tensor_scalar(xt[b][:], xt[b][:], acol[:, c:c + 1], bcol[:, c:c + 1],
                                                                    ALU.mult, ALU.subtract),
                     waits=(t_ldx, rt_toks[c][0], rt_toks[c][1]), inc=s_dv)
        t_st = pb.op("sync", lambda e, c=c, b=b: e.dma_start(out=x_out_tiled[c], in_=xt[b][:]),
                     waits=(t_bs,), inc=s_x[b], k=16)
        st_base.append(t_st)
        xfree[b] = [t_st]

    t_sa = None
    NG = 4
    for g in range(NG):
        toks = []
        for dc in range(ND):
            sl = PT[dc % 2]
            t_tp = pb.op("tensor", lambda e, g=g, dc=dc, sl=sl: e.transpose(sl, blk_bf[dc][:, g * 128:g * 128 + 128], cst["ident"]),
                         waits=(blk_toks[dc], ptfree[dc % 2]), inc=s_pe)
            t_vv = pb.op("vector", lambda e, g=g, dc=dc, sl=sl: e.tensor_scalar(val[:, dc * 128:dc * 128 + 128], sl,
                                                                                 gwv[:, g:g + 1], None, ALU.mult),
                         waits=(t_tp, t_gv, t_sa), inc=s_dv)
            ptfree[dc % 2] = t_vv
            toks.append(t_vv)
        t_sa = pb.op("gpsimd", lambda e, g=g: e.indirect_dma_start(
            out=x_out, out_offset=IndirectOffsetOnAxis(idx_sb[:, g:g + 1], 0),
            in_=val[:], in_offset=None, compute_op=ALU.add, bounds_check=T - 1, oob_is_err=False),
            waits=toks + st_base, inc=s_gi, k=16)

    bfA.pop(); f32A.pop()
    return [t_sa]




H = 16
QK_NOPE = 128
QK_ROPE = 64
V_HEAD = 128
KV_LORA = 512
Q_LORA = 1536
MLA_SCALE = float(192.0 ** -0.5)


def _deint_perm():
    p = np.empty(64, np.int64)
    p[:32] = np.arange(0, 64, 2)
    p[32:] = np.arange(1, 64, 2)
    return p


def prep_l1_attn(inp):
    d = {}
    perm = _deint_perm()
    qa = np.asarray(inp["L1_attn_q_a_w"], np.float32)
    d["l1a_qa"] = rows_chunks(qa)
    kva = np.asarray(inp["L1_attn_kv_a_w"], np.float32).copy()
    kva[:, 512:576] = kva[:, 512:576][:, perm]
    d["l1a_kva"] = rows_chunks(kva)
    qb = np.asarray(inp["L1_attn_q_b_w"], np.float32).copy()
    qb3 = qb.reshape(1536, 16, 192)
    qb3[:, :, 128:192] = qb3[:, :, 128:192][:, :, perm[:64] - 0] if False else qb3[:, :, 128:192]
    qbp = qb.reshape(1536, 16, 192).copy()
    qbp[:, :, 128:] = qbp[:, :, 128:][:, :, perm]
    d["l1a_qb"] = np.ascontiguousarray(qbp.reshape(12, 128, 16, 192).transpose(2, 0, 1, 3))
    kvb = np.asarray(inp["L1_attn_kv_b_w"], np.float32)
    d["l1a_kvb"] = np.ascontiguousarray(kvb.reshape(4, 128, 16, 256).transpose(2, 0, 1, 3))
    d["l1a_ow"] = rows_chunks(np.asarray(inp["L1_attn_o_w"], np.float32))
    d["l1a_qln"] = np.ascontiguousarray(np.asarray(inp["L1_attn_q_a_ln_w"], np.float32).reshape(12, 128).T)
    d["l1a_kvln"] = np.ascontiguousarray(np.asarray(inp["L1_attn_kv_a_ln_w"], np.float32).reshape(4, 128).T)
    i = np.arange(0, 64, 2, dtype=np.float64) / 64.0
    inv_freq = 1.0 / (500000.0 ** i)
    t = np.arange(2048, dtype=np.float64)
    fr = np.outer(t, inv_freq)
    cos = np.cos(fr); sin = np.sin(fr)
    cos_tm = cos.reshape(16, 128, 32).transpose(1, 0, 2).reshape(128, 512)
    sin_tm = sin.reshape(16, 128, 32).transpose(1, 0, 2).reshape(128, 512)
    d["l1a_costm"] = np.ascontiguousarray(cos_tm.astype(np.float32))
    d["l1a_sintm"] = np.ascontiguousarray(sin_tm.astype(np.float32))
    cos_fm = np.concatenate([cos.T, cos.T], axis=0)
    sin_fm = np.concatenate([sin.T, sin.T], axis=0)
    d["l1a_cosfm"] = np.ascontiguousarray(cos_fm.astype(np.float32))
    d["l1a_sinfm"] = np.ascontiguousarray(sin_fm.astype(np.float32))
    return d


L1A_META = [
    ("l1a_qa", (16, 128, 1536), F32), ("l1a_kva", (16, 128, 576), F32),
    ("l1a_qb", (16, 12, 128, 192), F32), ("l1a_kvb", (16, 4, 128, 256), F32),
    ("l1a_ow", (16, 128, 2048), F32),
    ("l1a_qln", (128, 12), F32), ("l1a_kvln", (128, 4), F32),
    ("l1a_costm", (128, 512), F32), ("l1a_sintm", (128, 512), F32),
    ("l1a_cosfm", (64, 2048), F32), ("l1a_sinfm", (64, 2048), F32),
]


def xn3_blk(sp, tc):
    return sp["_xn3_blk"][tc]


def attn_blk(sp, tc):
    return sp["_attn_blk"][tc]


L1A_PH = set("p1,p3,p5".split(","))
L1A_NH = 16

def emit_l1_attn(pb, A, x_in, x_out, wd, cst, S, sp):
    nc = pb.nc
    bfA, f32A = A["bf"], A["f32"]
    s_w, s_w2, s_pe, s_dv, s_ac = S["s_w"], S["s_w2"], S["s_pe"], S["s_dv"], S["s_ac"]
    s_x = [S["s_x0"], S["s_x1"]]
    s_ev = [S[f"s_ev{i}"] for i in range(4)]
    s_xnt = [S["s_q"], S["s_g0"]]
    s_qb = [S["s_k0"], S["s_k1"]]
    s_kvb = S["s_v"]
    s_sp2 = S["s_g1"]
    s_rb = S["s_rb"]
    PS, PT = A["PS"], A["PT"]

    sp["obfree"] = [None, None]
    sp["evt3free"] = [None, None]
    sp["last_osp"] = None
    bfA.push(); f32A.push()
    t_fm = [bfA.tile(T) for _ in range(12)]
    cn_fm = [bfA.tile(T) for _ in range(4)]
    krope_fm = bfA.tile(T)
    costm = bfA.tile(512); sintm = bfA.tile(512)
    cosfm = bfA.tile(T); sinfm = bfA.tile(T)
    qln = f32A.tile(12); kvln = f32A.tile(4)
    xt = f32A.tiles(2, T)
    junk = f32A.tile(T)
    statc = f32A.tile(6 * NT)
    strow = f32A.tile(64)

    x_tiled = x_in.rearrange("(c p) d -> c p d", p=128)
    x_out_tiled = x_out.rearrange("(c p) d -> c p d", p=128)

    t_ct1 = pb.op("gpsimd", lambda e: e.dma_start(out=costm[:], in_=wd["l1a_costm"]), inc=s_w, k=16)
    t_ct2 = pb.op("gpsimd", lambda e: e.dma_start(out=sintm[:], in_=wd["l1a_sintm"]), inc=s_w, k=16)
    t_ct3 = pb.op("gpsimd", lambda e: e.dma_start(out=cosfm[0:64, :], in_=wd["l1a_cosfm"]), inc=s_w, k=16)
    t_ct4 = pb.op("gpsimd", lambda e: e.dma_start(out=sinfm[0:64, :], in_=wd["l1a_sinfm"]), inc=s_w, k=16)
    t_ql = pb.op("sync", lambda e: e.dma_start(out=qln[:], in_=wd["l1a_qln"]), inc=s_rb, k=16)
    t_kl = pb.op("sync", lambda e: e.dma_start(out=kvln[:], in_=wd["l1a_kvln"]), inc=s_rb, k=16)

    bfA.push()
    xn_bf = bfA.tile(T)
    evtp = bfA.tiles(2, 128)
    xfree = [None, None]
    ptfree = [None, None]
    evfree = [None, None]
    xnb_free = None
    for c in range(NT):
        b = c % 2
        t_ld = pb.op("sync", lambda e, c=c, b=b: e.dma_start(out=xt[b][:], in_=x_tiled[c]),
                     waits=(xfree[b],), inc=s_x[b], k=16)
        st = {k2: statc[:, c * 6 + i:c * 6 + i + 1] for i, k2 in enumerate(("m", "ssq", "var", "r", "rho", "tmp"))}
        t_r, t_sq = stat_ln_rms(pb, cst, xt[b][:], st, (t_ld,), s_dv, s_ac, junk)
        t_mr = pb.op("vector", lambda e, st=st: e.tensor_mul(st["tmp"], st["m"], st["r"]), waits=(t_r,), inc=s_dv)
        t_xn = pb.op("vector", lambda e, b=b, st=st: e.tensor_scalar(xn_bf[:], xt[b][:], st["r"], st["tmp"], ALU.mult, ALU.subtract),
                     waits=(xnb_free, t_mr), inc=s_dv)
        xfree[b] = [t_xn, t_sq]
        tp_last = None
        for dch in (range(ND) if "p0tp" not in set("".split(",")) else range(0)):
            pi = dch % 2
            sl = PT[pi]
            t_tp = pb.op("tensor", lambda e, dch=dch, sl=sl: e.transpose(sl, xn_bf[:, dch * 128:dch * 128 + 128], cst["ident"]),
                         waits=(t_xn, ptfree[pi]), inc=s_pe)
            t_ev = pb.op("vector", lambda e, pi=pi, sl=sl: e.tensor_copy(evtp[pi][:], sl),
                         waits=(t_tp, evfree[pi]), inc=s_dv)
            ptfree[pi] = t_ev
            if "p0sp" in set("".split(",")):
                evfree[pi] = t_ev
            else:
                t_sp = pb.op("sync", lambda e, c=c, dch=dch, pi=pi: e.dma_start(out=sp["xn3_sp"][c][dch], in_=evtp[pi][:]),
                             waits=(t_ev,), inc=s_ev[pi], k=16)
                evfree[pi] = t_sp
            tp_last = t_tp
        xnb_free = tp_last
    pb.fence([evfree[0], evfree[1]])
    bfA.pop()
    if "p1" not in L1A_PH:
        bfA.pop(); f32A.pop()
        return [evfree[0], evfree[1]]

    bfA.push()
    qa_sb = [bfA.tile(1536) for _ in range(ND)]
    kva_sb = [bfA.tile(576) for _ in range(ND)]
    xnt_sb = bfA.tiles(2, 2048)
    ttm = bfA.tile(1536)
    ctm = bfA.tile(576)
    tscr = f32A.tile(1536)
    psq = strow

    t_qa = None
    for dch in range(ND):
        t_qa = pb.op("gpsimd", lambda e, dch=dch: e.dma_start(out=qa_sb[dch][:], in_=wd["l1a_qa"][dch]), inc=s_w, k=16)
    t_kva = None
    for dch in range(ND):
        t_kva = pb.op("gpsimd", lambda e, dch=dch: e.dma_start(out=kva_sb[dch][:], in_=wd["l1a_kva"][dch]), inc=s_w2, k=16)

    L1A_NTC = 16
    L1A_SUB = set("t,tstat,ttm,tp,c,rope,ctp".split(","))
    xnt_free = [None, None]
    psA = [None, None, None]
    psB = [None, None]
    ptB = [None, None]
    ttm_free = None
    ctm_free = None
    last_tfm = [None, None]
    last_cn = []
    for tc in range(L1A_NTC):
        b = tc % 2
        t_xl = pb.op("sync", lambda e, tc=tc, b=b: e.dma_start(
            out=xnt_sb[b].rearrange("p (c m) -> p c m", m=128), in_=xn3_blk(sp, tc).rearrange("c p m -> p c m")),
            waits=(xnt_free[b],), inc=s_xnt[b], k=16)
        ev_toks = []
        mm_toks = []
        for g in range(3):
            ps = PS[g]
            mm = None
            for d in range(ND):
                mm = pb.op("tensor", lambda e, d=d, g=g, b=b, ps=ps: e.matmul(
                    ps, xnt_sb[b][:, d * 128:d * 128 + 128], qa_sb[d][:, g * 512:g * 512 + 512],
                    start=(d == 0), stop=(d == ND - 1)),
                    waits=(t_xl, t_qa, psA[g]) if d == 0 else (),
                    inc=s_pe if d == ND - 1 else None)
            t_pef = pe_flush(pb, cst, A, s_pe)
            t_cp = pb.op("vector", lambda e, g=g, ps=ps: e.tensor_copy(tscr[:, g * 512:g * 512 + 512], ps),
                         waits=(t_pef, ttm_free), inc=s_dv)
            t_s = pb.op("scalar", lambda e, g=g, ps=ps: e.activation(junk[:, 0:512], ps, AF.Square, accum_out=psq[:, g:g + 1]),
                        waits=(mm, t_cp), inc=s_ac)
            psA[g] = [t_cp, t_s]
            ev_toks.extend([t_cp, t_s])
            mm_toks.append(mm)
        if "tstat" not in L1A_SUB:
            xnt_free[b] = [ev_toks[0]]
            continue
        t_afl = act_flush(pb, cst, s_ac)
        t_a1 = pb.op("vector", lambda e: e.tensor_add(psq[:, 3:4], psq[:, 0:1], psq[:, 1:2]), waits=ev_toks + [t_afl], inc=s_dv)
        t_a2 = pb.op("vector", lambda e: e.tensor_add(psq[:, 3:4], psq[:, 3:4], psq[:, 2:3]), waits=(t_a1,), inc=s_dv)
        t_a3 = pb.op("scalar", lambda e: e.activation(psq[:, 4:5], psq[:, 3:4], AF.Sqrt, bias=cst["mla_eps"], scale=1.0 / Q_LORA),
                     waits=(t_a2,), inc=s_ac)
        t_a4 = pb.op("vector", lambda e: e.reciprocal(psq[:, 5:6], psq[:, 4:5]), waits=(t_a3,), inc=s_dv)
        t_a5 = pb.op("vector", lambda e: e.tensor_scalar_mul(psq[:, 5:6], psq[:, 5:6], MLA_SCALE), waits=(t_a4,), inc=s_dv)
        if "ttm" not in L1A_SUB:
            xnt_free[b] = [ev_toks[0], t_a5]
            continue
        t_tn = pb.op("vector", lambda e: e.tensor_scalar(ttm[:], tscr[:], psq[:, 5:6], None, ALU.mult), waits=(t_a5, ttm_free), inc=s_dv)
        ttm_free = None
        for da in (range(12) if "tp" in L1A_SUB else range(0)):
            pi = da % 2
            sl = PT[pi]
            t_tp = pb.op("tensor", lambda e, da=da, sl=sl: e.transpose(sl, ttm[:, da * 128:da * 128 + 128], cst["ident"]),
                         waits=(t_tn, ptB[pi]), inc=s_pe)
            t_ev = pb.op("vector", lambda e, da=da, tc=tc, sl=sl: e.tensor_scalar(t_fm[da][:, tc * 128:tc * 128 + 128], sl,
                                                                                   qln[:, da:da + 1], None, ALU.mult),
                         waits=(t_tp, t_ql), inc=s_dv)
            ptB[pi] = t_ev
            if da == 11:
                ttm_free = t_tp
        last_tfm = [ptB[0], ptB[1]]
        if "c" not in L1A_SUB:
            xnt_free[b] = [ttm_free if ttm_free is not None else t_tn]
            last_tfm = [ptB[0], ptB[1]]
            continue
        mmc = [None, None]
        for g, (o0, nn_) in enumerate(((0, 512), (512, 64))):
            ps = PS[3 + g]
            mm = None
            for d in range(ND):
                mm = pb.op("tensor", lambda e, d=d, o0=o0, nn_=nn_, b=b, ps=ps: e.matmul(
                    ps[:, 0:nn_], xnt_sb[b][:, d * 128:d * 128 + 128], kva_sb[d][:, o0:o0 + nn_],
                    start=(d == 0), stop=(d == ND - 1)),
                    waits=(t_xl, t_kva, psB[g]) if d == 0 else (),
                    inc=s_pe if d == ND - 1 else None)
            mmc[g] = mm
        t_pef2 = pe_flush(pb, cst, A, s_pe)
        t_cs = pb.op("scalar", lambda e: e.activation(junk[:, 512:1024], PS[3][:, 0:512], AF.Square, accum_out=psq[:, 8:9]),
                     waits=(t_pef2,), inc=s_ac)
        t_k3 = pb.op("scalar", lambda e: e.activation(psq[:, 9:10], psq[:, 8:9], AF.Sqrt, bias=cst["mla_eps"], scale=1.0 / KV_LORA),
                     waits=(t_cs,), inc=s_ac)
        t_k4 = pb.op("vector", lambda e: e.reciprocal(psq[:, 10:11], psq[:, 9:10]), waits=(t_k3,), inc=s_dv)
        t_ce1 = pb.op("vector", lambda e: e.tensor_scalar(ctm[:, 0:512], PS[3][:, 0:512], psq[:, 10:11], None, ALU.mult),
                      waits=(t_k4, t_cs, ctm_free), inc=s_dv)
        t_ce2 = pb.op("vector", lambda e: e.tensor_copy(ctm[:, 512:576], PS[4][:, 0:64]), waits=(t_pef2, mmc[1], ctm_free), inc=s_dv)
        psB[0] = t_ce1
        psB[1] = t_ce2
        if "rope" not in L1A_SUB:
            te = tf = t_ce2
            psB[0] = t_ce1; psB[1] = t_ce2
            xnt_free[b] = [mmc[0], mmc[1], ttm_free]
            last_tfm = [ptB[0], ptB[1]]
            continue
        p1 = junk[:, 1024:1056]; p2 = junk[:, 1056:1088]; p3 = junk[:, 1088:1120]; p4 = junk[:, 1120:1152]
        ta = pb.op("vector", lambda e, tc=tc: e.tensor_mul(p1, ctm[:, 512:544], costm[:, tc * 32:tc * 32 + 32]), waits=(t_ce2, t_ct1), inc=s_dv)
        tb = pb.op("vector", lambda e, tc=tc: e.tensor_mul(p2, ctm[:, 544:576], sintm[:, tc * 32:tc * 32 + 32]), waits=(t_ce2, t_ct2), inc=s_dv)
        tc_ = pb.op("vector", lambda e, tc=tc: e.tensor_mul(p3, ctm[:, 544:576], costm[:, tc * 32:tc * 32 + 32]), waits=(t_ce2,), inc=s_dv)
        td = pb.op("vector", lambda e, tc=tc: e.tensor_mul(p4, ctm[:, 512:544], sintm[:, tc * 32:tc * 32 + 32]), waits=(t_ce2,), inc=s_dv)
        te = pb.op("vector", lambda e: e.tensor_sub(ctm[:, 512:544], p1, p2), waits=(ta, tb), inc=s_dv)
        tf = pb.op("vector", lambda e: e.tensor_add(ctm[:, 544:576], p3, p4), waits=(tc_, td), inc=s_dv)
        if "ctp" not in L1A_SUB:
            xnt_free[b] = [mmc[0], mmc[1], ttm_free]
            last_tfm = [ptB[0], ptB[1]]
            ctm_free = [te, tf]
            continue
        ctm_parts = []
        for dc in range(4):
            pi = dc % 2
            sl = PT[pi]
            t_tp = pb.op("tensor", lambda e, dc=dc, sl=sl: e.transpose(sl, ctm[:, dc * 128:dc * 128 + 128], cst["ident"]),
                         waits=(t_ce1, ptB[pi]), inc=s_pe)
            t_ev = pb.op("vector", lambda e, dc=dc, tc=tc, sl=sl: e.tensor_copy(cn_fm[dc][:, tc * 128:tc * 128 + 128], sl),
                         waits=(t_tp,), inc=s_dv)
            ptB[pi] = t_ev
            ctm_parts.append(t_tp)
        sl = PT[0]
        t_tpk = pb.op("tensor", lambda e, sl=sl: e.transpose(sl[0:64, :], ctm[:, 512:576], cst["ident"]),
                      waits=(te, tf, ptB[0]), inc=s_pe)
        t_evk = pb.op("vector", lambda e, tc=tc, sl=sl: e.tensor_copy(krope_fm[0:64, tc * 128:tc * 128 + 128], sl[0:64, :]),
                      waits=(t_tpk,), inc=s_dv)
        ptB[0] = t_evk
        last_cn = [ptB[0], ptB[1]]
        ctm_free = [t_tpk] + ctm_parts
        xnt_free[b] = [mmc[0], mmc[1], ttm_free]
    if sp.get("l1a_dbg") is not None:
        dbg = sp["l1a_dbg"]
        pb.op("gpsimd", lambda e: e.dma_start(out=dbg[0], in_=t_fm[0][:]), waits=last_tfm, inc=S["s_gi"], k=16)
        pb.op("gpsimd", lambda e: e.dma_start(out=dbg[1], in_=cn_fm[0][:]), waits=last_cn, inc=S["s_gi"], k=16)
        pb.op("gpsimd", lambda e: e.dma_start(out=dbg[2][:, 0:1536], in_=ttm[:]), waits=last_tfm, inc=S["s_gi"], k=16)
        pb.op("gpsimd", lambda e: e.dma_start(out=dbg[3][:, 0:576], in_=ctm[:]), waits=last_cn, inc=S["s_gi"], k=16)
        pb.op("gpsimd", lambda e: e.dma_start(out=dbg[4][:, 0:1536], in_=tscr[:]), waits=last_tfm, inc=S["s_gi"], k=16)
        pb.op("gpsimd", lambda e: e.dma_start(out=dbg[5][:, 0:64], in_=strow[:]), waits=last_tfm, inc=S["s_gi"], k=16)
    pb.fence(last_tfm + last_cn)
    bfA.pop()
    if "p3" not in L1A_PH:
        bfA.pop(); f32A.pop()
        return last_tfm + last_cn

    bfA.push()
    kvbh = bfA.tile(1024)
    qbh = bfA.tile(2304)
    kvh = [bfA.tile(256) for _ in range(NT)]
    knope = bfA.tile(T)
    qnope = bfA.tile(T)
    qrraw = bfA.tile(T)
    qswap = bfA.tile(T)
    qrope = bfA.tile(T)
    P_sb = bfA.tile(T)
    ptsb = bfA.tiles(2, 128)
    ob_sb = bfA.tiles(2, 128)
    evt3 = bfA.tiles(2, 128)

    kvb_free = None
    qb_free = [None, None]
    prev_attn = None
    psS_free = [None, None, None, None]
    ps4_free = None
    ps5_free = None
    qsw_free = None
    for h in range(min(H, L1A_NH)):
        t_kvb = pb.op("gpsimd", lambda e, h=h: e.dma_start(
            out=kvbh.rearrange("p (c m) -> p c m", m=256), in_=wd["l1a_kvb"][h].rearrange("c p m -> p c m")),
            waits=(kvb_free,), inc=s_kvb, k=16)
        kvsc = []
        for dc in range(4):
            tsc = pb.op("vector", lambda e, dc=dc: e.tensor_scalar(kvbh[:, dc * 256:dc * 256 + 256], kvbh[:, dc * 256:dc * 256 + 256],
                                                                    kvln[:, dc:dc + 1], None, ALU.mult),
                        waits=(t_kvb, t_kl, prev_attn), inc=s_dv)
            kvsc.append(tsc)
        kvh_ev = [None] * NT
        last_kv_mm = None
        for tc in range(NT):
            ps = PS[4 + tc % 2]
            psfree = ps4_free if tc % 2 == 0 else ps5_free
            mm = None
            for dc in range(4):
                mm = pb.op("tensor", lambda e, dc=dc, tc=tc, ps=ps: e.matmul(
                    ps[:, 0:256], cn_fm[dc][:, tc * 128:tc * 128 + 128], kvbh[:, dc * 256:dc * 256 + 256],
                    start=(dc == 0), stop=(dc == 3)),
                    waits=tuple(kvsc) + (psfree, prev_attn) if dc == 0 else (),
                    inc=s_pe if dc == 3 else None)
            t_ev = pb.op("vector", lambda e, tc=tc, ps=ps: e.tensor_copy(kvh[tc][:], ps[:, 0:256]), waits=(mm, prev_attn), inc=s_dv)
            kvh_ev[tc] = t_ev
            if tc % 2 == 0:
                ps4_free = t_ev
            else:
                ps5_free = t_ev
            last_kv_mm = mm
        kvb_free = last_kv_mm
        ptb2 = [None, None]
        for tc in range(NT):
            pi = tc % 2
            sl = PT[pi]
            t_tp = pb.op("tensor", lambda e, tc=tc, sl=sl: e.transpose(sl, kvh[tc][:, 0:128], cst["ident"]),
                         waits=(kvh_ev[tc], ptb2[pi], prev_attn), inc=s_pe)
            t_ev = pb.op("vector", lambda e, tc=tc, sl=sl: e.tensor_copy(knope[:, tc * 128:tc * 128 + 128], sl),
                         waits=(t_tp,), inc=s_dv)
            ptb2[pi] = t_ev
        kn_done = [ptb2[0], ptb2[1]]
        t_qb = pb.op("gpsimd", lambda e, h=h: e.dma_start(
            out=qbh.rearrange("p (c m) -> p c m", m=192), in_=wd["l1a_qb"][h].rearrange("c p m -> p c m")),
            waits=(qb_free[0],), inc=s_qb[0], k=16)
        mmn = [None] * 4
        for da in range(12):
            for g in range(4):
                w0 = (t_qb, psS_free[g], prev_attn) if da == 0 else ()
                mmn[g] = pb.op("tensor", lambda e, da=da, g=g: e.matmul(
                    PS[g], qbh[:, da * 192:da * 192 + 128], t_fm[da][:, g * 512:g * 512 + 512],
                    start=(da == 0), stop=(da == 11)),
                    waits=w0, inc=s_pe if da == 11 else None)
        qn_ev = []
        for g in range(4):
            t_ev = pb.op("vector", lambda e, g=g: e.tensor_copy(qnope[:, g * 512:g * 512 + 512], PS[g]),
                         waits=(mmn[g], prev_attn), inc=s_dv)
            qn_ev.append(t_ev)
            psS_free[g] = t_ev
        qr_ev = []
        mmr_last = None
        for g in range(4):
            ps = PS[4 + g % 2]
            psfree = ps4_free if g % 2 == 0 else ps5_free
            mm = None
            for da in range(12):
                w0 = (t_qb, psfree, prev_attn) if da == 0 else ()
                mm = pb.op("tensor", lambda e, da=da, g=g, ps=ps: e.matmul(
                    ps[0:64, 0:512], qbh[:, da * 192 + 128:da * 192 + 192], t_fm[da][:, g * 512:g * 512 + 512],
                    start=(da == 0), stop=(da == 11)),
                    waits=w0, inc=s_pe if da == 11 else None)
            t_ev = pb.op("vector", lambda e, g=g, ps=ps: e.tensor_copy(qrraw[0:64, g * 512:g * 512 + 512], ps[0:64, 0:512]),
                         waits=(mm, prev_attn, qsw_free), inc=s_dv)
            qr_ev.append(t_ev)
            if g % 2 == 0:
                ps4_free = t_ev
            else:
                ps5_free = t_ev
            mmr_last = mm
        qb_free[0] = mmr_last
        t_sw1 = pb.op("sync", lambda e: e.dma_start(out=qswap[0:32, :], in_=qrraw[32:64, :]), waits=qr_ev, inc=s_rb, k=16)
        t_sw2 = pb.op("sync", lambda e: e.dma_start(out=qswap[32:64, :], in_=qrraw[0:32, :]), inc=s_rb, k=16)
        rp = []
        qrp1 = pb.op("vector", lambda e: e.tensor_mul(qrope[0:32, :], qrraw[0:32, :], cosfm[0:32, :]), waits=qr_ev + [t_ct3], inc=s_dv)
        qrp2 = pb.op("vector", lambda e: e.tensor_mul(P_sb[0:32, :], qswap[0:32, :], sinfm[0:32, :]), waits=(t_sw2, t_ct4), inc=s_dv)
        qrp3 = pb.op("vector", lambda e: e.tensor_sub(qrope[0:32, :], qrope[0:32, :], P_sb[0:32, :]), waits=(qrp1, qrp2), inc=s_dv)
        qrp4 = pb.op("vector", lambda e: e.tensor_mul(qrope[32:64, :], qrraw[32:64, :], cosfm[32:64, :]), waits=qr_ev + [t_ct3], inc=s_dv)
        qrp5 = pb.op("vector", lambda e: e.tensor_mul(P_sb[32:64, :], qswap[32:64, :], sinfm[32:64, :]), waits=(t_sw2, t_ct4), inc=s_dv)
        qrp6 = pb.op("vector", lambda e: e.tensor_add(qrope[32:64, :], qrope[32:64, :], P_sb[32:64, :]), waits=(qrp4, qrp5), inc=s_dv)
        qsw_free = [qrp3, qrp6]
        rope_done = [qrp3, qrp6]

        den = strow[:, 16:17]; negm = strow[:, 17:18]; rden = strow[:, 18:19]
        mxs = [strow[:, 20 + i:21 + i] for i in range(4)]
        dns = [strow[:, 24 + i:25 + i] for i in range(4)]
        psP_free = None
        prev_den = ()
        for qc in range(16):
            ngr = qc // 4 + 1
            exp_toks = []
            mx_toks = []
            s_toks = []
            for kg in range(ngr):
                ncols = 512 if kg < qc // 4 else (qc % 4 + 1) * 128
                ps = PS[kg]
                mm1 = pb.op("tensor", lambda e, qc=qc, kg=kg, ncols=ncols, ps=ps: e.matmul(
                    ps[:, 0:ncols], qnope[:, qc * 128:qc * 128 + 128], knope[:, kg * 512:kg * 512 + ncols],
                    start=True, stop=False),
                    waits=(qn_ev[0], qn_ev[1], qn_ev[2], qn_ev[3], kn_done[0], kn_done[1], psS_free[kg]),
                    inc=None)
                mm2 = pb.op("tensor", lambda e, qc=qc, kg=kg, ncols=ncols, ps=ps: e.matmul(
                    ps[:, 0:ncols], qrope[0:64, qc * 128:qc * 128 + 128], krope_fm[0:64, kg * 512:kg * 512 + ncols],
                    start=False, stop=True),
                    waits=tuple(rope_done), inc=s_pe)
                dcol = qc * 128 - kg * 512
                if 0 <= dcol < 512:
                    t_msk = pb.op("vector", lambda e, kg=kg, dcol=dcol, ps=ps: e.tensor_tensor(
                        ps[:, dcol:dcol + 128], ps[:, dcol:dcol + 128], cst["addmask"], ALU.add),
                        waits=(mm2,), inc=s_dv)
                else:
                    t_msk = mm2
                t_mx = pb.op("vector", lambda e, kg=kg, ncols=ncols, ps=ps: e.reduce_max(mxs[kg], ps[:, 0:ncols], axis=AX.X),
                             waits=(t_msk,), inc=s_dv)
                mx_toks.append(t_mx)
                s_toks.append(t_msk)
            t_gm = mx_toks[0]
            for kg in range(1, ngr):
                t_gm = pb.op("vector", lambda e, kg=kg: e.tensor_max(mxs[0], mxs[0], mxs[kg]), waits=(t_gm, mx_toks[kg]), inc=s_dv)
            t_nm = pb.op("vector", lambda e: e.tensor_scalar_mul(negm, mxs[0], -1.0), waits=(t_gm,) + tuple(prev_den), inc=s_dv)
            den_toks = []
            for kg in range(ngr):
                ncols = 512 if kg < qc // 4 else (qc % 4 + 1) * 128
                ps = PS[kg]
                t_ex = pb.op("scalar", lambda e, kg=kg, ncols=ncols, ps=ps: e.activation(
                    P_sb[:, kg * 512:kg * 512 + ncols], ps[:, 0:ncols], AF.Exp, bias=negm, accum_out=dns[kg]),
                    waits=(t_nm, s_toks[kg], psP_free), inc=s_ac)
                psS_free[kg] = t_ex
                den_toks.append(t_ex)
            t_d = den_toks[0]
            prev = dns[0]
            for kg in range(1, ngr):
                t_d = pb.op("vector", lambda e, kg=kg: e.tensor_add(dns[0], dns[0], dns[kg]), waits=(t_d, den_toks[kg]), inc=s_dv)
            t_rd = pb.op("vector", lambda e: e.reciprocal(rden, dns[0]), waits=(t_d, den_toks[-1]), inc=s_dv)
            nk = qc + 1
            mmPV = None
            ptb3_local = ptb2
            for kc in range(nk):
                pi = kc % 2
                sl = PT[pi]
                t_tp = pb.op("tensor", lambda e, kc=kc, sl=sl: e.transpose(sl, P_sb[:, kc * 128:kc * 128 + 128], cst["ident"]),
                             waits=(den_toks[kc // 4], ptb2[pi]), inc=s_pe)
                t_pe = pb.op("vector", lambda e, pi=pi, sl=sl: e.tensor_copy(ptsb[pi][:], sl), waits=(t_tp,), inc=s_dv)
                mmPV = pb.op("tensor", lambda e, kc=kc, pi=pi, nk=nk: e.matmul(
                    PS[4][:, 0:128], ptsb[pi][:], kvh[kc][:, 128:256],
                    start=(kc == 0), stop=(kc == nk - 1)),
                    waits=(t_pe, ps4_free) if kc == 0 else (t_pe,),
                    inc=s_pe if kc == nk - 1 else None)
                ptb2[pi] = mmPV
            pi2 = (h * NT + qc) % 2
            t_oe = pb.op("vector", lambda e, pi2=pi2: e.tensor_scalar(ob_sb[pi2][:], PS[4][:, 0:128], rden, None, ALU.mult),
                         waits=(mmPV, t_rd, sp["obfree"][pi2]), inc=s_dv)
            if False:
                dbg = sp["l1a_dbg"]
                pb.op("gpsimd", lambda e: e.dma_start(out=dbg[0], in_=qnope[:]), waits=qn_ev, inc=S["s_gi"], k=16)
                pb.op("gpsimd", lambda e: e.dma_start(out=dbg[1], in_=qrope[:]), waits=rope_done, inc=S["s_gi"], k=16)
                pb.op("gpsimd", lambda e: e.dma_start(out=dbg[2], in_=knope[:]), waits=kn_done, inc=S["s_gi"], k=16)
                pb.op("gpsimd", lambda e: e.dma_start(out=dbg[3], in_=krope_fm[:]), waits=(), inc=S["s_gi"], k=16)
                pb.op("gpsimd", lambda e: e.dma_start(out=dbg[4][:, 0:256], in_=kvh[0][:]), waits=(kvh_ev[0],), inc=S["s_gi"], k=16)
                pb.op("gpsimd", lambda e: e.dma_start(out=dbg[5], in_=P_sb[:]), waits=tuple(den_toks), inc=S["s_gi"], k=16)
                pb.op("gpsimd", lambda e: e.dma_start(out=dbg[6][:, 0:128], in_=ob_sb[pi2][:]), waits=(t_oe,), inc=S["s_gi"], k=16)
                pb.op("gpsimd", lambda e: e.dma_start(out=dbg[7][:, 0:64], in_=strow[:]), waits=(t_rd,), inc=S["s_gi"], k=16)
            t_ot = pb.op("tensor", lambda e, pi2=pi2: e.transpose(PT[pi2], ob_sb[pi2][:], cst["ident"]),
                         waits=(t_oe, ptb2[pi2]), inc=s_pe)
            ptb2[pi2] = t_ot
            t_oev = pb.op("vector", lambda e, pi2=pi2: e.tensor_copy(evt3[pi2][:], PT[pi2]),
                          waits=(t_ot, sp["evt3free"][pi2]), inc=s_dv)
            t_osp = pb.op("sync", lambda e, qc=qc, h=h, pi2=pi2: e.dma_start(out=sp["attn_sp"][qc][h], in_=evt3[pi2][:]),
                          waits=(t_oev,), inc=s_sp2, k=16)
            sp["evt3free"][pi2] = t_osp
            sp["obfree"][pi2] = t_ot
            sp["last_osp"] = t_osp
            ps4_free = t_oe
            prev_attn = t_oe
            psP_free = mmPV
            prev_den = tuple(den_toks)
    pb.fence([prev_attn, sp["last_osp"]])
    bfA.pop()
    if "p5" not in L1A_PH:
        bfA.pop(); f32A.pop()
        return [prev_attn, sp["last_osp"]]

    bfA.push()
    ow_sb = [bfA.tile(T) for _ in range(ND)]
    at_sb = bfA.tiles(2, 2048)
    t_ow = None
    for dch in range(ND):
        t_ow = pb.op("gpsimd", lambda e, dch=dch: e.dma_start(out=ow_sb[dch][:], in_=wd["l1a_ow"][dch]),
                     waits=(kvb_free,) if dch == 0 else (), inc=s_w, k=16)
    at_free = [None, None]
    psfree = [None] * 4
    xfree2 = [None, None]
    final = None
    for tc in range(NT):
        b = tc % 2
        t_al = pb.op("sync", lambda e, tc=tc, b=b: e.dma_start(
            out=at_sb[b].rearrange("p (c m) -> p c m", m=128), in_=attn_blk(sp, tc).rearrange("c p m -> p c m")),
            waits=(at_free[b], sp["last_osp"]), inc=s_xnt[b], k=16)
        t_ldx = pb.op("sync", lambda e, tc=tc, b=b: e.dma_start(out=xt[b][:], in_=x_tiled[tc]),
                      waits=(xfree2[b], xfree[b]), inc=s_x[b], k=16)
        ev_toks = []
        for g in range(4):
            ps = PS[g]
            mm = None
            for dv in range(ND):
                mm = pb.op("tensor", lambda e, dv=dv, g=g, b=b, ps=ps: e.matmul(
                    ps, at_sb[b][:, dv * 128:dv * 128 + 128], ow_sb[dv][:, g * 512:g * 512 + 512],
                    start=(dv == 0), stop=(dv == ND - 1)),
                    waits=(t_al, t_ow, psfree[g]) if dv == 0 else (),
                    inc=s_pe if dv == ND - 1 else None)
            t_ev = pb.op("vector", lambda e, g=g, b=b, ps=ps: e.tensor_add(xt[b][:, g * 512:g * 512 + 512], ps, xt[b][:, g * 512:g * 512 + 512]),
                         waits=(mm, t_ldx), inc=s_dv)
            psfree[g] = t_ev
            ev_toks.append(t_ev)
        t_st = pb.op("sync", lambda e, tc=tc, b=b: e.dma_start(out=x_out_tiled[tc], in_=xt[b][:]),
                     waits=ev_toks, inc=s_x[b], k=16)
        at_free[b] = [ev_toks[-1], mm]
        xfree2[b] = [t_st]
        final = t_st
    bfA.pop()
    bfA.pop(); f32A.pop()
    return [final]




HID = 5632
NH_FF = 44


def prep_l1_ffn(inp):
    d = {}
    wg = np.asarray(inp["L1_ffn_wq_gate"], np.int32).T
    d["l1f_wg"] = np.ascontiguousarray(wg.reshape(16, 128, 44, 128).transpose(2, 0, 1, 3))
    wu = np.asarray(inp["L1_ffn_wq_up"], np.int32).T
    d["l1f_wu"] = np.ascontiguousarray(wu.reshape(16, 128, 44, 128).transpose(2, 0, 1, 3))
    wd_ = np.asarray(inp["L1_ffn_wq_down"], np.int32).T
    d["l1f_wd"] = np.ascontiguousarray(wd_.reshape(44, 128, 16, 128).transpose(2, 0, 1, 3))
    d["l1f_sg"] = np.ascontiguousarray(np.asarray(inp["L1_ffn_s_gate"], np.float32).reshape(44, 128).T)
    d["l1f_su"] = np.ascontiguousarray(np.asarray(inp["L1_ffn_s_up"], np.float32).reshape(44, 128).T)
    d["l1f_sd"] = np.ascontiguousarray(np.asarray(inp["L1_ffn_s_down"], np.float32).reshape(16, 128).T)
    return d


L1F_META = [
    ("l1f_wg", (44, 16, 128, 128), I32), ("l1f_wu", (44, 16, 128, 128), I32),
    ("l1f_wd", (16, 44, 128, 128), I32),
    ("l1f_sg", (128, 44), F32), ("l1f_su", (128, 44), F32), ("l1f_sd", (128, 16), F32),
]


def emit_l1_ffn(pb, A, x_in, x_out, wd, cst, S, sp):
    nc = pb.nc
    bfA, f32A = A["bf"], A["f32"]
    s_w, s_w2, s_pe, s_dv, s_ac = S["s_w"], S["s_w2"], S["s_pe"], S["s_dv"], S["s_ac"]
    s_x = [S["s_x0"], S["s_x1"]]
    s_ev = [S[f"s_ev{i}"] for i in range(4)]
    s_xnt = S["s_q"]
    s_wd = S["s_v"]
    PS, PT = A["PS"], A["PT"]

    bfA.push(); f32A.push()
    sgt = f32A.tile(NH_FF)
    sut = f32A.tile(NH_FF)
    sdt = f32A.tile(16)
    xt = f32A.tiles(2, T)
    junk = f32A.tile(T)
    statc = f32A.tile(6 * NT)
    gu_off = bfA.off
    gu = [bfA.tile(1024) for _ in range(NH_FF)]
    wgbuf = bfA.tiles(2, 2048)
    wubuf = bfA.tiles(2, 2048)
    g_sb = bfA.tile(1024)

    t_sg = pb.op("sync", lambda e: e.dma_start(out=sgt[:], in_=wd["l1f_sg"]), inc=S["s_rb"], k=16)
    t_su = pb.op("sync", lambda e: e.dma_start(out=sut[:], in_=wd["l1f_su"]), inc=S["s_rb"], k=16)
    t_sd = pb.op("sync", lambda e: e.dma_start(out=sdt[:], in_=wd["l1f_sd"]), inc=S["s_rb"], k=16)

    x_tiled = x_in.rearrange("(c p) d -> c p d", p=128)
    x_out_tiled = x_out.rearrange("(c p) d -> c p d", p=128)

    psfree = [None] * 4
    ptfree = [None, None]
    xfree = [None, None]

    bfA.push()
    xn4 = [bfA.tile(T) for _ in range(ND)]
    xn_bf = bfA.h[:, gu_off:gu_off + 2048]
    xnb_free = None
    for c in range(NT):
        b = c % 2
        t_ld = pb.op("sync", lambda e, c=c, b=b: e.dma_start(out=xt[b][:], in_=x_tiled[c]),
                     waits=(xfree[b],), inc=s_x[b], k=16)
        st = {k2: statc[:, c * 6 + i:c * 6 + i + 1] for i, k2 in enumerate(("m", "ssq", "var", "r", "rho", "tmp"))}
        t_r, t_sq = stat_ln_rms(pb, cst, xt[b][:], st, (t_ld,), s_dv, s_ac, junk)
        t_mr = pb.op("vector", lambda e, st=st: e.tensor_mul(st["tmp"], st["m"], st["r"]), waits=(t_r,), inc=s_dv)
        t_xn = pb.op("vector", lambda e, b=b, st=st: e.tensor_scalar(xn_bf[:], xt[b][:], st["r"], st["tmp"], ALU.mult, ALU.subtract),
                     waits=(xnb_free, t_mr), inc=s_dv)
        xfree[b] = [t_xn, t_sq]
        tp_last = None
        for dch in range(ND):
            pi = dch % 2
            sl = PT[pi]
            t_tp = pb.op("tensor", lambda e, dch=dch, sl=sl: e.transpose(sl, xn_bf[:, dch * 128:dch * 128 + 128], cst["ident"]),
                         waits=(t_xn, ptfree[pi]), inc=s_pe)
            t_ev = pb.op("vector", lambda e, dch=dch, c=c, sl=sl: e.tensor_copy(xn4[dch][:, c * 128:c * 128 + 128], sl),
                         waits=(t_tp,), inc=s_dv)
            ptfree[pi] = t_ev
            tp_last = t_tp
        xnb_free = tp_last
    sp_last = [None, None]
    for dch in range(ND):
        sp_last[dch % 2] = pb.op("sync", lambda e, dch=dch: e.dma_start(out=sp["xn4_sp"][dch], in_=xn4[dch][:]),
                                 waits=(ptfree[0], ptfree[1]) if dch == 0 else (), inc=s_ev[dch % 2], k=16)
    xn4_ready = [ptfree[0], ptfree[1]]

    halves_final = []
    wg_last = [None, None]
    wu_last = [None, None]
    wu_free = None
    g_done = None
    wd_free = [None, None]
    for half in range(2):
        tau0 = half * 1024
        if half == 1:
            bfA.pop(); bfA.push()
            xn4 = [bfA.tile(T) for _ in range(ND)]
            t_l = None
            for dch in range(ND):
                t_l = pb.op("sync", lambda e, dch=dch: e.dma_start(out=xn4[dch][:], in_=sp["xn4_sp"][dch]),
                            waits=(halves_final[-1], wd_free[0], wd_free[1], sp_last[0], sp_last[1]) if dch == 0 else (),
                            inc=s_xnt, k=16)
            xn4_ready = [t_l]
        gu_toks = [None] * NH_FF
        for h in range(NH_FF):
            wb = h % 2
            t_lg = pb.op("gpsimd", lambda e, h=h, wb=wb: e.dma_start(
                out=wgbuf[wb].rearrange("p (c m) -> p c m", m=128), in_=wd["l1f_wg"][h].rearrange("c p m -> p c m")),
                waits=(wg_last[wb],), inc=(s_w if wb == 0 else s_w2), k=16)
            t_lu = pb.op("gpsimd", lambda e, h=h, wb=wb: e.dma_start(
                out=wubuf[wb].rearrange("p (c m) -> p c m", m=128), in_=wd["l1f_wu"][h].rearrange("c p m -> p c m")),
                waits=(wu_last[wb],), inc=(s_w if wb == 0 else s_w2), k=16)
            ev_g = []
            for g in range(2):
                ps = PS[g]
                mm = None
                for d in range(ND):
                    mm = pb.op("tensor", lambda e, d=d, g=g, wb=wb, tau0=tau0, ps=ps: e.matmul(
                        ps, wgbuf[wb][:, d * 128:d * 128 + 128], xn4[d][:, tau0 + g * 512:tau0 + g * 512 + 512],
                        start=(d == 0), stop=(d == ND - 1)),
                        waits=(t_lg, psfree[g], xn4_ready) if d == 0 else (),
                        inc=s_pe if d == ND - 1 else None)
                t_pf = pe_flush(pb, cst, A, s_pe)
                t_ev = pb.op("scalar", lambda e, g=g, h=h, ps=ps: e.activation(g_sb[:, g * 512:g * 512 + 512], ps, AF.Silu,
                                                                               scale=sgt[:, h:h + 1]),
                             waits=(t_pf, t_sg, g_done), inc=s_ac)
                psfree[g] = t_ev
                ev_g.append(t_ev)
            wg_last[wb] = mm
            ev_u = []
            for g in range(2):
                ps = PS[2 + g]
                mm = None
                for d in range(ND):
                    mm = pb.op("tensor", lambda e, d=d, g=g, wb=wb, tau0=tau0, ps=ps: e.matmul(
                        ps, wubuf[wb][:, d * 128:d * 128 + 128], xn4[d][:, tau0 + g * 512:tau0 + g * 512 + 512],
                        start=(d == 0), stop=(d == ND - 1)),
                        waits=(t_lu, psfree[2 + g]) if d == 0 else (),
                        inc=s_pe if d == ND - 1 else None)
                t_pf = pe_flush(pb, cst, A, s_pe)
                t_ev = pb.op("vector", lambda e, g=g, h=h, ps=ps: e.scalar_tensor_tensor(
                    gu[h][:, g * 512:g * 512 + 512], ps, sut[:, h:h + 1], g_sb[:, g * 512:g * 512 + 512],
                    ALU.mult, ALU.mult),
                    waits=(t_pf, ev_g[g], t_su, gu_toks[h]), inc=s_dv)
                psfree[2 + g] = t_ev
                ev_u.append(t_ev)
            wu_last[wb] = mm
            wu_free = mm
            g_done = ev_u[-1]
            gu_toks[h] = [ev_u[0], ev_u[1]]
        bfA.pop(); bfA.push()
        wdbuf = bfA.tiles(2, NH_FF * 128)
        delta = [bfA.tile(1024) for _ in range(ND)]
        wd_free = [None, None]
        delta_toks = []
        first_ev = True
        for dc in range(ND):
            wb = dc % 2
            t_lw = pb.op("gpsimd", lambda e, dc=dc, wb=wb: e.dma_start(
                out=wdbuf[wb].rearrange("p (c m) -> p c m", m=128), in_=wd["l1f_wd"][dc].rearrange("c p m -> p c m")),
                waits=(wd_free[wb],) if wd_free[wb] is not None else (wu_free, sp_last[0], sp_last[1]),
                inc=s_wd, k=16)
            t_sub = pb.op("vector", lambda e, wb=wb: e.tensor_scalar(wdbuf[wb][:], wdbuf[wb][:], -128.0, None, ALU.add),
                          waits=(t_lw,), inc=s_dv)
            evs = []
            for g in range(2):
                pi = (2 * dc + g) % 4
                ps = PS[pi]
                mm = None
                for h in range(NH_FF):
                    mm = pb.op("tensor", lambda e, h=h, g=g, wb=wb, ps=ps: e.matmul(
                        ps, wdbuf[wb][:, h * 128:h * 128 + 128], gu[h][:, g * 512:g * 512 + 512],
                        start=(h == 0), stop=(h == NH_FF - 1)),
                        waits=(t_sub, psfree[pi], gu_toks[h]) if h == 0 else (gu_toks[h],),
                        inc=s_pe if h == NH_FF - 1 else None)
                t_pf = pe_flush(pb, cst, A, s_pe)
                wex = (sp_last[0], sp_last[1]) if first_ev and half == 0 else ()
                first_ev = False
                t_ev = pb.op("vector", lambda e, dc=dc, g=g, ps=ps: e.tensor_scalar(delta[dc][:, g * 512:g * 512 + 512], ps,
                                                                                    sdt[:, dc:dc + 1], None, ALU.mult),
                             waits=(t_pf, t_sd) + wex, inc=s_dv)
                psfree[pi] = t_ev
                evs.append(t_ev)
            wd_free[wb] = mm
            delta_toks.append(evs)
        last_st = None
        for tl in range(8):
            tc = half * 8 + tl
            b = tc % 2
            t_ldx = pb.op("sync", lambda e, tc=tc, b=b: e.dma_start(out=xt[b][:], in_=x_tiled[tc]),
                          waits=(xfree[b],), inc=s_x[b], k=16)
            adds = []
            for dc in range(ND):
                pi = dc % 2
                sl = PT[pi]
                t_tp = pb.op("tensor", lambda e, dc=dc, tl=tl, sl=sl: e.transpose(sl, delta[dc][:, tl * 128:tl * 128 + 128], cst["ident"]),
                             waits=(delta_toks[dc][tl // 4], ptfree[pi]), inc=s_pe)
                t_ad = pb.op("vector", lambda e, dc=dc, b=b, sl=sl: e.tensor_add(xt[b][:, dc * 128:dc * 128 + 128], sl,
                                                                                  xt[b][:, dc * 128:dc * 128 + 128]),
                             waits=(t_tp, t_ldx), inc=s_dv)
                ptfree[pi] = t_ad
                adds.append(t_ad)
            t_st = pb.op("sync", lambda e, tc=tc, b=b: e.dma_start(out=x_out_tiled[tc], in_=xt[b][:]),
                         waits=adds, inc=s_x[b], k=16)
            xfree[b] = [t_st]
            last_st = t_st
        halves_final.append(last_st)
    bfA.pop()
    bfA.pop(); f32A.pop()
    return [halves_final[0], halves_final[1]]



BF_COLS = 87040
F32_COLS = 8704

SEM_NAMES = ("s_w", "s_w2", "s_pe", "s_dv", "s_ac", "s_c", "s_cf", "s_gi", "s_rb",
             "s_x0", "s_x1", "s_q", "s_k0", "s_k1", "s_v", "s_g0", "s_g1",
             "s_ev0", "s_ev1", "s_ev2", "s_ev3")


def build_launch(subblocks, weights_meta):
    nc = bass.Bass("TRN2", target_bir_lowering=False, debug=False, detect_race_conditions=False)
    ctx = ExitStack()
    nc._keepalive_ctx = ctx

    x_in = nc.dram_tensor("x_in", [T, D], F32, kind="ExternalInput").ap()
    x_out = nc.dram_tensor("x_out", [T, D], F32, kind="ExternalOutput").ap()
    cst_in = nc.dram_tensor("consts_f32", [128, 4 * 128], F32, kind="ExternalInput").ap()

    wd_aps = {}
    for name, shape, dt in weights_meta:
        wd_aps[name] = nc.dram_tensor(name, list(shape), dt, kind="ExternalInput").ap()

    pb = PB(nc, ctx)
    bf_h = ctx.enter_context(nc.sbuf_tensor("arena_bf", [128, BF_COLS], BF16))
    f32_h = ctx.enter_context(nc.sbuf_tensor("arena_f32", [128, F32_COLS], F32))
    cstb_h = ctx.enter_context(nc.sbuf_tensor("cst_bf", [128, 256], BF16))
    cstf_h = ctx.enter_context(nc.sbuf_tensor("cst_f32", [128, 420], F32))
    PS = [ctx.enter_context(nc.psum_tensor(f"ps{i}", [128, 512], F32)) for i in range(6)]
    PT = [ctx.enter_context(nc.psum_tensor(f"pt{i}", [128, 128], BF16)) for i in range(2)]

    flush_t = ctx.enter_context(nc.sbuf_tensor("flush_t", [1, 4], F32))
    nc._flush_tile = flush_t[0:1, 0:1]
    A = {"bf": Arena(bf_h, BF_COLS), "f32": Arena(f32_h, F32_COLS),
         "PS": [p[:] for p in PS], "PT": [p[:] for p in PT]}

    S = {k: pb.sem(k) for k in SEM_NAMES}

    pb.op("gpsimd", lambda e: e.dma_start(out=cstb_h[:, 0:128], in_=cst_in[:, 0:128]), inc=S["s_c"], k=16)
    pb.op("gpsimd", lambda e: e.dma_start(out=cstb_h[:, 128:256], in_=cst_in[:, 128:256]), inc=S["s_c"], k=16)
    t_c = pb.op("sync", lambda e: e.dma_start(out=cstf_h[:, 0:128], in_=cst_in[:, 256:384]), inc=S["s_cf"], k=16)
    t_cI = pb.op("sync", lambda e: e.dma_start(out=cstf_h[:, 160:288], in_=cst_in[:, 384:512]), inc=S["s_cf"], k=16)
    t_m1 = pb.op("vector", lambda e: e.memset(cstf_h[:, 128:129], LN_EPS), inc=S["s_dv"])
    t_m2 = pb.op("vector", lambda e: e.memset(cstf_h[:, 129:130], RMS_EPS), inc=S["s_dv"])
    t_m3 = pb.op("vector", lambda e: e.memset(cstf_h[:, 130:131], 1e-6), inc=S["s_dv"])
    t_m4 = pb.op("vector", lambda e: e.memset(cstf_h[0:1, 288:416], 1.0), inc=S["s_dv"])
    t_m5 = pb.op("vector", lambda e: e.memset(flush_t[:], 0.0), inc=S["s_dv"])
    pb.fence([t_c, t_cI, t_m4, t_m5, S["s_c"].cur(), S["s_cf"].cur()])

    cst = {
        "ident": cstb_h[:, 0:128],
        "triu": cstb_h[:, 128:256],
        "addmask": cstf_h[:, 0:128],
        "ln_eps": cstf_h[:, 128:129],
        "rms_eps": cstf_h[:, 129:130],
        "mla_eps": cstf_h[:, 130:131],
        "ident_f32": cstf_h[:, 160:288],
        "ones_row": cstf_h[0:1, 288:416],
        "flush": flush_t[0:1, 0:1],
    }
    it_pos = ctx.enter_context(nc.sbuf_tensor("it_pos", [128, 16], I32))
    it_tok = ctx.enter_context(nc.sbuf_tensor("it_tok", [128, 16], I32))
    it_idx = ctx.enter_context(nc.sbuf_tensor("it_idx", [128, 4], I32))
    it_mku = ctx.enter_context(nc.sbuf_tensor("it_mku", [128, 16], U32))
    A["PSF"] = PS[5][0:1, 0:128]
    A["IT"] = {"pos_i": it_pos[:], "tokid": it_tok[:], "idx_sb": it_idx[:], "mku": it_mku[:]}

    sp = {}
    _kind = "Internal"
    q_sp = nc.dram_tensor("q_spill", [64, 128, 512], BF16, kind=_kind).ap()
    g_sp = nc.dram_tensor("g_spill", [64, 128, 512], BF16, kind=_kind).ap()
    k_spc = nc.dram_tensor("k_spill_t", [16, 16, 128, 128], BF16, kind=_kind).ap()
    v_sp = nc.dram_tensor("v_spill", [16, 128, 2048], BF16, kind=_kind).ap()
    xn_dbg = None
    sp["xn_dbg"] = xn_dbg
    sp["q_sp"] = [q_sp[i] for i in range(64)]
    sp["g_sp"] = [g_sp[i] for i in range(64)]
    sp["k_spc"] = [[k_spc[kc, do] for do in range(16)] for kc in range(16)]
    sp["k_spc_blk"] = [k_spc[kc] for kc in range(16)]
    sp["v_sp"] = [v_sp[i] for i in range(16)]
    sp["idx_hbm"] = nc.dram_tensor("idx_hbm", [512, 1], I32, kind="Internal").ap()
    sp["gw_hbm"] = nc.dram_tensor("gw_hbm", [512, 1], F32, kind="Internal").ap()
    xn3_sp = nc.dram_tensor("xn3_sp", [16, 16, 128, 128], BF16, kind=_kind).ap()
    sp["l1a_dbg"] = None
    attn_sp = nc.dram_tensor("attn_sp", [16, 16, 128, 128], BF16, kind=_kind).ap()
    sp["xn3_sp"] = [[xn3_sp[a, b] for b in range(16)] for a in range(16)]
    sp["attn_sp"] = [[attn_sp[a, b] for b in range(16)] for a in range(16)]
    sp["_xn3_blk"] = [xn3_sp[a] for a in range(16)]
    xn4_sp = nc.dram_tensor("xn4_sp", [16, 128, 2048], BF16, kind="Internal").ap()
    sp["xn4_sp"] = [xn4_sp[i] for i in range(16)]
    sp["_attn_blk"] = [attn_sp[a] for a in range(16)]

    xmids = [nc.dram_tensor(f"x_mid{i}", [T, D], F32, kind="Internal").ap()
             for i in range(len(subblocks) - 1)]
    bufs = [x_in] + xmids + [x_out]
    final_toks = []
    for i, (name, emit_fn) in enumerate(subblocks):
        final_toks = emit_fn(pb, A, bufs[i], bufs[i + 1], wd_aps, cst, S, sp)
        pb.fence(final_toks + [S[k].cur() for k in SEM_NAMES])

    pb.op("gpsimd", lambda e: e.nop(), waits=tuple(final_toks))

    block = ctx.enter_context(nc.Block())
    pb.build(block)
    ctx.close()
    return nc



_DT_MAP = {np.dtype(np.float32): F32, np.dtype(np.int32): I32}


def kernel(**inputs):
    x = np.ascontiguousarray(np.asarray(inputs["L0_attn_x"], np.float32))
    consts = make_host_consts()
    stages = [
        ("l0a", emit_l0_attn, prep_l0_attn(inputs)),
        ("l0f", emit_l0_ffn, prep_l0_ffn(inputs)),
        ("l1a", emit_l1_attn, prep_l1_attn(inputs)),
        ("l1f", emit_l1_ffn, prep_l1_ffn(inputs)),
    ]
    wd_all = {}
    for name, emit, wd in stages:
        wd_all.update(wd)
    meta = [(k, v.shape, _DT_MAP[v.dtype]) for k, v in wd_all.items()]
    nc = build_launch([(name, emit) for name, emit, wd in stages], meta)
    in_maps = []
    for b in range(8):
        m = {"x_in": np.ascontiguousarray(x[b]), "consts_f32": consts}
        m.update(wd_all)
        in_maps.append(m)
    res = run_bass_kernel_spmd(nc, in_maps, core_ids=list(range(8)))
    return np.stack([np.asarray(res.results[b]["x_out"], np.float32) for b in range(8)], axis=0)
